# version 20
# baseline (speedup 1.0000x reference)
import numpy as np
import ml_dtypes
import concourse.bass as bass
import concourse.mybir as mybir
from concourse.bass_utils import run_bass_kernel_spmd

F32 = mybir.dt.float32
BF16 = mybir.dt.bfloat16
AF = mybir.ActivationFunctionType
ALU = mybir.AluOpType

NCORES = 8
SEQ = 2048
D = 1024
NKT = SEQ // 128
EPS = 1e-5
MASKVAL = -30000.0

ENGINES = ("sync", "scalar", "vector", "tensor", "gpsimd")
DEBUG_NB = None
DEBUG_STAGE = "full"
SKIP_FFN = False


class Plan:

    def __init__(self, nc, ctx):
        self.nc = nc
        self.ctx = ctx
        self.ops = {e: [] for e in ENGINES}
        self.sems = {}
        self.val = {}

    def sem(self, name):
        if name not in self.sems:
            self.sems[name] = self.ctx.enter_context(self.nc.semaphore(name))
            self.val[name] = 0
        return self.sems[name]

    def op(self, eng, fn, inc=None, inc_amt=1):
        semh = self.sem(inc) if inc else None
        self.ops[eng].append(("op", fn, semh, inc_amt))
        if inc:
            self.val[inc] += inc_amt

    def dma(self, eng, fn, inc):
        self.op(eng, fn, inc=inc, inc_amt=16)

    def wait(self, eng, sem, value=None):
        if value is None:
            value = self.val.get(sem, 0)
        if value > 0:
            self.ops[eng].append(("wait", self.sem(sem), value))

    def v(self, sem):
        return self.val.setdefault(sem, 0)

    def run(self, block):
        def make(name):
            items = self.ops[name]

            def body(eng):
                for it in items:
                    if it[0] == "wait":
                        eng.wait_ge(it[1], it[2])
                    else:
                        inst = it[1](eng)
                        if it[2] is not None:
                            inst.then_inc(it[2], it[3])
            return body
        block.sync(make("sync"))
        block.scalar(make("scalar"))
        block.vector(make("vector"))
        block.tensor(make("tensor"))
        block.gpsimd(make("gpsimd"))




def _apof(buf):
    return buf.ap() if hasattr(buf, "ap") and callable(getattr(buf, "ap")) \
        else buf


def sub3(buf, off, d1, d2):
    a = _apof(buf)
    return bass.AP(a.tensor, a.offset + off, [[a.ap[0][0], a.ap[0][1]], d1, d2])


def sub2(buf, off, d1):
    a = _apof(buf)
    return bass.AP(a.tensor, a.offset + off, [[a.ap[0][0], a.ap[0][1]], d1])


def rowstride(buf):
    return _apof(buf).ap[0][0]



def alloc_common(nc, ctx, need_attn, need_ffn):
    E = ctx.enter_context
    B = {}
    B["PA"] = E(nc.psum_tensor("PA", [128, 2048], F32))
    B["PB"] = E(nc.psum_tensor("PB", [128, 2048], F32))
    B["bnb"] = [E(nc.sbuf_tensor(f"bnb{i}", [128, 12], F32)) for i in range(4)]
    B["mv"] = [E(nc.sbuf_tensor(f"mv{i}", [128, 2], F32)) for i in range(4)]
    B["nm"] = [E(nc.sbuf_tensor(f"nm{i}", [128, 1], F32)) for i in range(4)]
    B["rs"] = [E(nc.sbuf_tensor(f"rs{i}", [128, 1], F32)) for i in range(4)]
    B["sq1"] = [E(nc.sbuf_tensor(f"sq1_{i}", [128, 1], F32)) for i in range(4)]
    B["sq2"] = [E(nc.sbuf_tensor(f"sq2_{i}", [128, 1], F32)) for i in range(4)]
    B["eps"] = E(nc.sbuf_tensor("eps", [128, 1], F32))
    B["xt"] = [E(nc.sbuf_tensor(f"xt{i}", [128, 1024], F32)) for i in range(2)]
    B["lt"] = E(nc.sbuf_tensor("lt", [128, 1024], F32))
    B["identf"] = E(nc.sbuf_tensor("identf", [128, 128], F32))
    B["identb"] = E(nc.sbuf_tensor("identb", [128, 128], BF16))
    B["ones1"] = E(nc.sbuf_tensor("ones1", [1, 128], BF16))
    if need_attn:
        B["lnf"] = E(nc.sbuf_tensor("lnf", [128, 8 * 2048], BF16))
        B["kf"] = E(nc.sbuf_tensor("kf", [128, 8 * 2048], BF16))
        B["va"] = E(nc.sbuf_tensor("va", [128, 16 * 1040], BF16))
        B["vt"] = E(nc.sbuf_tensor("vt", [128, 1024], BF16))
        B["qf"] = E(nc.sbuf_tensor("qf", [128, 2 * 8 * 256], BF16))
        B["ex"] = E(nc.sbuf_tensor("ex", [128, 2048], BF16))
        B["of"] = E(nc.sbuf_tensor("of", [128, 8 * 256], BF16))
        B["al"] = [E(nc.sbuf_tensor(f"al{i}", [128, 1024], BF16)) for i in range(2)]
        B["wq"] = E(nc.sbuf_tensor("wq_sb", [128, 8192], BF16))
        B["wk"] = E(nc.sbuf_tensor("wk_sb", [128, 8192], BF16))
        B["wv"] = E(nc.sbuf_tensor("wv_sb", [128, 8192], BF16))
        B["wu"] = E(nc.sbuf_tensor("wu_sb", [128, 8192], BF16))
        B["bu_row"] = E(nc.sbuf_tensor("bu_row", [1, 1024], BF16))
        B["rb"] = E(nc.sbuf_tensor("rb", [64, 1024], BF16))
    if need_ffn == "alias":
        lnf, kf, va = B["lnf"].ap(), B["kf"].ap(), B["va"].ap()
        B["wfT"] = lnf[:, 0:8192]
        B["wiT"] = lnf[:, 8192:16384]
        B["wcT"] = kf[:, 0:8192]
        B["ufT"] = kf[:, 8192:16384]
        B["uiT"] = B["wq"].ap()[:, :]
        B["ucT"] = B["wk"].ap()[:, :]
        B["woT"] = B["wv"].ap()[:, :]
        B["uoT"] = B["wu"].ap()[:, :]
        B["cb"] = [va[:, 0:2048], va[:, 2048:4096]]
        B["l2"] = [va[:, 4096:6144], va[:, 6144:8192]]
        B["alt"] = [va[:, 8192:9216], va[:, 9216:10240]]
        B["cuo"] = va[:, 10240:12288]
        B["bo_row"] = va[0:1, 12288:13312]
        cap = B["PA"].ap()[:, 0:2048]
        B["c"] = [cap, cap]
        B["c_in_psum"] = True
        B["hh"] = B["PA"].ap()[:, 0:1024]
        ltap = B["lt"].ap()
        B["oo"] = ltap[:, 0:1024]
        B["gates"] = [ltap[:, 0:256], ltap[:, 256:512], ltap[:, 512:768]]
        B["x1"] = E(nc.sbuf_tensor("x1_", [128, 1024], F32))
        B["bcol"] = E(nc.sbuf_tensor("bcol", [128, 24], F32))
        B["inccol"] = E(nc.sbuf_tensor("inccol", [128, 8], F32))
    elif need_ffn:
        for w in ("wfT", "wiT", "wcT", "ufT", "uiT", "ucT", "woT", "uoT"):
            B[w] = E(nc.sbuf_tensor(w, [128, 8192], BF16))
        B["c"] = [E(nc.sbuf_tensor(f"c{i}", [128, 2048], F32)) for i in range(2)]
        B["cb"] = [E(nc.sbuf_tensor(f"cb{i}", [128, 2048], BF16)) for i in range(2)]
        B["l2"] = [E(nc.sbuf_tensor(f"l2{i}", [128, 2048], BF16)) for i in range(2)]
        B["gates"] = [E(nc.sbuf_tensor(f"gate{i}", [128, 256], F32)) for i in range(3)]
        B["x1"] = E(nc.sbuf_tensor("x1_", [128, 1024], F32))
        B["alt"] = [E(nc.sbuf_tensor(f"alt{i}", [128, 1024], BF16)) for i in range(2)]
        B["cuo"] = E(nc.sbuf_tensor("cuo", [128, 2048], BF16))
        B["hh"] = E(nc.sbuf_tensor("hh", [128, 1024], F32))
        B["oo"] = E(nc.sbuf_tensor("oo", [128, 1024], F32))
        B["bcol"] = E(nc.sbuf_tensor("bcol", [128, 24], F32))
        B["inccol"] = E(nc.sbuf_tensor("inccol", [128, 8], F32))
        B["bo_row"] = E(nc.sbuf_tensor("bo_row", [1, 1024], BF16))
    return B


def emit_setup(P, B):
    P.op("gpsimd", lambda g: g.memset(B["eps"][:], EPS))
    P.op("gpsimd", lambda g: g.memset(B["identf"][:], 0.0), inc="gp_setup")
    P.wait("gpsimd", "gp_setup")
    P.op("gpsimd", lambda g: g.affine_select(
        out=B["identf"][:], in_=B["identf"][:], compare_op=ALU.not_equal,
        fill=1.0, base=0, pattern=[[-1, 128]], channel_multiplier=1))
    P.op("gpsimd", lambda g: g.memset(B["ones1"][:], 1.0), inc="gp_setup")
    P.op("gpsimd", lambda g: g.memset(B["identb"][:], 0.0), inc="gp_setup")
    P.wait("gpsimd", "gp_setup")
    P.op("gpsimd", lambda g: g.affine_select(
        out=B["identb"][:], in_=B["identb"][:], compare_op=ALU.not_equal,
        fill=1.0, base=0, pattern=[[-1, 128]], channel_multiplier=1),
        inc="gp_setup")
    if "va" in B:
        P.op("gpsimd", lambda g: g.memset(B["qf"][:], 0.0), inc="gp_setup")
        P.op("gpsimd", lambda g: g.memset(B["va"][:], 0.0), inc="gp_setup")
        P.wait("gpsimd", "gp_setup")
        P.op("gpsimd", lambda g: g.memset(
            sub3(B["va"], 64, [1040, 16], [65, 16]), 1.0),
            inc="gp_setup")
    P.wait("tensor", "gp_setup")
    P.wait("vector", "gp_setup")
    P.wait("scalar", "gp_setup")


def emit_small_ln(P, B, par, src_fn, width=1024):
    bnb, mv, nm, rs = B["bnb"][par], B["mv"][par], B["nm"][par], B["rs"][par]

    def stats(v):
        src = src_fn()
        nh = width // 512
        for i in range(nh - 1):
            v.bn_stats(bnb[:, 6 * i:6 * (i + 1)], src[:, 512 * i:512 * (i + 1)])
        i = nh - 1
        return v.bn_stats(bnb[:, 6 * i:6 * (i + 1)],
                          src[:, 512 * i:512 * (i + 1)])
    P.op("vector", stats, inc="ve")
    P.wait("vector", "ve")
    P.op("vector", lambda v: v.bn_aggr(mv[:], bnb[:, 0:6 * (width // 512)]),
         inc="ve")
    P.wait("vector", "ve")
    P.op("vector", lambda v: v.tensor_scalar(nm[:], mv[:, 0:1], -1.0, None,
                                             ALU.mult), inc="ve")
    P.wait("scalar", "ve")
    P.op("scalar", lambda s: s.activation(rs[:], mv[:, 1:2], AF.Ln,
                                          bias=B["eps"][:, 0:1]), inc="se")
    P.wait("scalar", "se")
    P.op("scalar", lambda s: s.activation(rs[:], rs[:], AF.Exp, scale=-0.5),
         inc="ve_rs")
    return nm, rs


I32 = mybir.dt.int32
RSQRT_K = 0x5f3759df


def emit_rsqrt_dve(P, B, par):
    mv, nm, rs = B["mv"][par], B["nm"][par], B["rs"][par]
    q1, q2 = B["sq1"][par], B["sq2"][par]
    q1i = q1.ap().bitcast(I32)
    q2i = q2.ap().bitcast(I32)
    P.op("vector", lambda v: v.tensor_scalar(nm[:], mv[:, 0:1], -1.0, None,
                                             ALU.mult), inc="ve")
    P.op("vector", lambda v: v.tensor_single_scalar(
        q1[:], mv[:, 1:2], EPS, ALU.add), inc="ve")
    P.wait("vector", "ve")
    P.op("vector", lambda v: v.tensor_single_scalar(
        q2i, q1i, 1, ALU.arith_shift_right), inc="ve")
    P.wait("vector", "ve")
    P.op("vector", lambda v: v.tensor_scalar(
        q2i, q2i, -1, RSQRT_K, ALU.mult, ALU.add), inc="ve")
    for it in range(2):
        dst = q2 if it == 0 else rs
        P.wait("vector", "ve")
        P.op("vector", lambda v: v.tensor_mul(rs[:], q1[:], q2[:]), inc="ve")
        P.wait("vector", "ve")
        P.op("vector", lambda v: v.tensor_mul(rs[:], rs[:], q2[:]), inc="ve")
        P.wait("vector", "ve")
        P.op("vector", lambda v: v.tensor_scalar(
            rs[:], rs[:], -0.5, 1.5, ALU.mult, ALU.add), inc="ve")
        P.wait("vector", "ve")
        P.op("vector", lambda v, dst=dst: v.tensor_mul(dst[:], q2[:], rs[:]),
             inc="ve" if it == 0 else "ve_rs")
    return nm, rs


def emit_small_ln_dve(P, B, par, src_fn, width=1024):
    bnb, mv = B["bnb"][par], B["mv"][par]

    def stats(v):
        src = src_fn()
        nh = width // 512
        for i in range(nh - 1):
            v.bn_stats(bnb[:, 6 * i:6 * (i + 1)], src[:, 512 * i:512 * (i + 1)])
        i = nh - 1
        return v.bn_stats(bnb[:, 6 * i:6 * (i + 1)],
                          src[:, 512 * i:512 * (i + 1)])
    P.op("vector", stats, inc="ve")
    P.wait("vector", "ve")
    P.op("vector", lambda v: v.bn_aggr(mv[:], bnb[:, 0:6 * (width // 512)]),
         inc="ve")
    P.wait("vector", "ve")
    return emit_rsqrt_dve(P, B, par)


def emit_ln_tile_bf(P, B, src_dram_ap, k, dst_fm, dst_col, slots, pend=None):
    nslot = len(slots)
    par4 = k % nslot
    par2 = k % 2
    xt = slots[par4]
    lt = (B["lt"] if par2 == 0 else B["x1"]).ap().bitcast(BF16)[:, 0:1024]
    PA = B["PA"]
    P.wait("sync", "gp_lnapply", max(P.v("gp_lnapply") - (nslot - 1), 0))
    P.dma("sync", lambda sy: sy.dma_start(out=xt, in_=src_dram_ap),
          inc=f"ldx{par4}")
    P.wait("vector", "gp_lnapply", max(P.v("gp_lnapply") - (nslot - 1), 0))
    P.wait("vector", f"ldx{par4}")
    bnb, mv, nm, rs = (B["bnb"][par4], B["mv"][par4], B["nm"][par4],
                       B["rs"][par4])

    def stats(v):
        v.bn_stats(bnb[:, 0:6], xt[:, 0:512])
        return v.bn_stats(bnb[:, 6:12], xt[:, 512:1024])
    P.op("vector", stats, inc="ve")
    P.wait("vector", "ve")
    P.op("vector", lambda v: v.bn_aggr(mv[:], bnb[:]), inc="ve")
    P.wait("vector", "ve")
    P.op("vector", lambda v: v.tensor_scalar(nm[:], mv[:, 0:1], -1.0, None,
                                             ALU.mult), inc="ve")
    P.wait("scalar", "ve")
    P.op("scalar", lambda sc: sc.activation(rs[:], mv[:, 1:2], AF.Ln,
                                            bias=B["eps"][:, 0:1]), inc="se")
    P.wait("scalar", "se")
    P.op("scalar", lambda sc: sc.activation(rs[:], rs[:], AF.Exp, scale=-0.5),
         inc="ve_rs")
    if pend is not None:
        pend()
    P.wait("gpsimd", "ve_rs")
    P.wait("gpsimd", "te_T", max(P.v("te_T") - 1, 0))
    P.op("gpsimd", lambda g, lt=lt: g.tensor_scalar(
        lt, xt, nm[:, 0:1], rs[:, 0:1], ALU.add, ALU.mult),
        inc="gp_lnapply")
    P.wait("tensor", "gp_lnapply")
    P.wait("tensor", "ve_Tevac", max(P.v("ve_Tevac") - 2, 0))
    pslot = k % 4
    pbuf = (PA if pslot < 2 else B["PB"]).ap().bitcast(BF16)
    po = 1024 * (pslot % 2)
    for ch in range(8):
        P.op("tensor", lambda t, ch=ch, po=po, lt=lt, pbuf=pbuf: t.transpose(
            pbuf[:, po + 128 * ch: po + 128 * (ch + 1)],
            lt[:, 128 * ch:128 * (ch + 1)], B["identb"][:]),
            inc="te_T" if ch == 7 else None)

    def evac():
        P.wait("scalar", "te_T")
        stride = dst_fm.shape[1] // 8
        P.op("scalar",
             lambda sc, po=po, dst_col=dst_col, dst_fm=dst_fm, stride=stride,
             pbuf=pbuf:
             sc.copy(
                 sub3(dst_fm, dst_col, [stride, 8], [1, 128]),
                 sub3(pbuf, po, [128, 8], [1, 128])),
             inc="ve_Tevac")
    return evac


def emit_ln_tile(P, B, src_dram_ap, par, dst_fm, dst_col, lt2=False):
    xt = B["xt"][par]
    lt = (B["lt"] if (par == 0 or not lt2) else B["x1"])
    PA = B["PA"]
    P.wait("sync", "gp_lnapply", max(P.v("gp_lnapply") - 1, 0))
    P.dma("sync", lambda sy: sy.dma_start(out=xt[:], in_=src_dram_ap),
          inc=f"ldx{par}")
    P.wait("vector", "gp_lnapply", max(P.v("gp_lnapply") - 1, 0))
    P.wait("vector", f"ldx{par}")
    nm, rs = emit_small_ln(P, B, par, lambda: xt[:])
    P.wait("gpsimd", "ve_rs")
    if lt2:
        P.wait("gpsimd", "te_T", max(P.v("te_T") - 1, 0))
    else:
        P.wait("gpsimd", "te_T")
    P.op("gpsimd", lambda g, lt=lt: g.tensor_scalar(
        lt[:], xt[:], nm[:, 0:1], rs[:, 0:1], ALU.add, ALU.mult),
        inc="gp_lnapply")
    P.wait("tensor", "gp_lnapply")
    P.wait("tensor", "ve_Tevac", max(P.v("ve_Tevac") - 1, 0))
    po = 1024 * par
    for ch in range(8):
        P.op("tensor", lambda t, ch=ch, po=po, lt=lt: t.transpose(
            PA[:, po + 128 * ch: po + 128 * (ch + 1)],
            lt[:, 128 * ch:128 * (ch + 1)], B["identf"][:]),
            inc="te_T" if ch == 7 else None)
    P.wait("scalar", "te_T")
    stride = dst_fm.shape[1] // 8
    P.op("scalar", lambda sc, po=po, dst_col=dst_col, dst_fm=dst_fm, stride=stride:
         sc.copy(
             sub3(dst_fm, dst_col, [stride, 8], [1, 128]),
             sub3(PA, po, [128, 8], [1, 128])),
         inc="ve_Tevac")



def emit_attention(P, B, cfg):
    NB = DEBUG_NB or cfg["NB"]
    units = cfg["units"]
    xg, xm = cfg["xg"], cfg["xm"]
    lnf, kf, va, qf, ex, of = (B["lnf"], B["kf"], B["va"], B["qf"], B["ex"],
                               B["of"])
    wq, wk, wv, wu = B["wq"], B["wk"], B["wv"], B["wu"]
    PA, PB = B["PA"], B["PB"]
    lnm = lnf if cfg["lnm_is_lnf"] else B["lnm"]
    masks = B["masks"]

    for name, arena in (("wq", wq), ("wk", wk), ("wv", wv), ("wu", wu)):
        dram = cfg[name]
        for dmc in range(8):
            P.dma("gpsimd", lambda g, dram=dram, arena=arena, dmc=dmc: g.dma_start(
                out=arena[:, 1024 * dmc:1024 * (dmc + 1)],
                in_=dram[128 * dmc:128 * (dmc + 1), :]), inc="ldw")
    P.dma("gpsimd", lambda g: g.dma_start(out=B["bu_row"][:],
                                          in_=cfg["bu"][None, :]), inc="ldw")
    P.dma("sync", lambda sy: sy.dma_start(out=masks[:], in_=cfg["masks_dram"]),
          inc="ld_mask")
    n_ldw = P.v("ldw")
    P.wait("tensor", "ldw", n_ldw)

    for b in range(NB):
        P.wait("vector", "te_v")
        P.wait("vector", "te_s")
        P.wait("vector", "te_q")
        P.wait("vector", "te_up")
        P.wait("gpsimd", "te_av")
        P.wait("tensor", "ve_al")
        P.wait("tensor", "ve_norm")
        if cfg.get("al_on_ex"):
            P.wait("scalar", "st_al0")
            P.wait("scalar", "st_al1")
            P.wait("vector", "st_al0")
            P.wait("vector", "st_al1")
            P.wait("gpsimd", "st_al0")
            P.wait("gpsimd", "st_al1")
            P.wait("sync", "st_al0")
            P.wait("sync", "st_al1")
        slots = cfg["ln_slots"]
        pends = []
        for tt in range(NKT):
            pd = pends.pop(0) if len(pends) >= 2 else None
            pends.append(emit_ln_tile_bf(
                P, B, xg[b, 128 * tt:128 * (tt + 1), :],
                tt, lnf, 128 * tt, slots, pd))
        if not cfg["lnm_is_lnf"]:
            for tt in range(cfg["NQ"] // 128):
                pd = pends.pop(0) if len(pends) >= 2 else None
                pends.append(emit_ln_tile_bf(
                    P, B, xm[b, 128 * tt:128 * (tt + 1), :],
                    tt, lnm, 128 * tt, slots[0:2], pd))
        for pd in pends:
            pd()
        P.wait("tensor", "ve_Tevac")
        P.wait("vector", "gp_lnapply")
        if DEBUG_STAGE == "ln1":
            continue

        for g4 in range(4):
            for hp in range(8):
                par = (g4 * 8 + hp) % 2
                po = 512 * par
                P.wait("tensor", "ve_kevac", max(P.v("ve_kevac") - 1, 0))
                for dmc in range(8):
                    P.op("tensor", lambda t, dmc=dmc, hp=hp, g4=g4, po=po: t.matmul(
                        PA[:, po:po + 512],
                        wk[:, 1024 * dmc + 128 * hp: 1024 * dmc + 128 * (hp + 1)],
                        lnf[:, 2048 * dmc + 512 * g4: 2048 * dmc + 512 * (g4 + 1)],
                        start=(dmc == 0), stop=(dmc == 7), skip_group_check=True),
                        inc="te_k" if dmc == 7 else None)
                P.wait("vector", "te_k")
                P.op("vector", lambda v, hp=hp, g4=g4, po=po: v.tensor_copy(
                    kf[:, 2048 * hp + 512 * g4: 2048 * hp + 512 * (g4 + 1)],
                    PA[:, po:po + 512]), inc="ve_kevac")

        for kt in range(NKT):
            par = kt % 2
            po = 1024 * par
            P.wait("tensor", "gp_vrepack", max(P.v("gp_vrepack") - 1, 0))
            for dmc in range(8):
                for hf in range(2):
                    P.op("tensor", lambda t, dmc=dmc, kt=kt, hf=hf, po=po: t.matmul(
                        PB[:, po + 512 * hf: po + 512 * (hf + 1)],
                        lnf[:, 2048 * dmc + 128 * kt: 2048 * dmc + 128 * (kt + 1)],
                        wv[:, 1024 * dmc + 512 * hf: 1024 * dmc + 512 * (hf + 1)],
                        start=(dmc == 0), stop=(dmc == 7),
                        skip_group_check=True),
                        inc="te_v" if (dmc == 7 and hf == 1) else None)
            P.wait("vector", "te_v")
            P.op("vector", lambda v, po=po, kt=kt: v.tensor_copy(
                sub3(va, 1040 * kt, [65, 16], [1, 64]),
                sub3(B["PB"], po, [64, 16], [1, 64])),
                inc="gp_vrepack")

        if DEBUG_STAGE == "kv":
            continue
        P.wait("tensor", "ve_kevac")
        P.wait("tensor", "gp_vrepack")
        P.wait("vector", "ld_mask", 16)
        for unit in units:
            q_off = unit["q_off"]
            src_stride = lnm.shape[1] // 8
            P.wait("tensor", "ve_norm")
            for hp in range(8):
                P.wait("tensor", "ve_qevac")
                for dmc in range(8):
                    P.op("tensor", lambda t, hp=hp, dmc=dmc, q_off=q_off,
                         src_stride=src_stride: t.matmul(
                             PB[:, 1792:2048],
                             wq[:, 1024 * dmc + 128 * hp: 1024 * dmc + 128 * (hp + 1)],
                             lnm[:, src_stride * dmc + q_off:
                                 src_stride * dmc + q_off + 256],
                             start=(dmc == 0), stop=(dmc == 7),
                             skip_group_check=True),
                         inc="te_q" if dmc == 7 else None)
                P.wait("vector", "te_q")
                P.op("vector", lambda v, hp=hp: v.tensor_copy(
                    qf[0:64, 256 * hp:256 * (hp + 1)], PB[0:64, 1792:2048]))
                P.op("vector", lambda v, hp=hp: v.tensor_copy(
                    qf[64:128, 2048 + 256 * hp:2048 + 256 * (hp + 1)],
                    PB[64:128, 1792:2048]), inc="ve_qevac")
            P.wait("tensor", "ve_qevac")
            if DEBUG_STAGE == "q":
                continue

            for quad in range(4):
                ktl = unit["ktiles"]
                nkt_u = len(ktl)
                P.wait("tensor", "ve_al")

                exp_base = P.v("se_exp")

                def emit_av(idx, quad=quad, ktl=ktl, nkt_u=nkt_u,
                            exp_base=exp_base):
                    i, vs, d_lo, d_w = ktl[idx]
                    P.wait("tensor", "se_exp", exp_base + idx + 1)
                    if idx == 0:
                        P.wait("tensor", "ve_norm")
                    for hq in range(4):
                        h = 4 * quad + hq
                        P.op("tensor", lambda t, i=i, vs=vs, hq=hq, h=h, idx=idx,
                             fb=(idx == 0 and hq % 2 == 0),
                             last=(idx == nkt_u - 1): t.matmul(
                                 PB[0:65, 256 * hq + vs: 256 * (hq + 1)],
                                 va[:, 1040 * i + 65 * h: 1040 * i + 65 * (h + 1)],
                                 sub2(ex, 1024 * (idx % 2) + 256 * hq + vs,
                                      [1, 256 - vs]),
                                 start=fb, stop=last, skip_group_check=True),
                             inc="te_av" if hq == 3 else None)

                for idx, (i, vs, d_lo, d_w) in enumerate(ktl):
                    slot = 1024 * (idx % 2)
                    P.wait("tensor", "se_exp", max(P.v("se_exp") - 1, 0))
                    for hq in range(4):
                        h = 4 * quad + hq
                        hp = h // 2
                        qo = 2048 * (h % 2) + 256 * hp
                        P.op("tensor",
                             lambda t, i=i, vs=vs, hq=hq, hp=hp, qo=qo, slot=slot: t.matmul(
                                 PA[:, slot + 256 * hq + vs: slot + 256 * (hq + 1)],
                                 kf[:, 2048 * hp + 128 * i: 2048 * hp + 128 * (i + 1)],
                                 qf[:, qo + vs: qo + 256],
                                 start=(hq % 2 == 0), stop=False,
                                 skip_group_check=True),
                             inc="te_s" if hq == 3 else None)
                    if d_w:
                        for hq in range(4):
                            P.op("tensor", lambda t, hq=hq, d_lo=d_lo, d_w=d_w,
                                 slot=slot: t.matmul(
                                     PA[:, slot + 256 * hq + d_lo:
                                        slot + 256 * hq + d_lo + d_w],
                                     B["identb"][:],
                                     masks[:, 0:d_w],
                                     start=False, stop=True,
                                     skip_group_check=True),
                                 inc="te_s" if hq == 3 else None)
                    P.wait("scalar", "te_s")
                    P.wait("scalar", "te_av")
                    P.op("scalar", lambda s, vs=vs, slot=slot, idx=idx: s.activation(
                        sub3(ex, 1024 * (idx % 2) + vs, [256, 4], [1, 256 - vs]),
                        sub3(PA, slot + vs, [256, 4], [1, 256 - vs]),
                        AF.Exp, scale=1.0 / 64.0), inc="se_exp")
                    if idx > 0:
                        emit_av(idx - 1)
                emit_av(nkt_u - 1)
                P.wait("scalar", "te_av")
                P.op("scalar", lambda s: s.activation(
                    PB[64:65, 1024:2048], PB[64:65, 0:1024], AF.Ln), inc="se_ln")
                P.wait("scalar", "se_ln")
                P.wait("scalar", "ve_norm")
                P.op("scalar", lambda s: s.activation(
                    B["rb"][0:1, 0:1024], PB[64:65, 1024:2048], AF.Exp,
                    scale=-1.0), inc="se_r2")
                P.wait("tensor", "se_r2")
                P.wait("tensor", "ve_bc")
                for j in range(2):
                    P.op("tensor", lambda t, j=j: t.matmul(
                        PB[0:64, 1024 + 512 * j:1024 + 512 * (j + 1)],
                        B["ones1"][0:1, 0:64],
                        B["rb"][0:1, 512 * j:512 * (j + 1)],
                        start=True, stop=True, skip_group_check=True),
                        inc="te_bc" if j == 1 else None)
                P.wait("vector", "te_bc")
                P.op("vector", lambda v: v.tensor_copy(
                    B["rb"][0:64, 0:1024], PB[0:64, 1024:2048]), inc="ve_bc")
                P.wait("vector", "ve_bc")
                P.wait("vector", "te_up")
                for hq in range(4):
                    h = 4 * quad + hq
                    hp, hlo = h // 2, 64 * (h % 2)
                    P.op("vector", lambda v, hq=hq, hp=hp, hlo=hlo: v.tensor_mul(
                        of[hlo:hlo + 64, 256 * hp:256 * (hp + 1)],
                        PB[0:64, 256 * hq:256 * (hq + 1)],
                        B["rb"][0:64, 256 * hq:256 * (hq + 1)]),
                        inc="ve_norm")
            if DEBUG_STAGE == "norm":
                continue
            P.wait("tensor", "ve_norm")
            for st in range(2):
                po = 1024 * st
                for dc in range(8):
                    for j in range(2):
                        P.op("tensor", lambda t, dc=dc, j=j, st=st, po=po: t.matmul(
                            PA[:, po + 512 * j: po + 512 * (j + 1)],
                            of[:, 256 * dc + 128 * st: 256 * dc + 128 * (st + 1)],
                            wu[:, 1024 * dc + 512 * j: 1024 * dc + 512 * (j + 1)],
                            start=(dc == 0), stop=False,
                            skip_group_check=True))
                for j in range(2):
                    P.op("tensor", lambda t, j=j, po=po: t.matmul(
                        PA[:, po + 512 * j: po + 512 * (j + 1)],
                        B["ones1"][0:1, :],
                        B["bu_row"][0:1, 512 * j:512 * (j + 1)],
                        start=False, stop=True, skip_group_check=True),
                        inc="te_up" if j == 1 else None)
                P.wait("vector", "te_up")
                P.wait("vector", f"st_al{st}")
                nm, rs = emit_small_ln(P, B, st,
                                       lambda po=po: PA[:, po:po + 1024])
                P.wait("vector", "ve_rs")
                P.op("vector", lambda v, st=st, po=po, nm=nm, rs=rs: v.tensor_scalar(
                    B["al"][st][:], PA[:, po:po + 1024], nm[:, 0:1], rs[:, 0:1],
                    ALU.add, ALU.mult), inc="ve_al")
                P.wait("sync", "ve_al")
                dst = cfg["aln_dst"](b, unit["row0"] + 128 * st)
                P.dma("sync", lambda sy, dst=dst, st=st: sy.dma_start(
                    out=dst, in_=B["al"][st][:]), inc=f"st_al{st}")
    P.wait("sync", "st_al0")
    P.wait("sync", "st_al1")



def emit_ffn(P, B, cfg):
    NT = 16
    PA, PB = B["PA"], B["PB"]
    xm = cfg["xm_ffn"]
    identf = B["identf"]

    wlist = [("wf", "wfT"), ("wi", "wiT"), ("wc", "wcT"), ("uf", "ufT"),
             ("ui", "uiT"), ("uc", "ucT"), ("wo", "woT"), ("uo", "uoT")]
    for src_name, dst_name in wlist:
        dram = cfg[src_name]
        dst = B[dst_name]
        for rnd in range(4):
            P.wait("sync", "te_wT")
            for oc in range(2):
                P.dma("sync", lambda sy, dram=dram, oc=oc, rnd=rnd:
                      sy.dma_start(
                          out=B["xt"][oc][:],
                          in_=dram[256 * rnd + 128 * oc:256 * rnd + 128 * (oc + 1), :]),
                      inc="ld_ws")
            P.wait("tensor", "ld_ws")
            for ic in range(8):
                P.wait("tensor", "ve_wevac")
                for oc in range(2):
                    P.op("tensor", lambda t, ic=ic, oc=oc: t.transpose(
                        PA[:, 128 * oc:128 * (oc + 1)],
                        B["xt"][oc][:, 128 * ic: 128 * (ic + 1)],
                        identf[:]),
                        inc="te_wT" if oc == 1 else None)
                P.wait("vector", "te_wT")
                P.op("vector", lambda v, ic=ic, rnd=rnd, dst=dst: v.tensor_copy(
                    dst[:, 1024 * ic + 256 * rnd: 1024 * ic + 256 * rnd + 256],
                    PA[:, 0:256]), inc="ve_wevac")
    P.wait("tensor", "ve_wevac")

    for gi, bname in enumerate(("bf", "bi", "bc")):
        for mc in range(8):
            P.dma("sync", lambda sy, bname=bname, gi=gi, mc=mc: sy.dma_start(
                out=B["bcol"][:, 8 * gi + mc: 8 * gi + mc + 1],
                in_=cfg[bname][128 * mc:128 * (mc + 1), None]), inc="ldb")
    for mc in range(8):
        P.dma("sync", lambda sy, mc=mc: sy.dma_start(
            out=B["inccol"][:, mc:mc + 1],
            in_=cfg["in_c"][128 * mc:128 * (mc + 1), None]), inc="ldb")
    P.dma("gpsimd", lambda g: g.dma_start(out=B["bo_row"][:],
                                          in_=cfg["bo"][None, :]), inc="ldw2")
    P.wait("scalar", "ldb")
    P.wait("scalar", "ldw2")

    P.wait("sync", "te_wT")

    def b1_spill(t):
        P.wait("sync", "ve_l2")
        P.dma("sync", lambda sy, t=t: sy.dma_start(
            out=cfg["ln2_dram"][t], in_=B["l2"][0][:]), inc="st_l2")

    pend = None
    spill_after = None
    for t in range(NT):
        for st in range(2):
            par = st
            P.wait("sync", "ve_x1", max(P.v("ve_x1") - 1, 0))
            P.dma("sync", lambda sy, t=t, st=st, par=par: sy.dma_start(
                out=B["xt"][par][:], in_=xm[t, 128 * st:128 * (st + 1), :]),
                inc=f"ldx{par}")
            P.dma("sync", lambda sy, t=t, st=st, par=par: sy.dma_start(
                out=B["alt"][par][:], in_=cfg["aln_src"](t, 128 * st)),
                inc=f"ldx{par}")
            P.wait("vector", f"ldx{par}")
            P.wait("vector", "gp_lnapply")
            P.op("vector", lambda v, par=par: v.tensor_add(
                B["x1"][:], B["xt"][par][:], B["alt"][par][:]), inc="ve_x1")
            P.wait("vector", "ve_x1")
            nm, rs = emit_small_ln(P, B, par, lambda: B["x1"][:])
            if pend is not None:
                pend()
            P.wait("gpsimd", "ve_rs")
            P.wait("gpsimd", "te_T")
            ltb = B["lt"].ap().bitcast(BF16)[:, 0:1024]
            P.op("gpsimd", lambda g, nm=nm, rs=rs, ltb=ltb: g.tensor_scalar(
                ltb, B["x1"][:], nm[:, 0:1], rs[:, 0:1],
                ALU.add, ALU.mult), inc="gp_lnapply")
            P.wait("tensor", "gp_lnapply")
            P.wait("tensor", "ve_l2", max(P.v("ve_l2") - 1, 0))
            po = 1024 * par
            PAb = PA.ap().bitcast(BF16)
            for ch in range(8):
                P.op("tensor", lambda t_, ch=ch, po=po, ltb=ltb, PAb=PAb:
                     t_.transpose(
                         PAb[:, po + 128 * ch:po + 128 * (ch + 1)],
                         ltb[:, 128 * ch:128 * (ch + 1)], B["identb"][:]),
                     inc="te_T" if ch == 7 else None)

            def evac(po=po, st=st, t=t, PAb=PAb):
                P.wait("scalar", "te_T")
                P.wait("scalar", "st_l2", 16 * t)
                P.op("scalar", lambda sc, po=po, st=st, PAb=PAb: sc.copy(
                    sub3(B["l2"][0], 128 * st, [256, 8], [1, 128]),
                    sub3(PAb, po, [128, 8], [1, 128])),
                    inc="ve_l2")
            pend = evac
            if st == 0 and spill_after is not None:
                b1_spill(spill_after)
                spill_after = None
        spill_after = t
    if pend is not None:
        pend()
    b1_spill(spill_after)

    P.wait("scalar", "ve_x1")
    P.wait("scalar", "ve_l2")
    for mc in range(8):
        P.op("scalar", lambda s, mc=mc: s.activation(
            B["c"][0][:, 256 * mc:256 * (mc + 1)], B["x1"][:, 0:256],
            AF.Identity, bias=B["inccol"][:, mc:mc + 1], scale=0.0),
            inc="se_c0")
    ceng0 = "vector" if B.get("c_in_psum") else "gpsimd"
    P.wait(ceng0, "se_c0")
    P.op(ceng0, lambda g: g.tensor_copy(B["cb"][0][:], B["c"][0][:]),
         inc="gp_cb")
    P.wait("vector", "se_c0")

    P.wait("sync", "st_l2")
    for t in range(NT):
        par = t % 2
        P.wait("sync", "te_g", max(P.v("te_g") - 24, 0))
        P.dma("sync", lambda sy, t=t, par=par: sy.dma_start(
            out=B["l2"][par][:], in_=cfg["ln2_dram"][t]), inc=f"ldl2_{par}")
        P.wait("tensor", f"ldl2_{par}")
        for mc in range(8):
            pgo = 1024 * (mc % 2)
            P.wait("tensor", "se_g", max(P.v("se_g") - 3, 0))
            for gi, wT, uT in ((0, "wfT", "ufT"), (1, "wiT", "uiT"),
                               (2, "wcT", "ucT")):
                doff = pgo + 256 * gi if gi < 2 else pgo + 512
                for ic in range(8):
                    P.op("tensor", lambda t_, wT=wT, ic=ic, mc=mc, doff=doff,
                         par=par, first=(ic == 0 and gi in (0, 2)): t_.matmul(
                             PB[:, doff:doff + 256],
                             B[wT][:, 1024 * ic + 128 * mc: 1024 * ic + 128 * (mc + 1)],
                             B["l2"][par][:, 256 * ic:256 * (ic + 1)],
                             start=first, stop=False, skip_group_check=True))
            P.wait("tensor", "gp_cb", 8 * t + 1)
            for gi, wT, uT in ((0, "wfT", "ufT"), (1, "wiT", "uiT"),
                               (2, "wcT", "ucT")):
                doff = pgo + 256 * gi if gi < 2 else pgo + 512
                for jc in range(8):
                    P.op("tensor", lambda t_, uT=uT, jc=jc, mc=mc, doff=doff,
                         par=par: t_.matmul(
                             PB[:, doff:doff + 256],
                             B[uT][:, 1024 * jc + 128 * mc: 1024 * jc + 128 * (mc + 1)],
                             B["cb"][par][:, 256 * jc:256 * (jc + 1)],
                             start=False, stop=(jc == 7), skip_group_check=True),
                         inc="te_g" if jc == 7 else None)
            P.wait("scalar", "te_g")
            P.wait("scalar", "ve_c")
            for gi, func in ((0, AF.Sigmoid), (1, AF.Sigmoid), (2, AF.Tanh)):
                doff = pgo + 256 * gi if gi < 2 else pgo + 512
                P.op("scalar", lambda s, gi=gi, mc=mc, doff=doff, func=func:
                     s.activation(
                         B["gates"][gi][:], PB[:, doff:doff + 256], func,
                         bias=B["bcol"][:, 8 * gi + mc:8 * gi + mc + 1]),
                     inc="se_g")
            P.wait("vector", "se_g")
            P.wait("vector", "ve_c", max(P.v("ve_c") - 7, 0))
            cs = slice(256 * mc, 256 * (mc + 1))
            P.op("vector", lambda v, cs=cs, par=par: v.tensor_mul(
                B["c"][1 - par][:, cs], B["gates"][0][:], B["c"][par][:, cs]),
                inc="ve_cu")
            P.wait("vector", "ve_cu")
            P.op("vector", lambda v: v.tensor_mul(
                B["gates"][0][:], B["gates"][1][:], B["gates"][2][:]),
                inc="ve_cu")
            P.wait("vector", "ve_cu")
            P.op("vector", lambda v, cs=cs, par=par: v.tensor_add(
                B["c"][1 - par][:, cs], B["c"][1 - par][:, cs],
                B["gates"][0][:]), inc="ve_c")
            ceng = "vector" if B.get("c_in_psum") else "gpsimd"
            P.wait(ceng, "ve_c")
            P.op(ceng, lambda g, cs=cs, par=par: g.tensor_copy(
                B["cb"][1 - par][:, cs], B["c"][1 - par][:, cs]), inc="gp_cb")

    cpar = NT % 2
    P.wait("tensor", "gp_cb", 8 * NT + 1)

    for st in range(2):
        po = 1024 * st
        for mc in range(8):
            for j in range(2):
                P.op("tensor", lambda t, mc=mc, j=j, st=st, po=po: t.matmul(
                    PA[:, po + 512 * j:po + 512 * (j + 1)],
                    B["cb"][cpar][:, 256 * mc + 128 * st: 256 * mc + 128 * (st + 1)],
                    B["uoT"][:, 1024 * mc + 512 * j: 1024 * mc + 512 * (j + 1)],
                    start=(mc == 0), stop=False, skip_group_check=True))
        for j in range(2):
            P.op("tensor", lambda t, j=j, po=po: t.matmul(
                PA[:, po + 512 * j:po + 512 * (j + 1)],
                B["ones1"][0:1, :], B["bo_row"][0:1, 512 * j:512 * (j + 1)],
                start=False, stop=True, skip_group_check=True),
                inc="te_cuo" if j == 1 else None)
    P.wait("vector", "te_cuo")
    P.op("vector", lambda v: v.tensor_copy(B["cuo"][:], PA[:]), inc="ve_cuo")

    P.wait("tensor", "ve_cuo")
    P.wait("vector", "ve_cuo")
    P.wait("scalar", "ve_cuo")
    P.wait("sync", "te_g")
    PAap = B["PA"].ap()
    hhs = [PAap[:, 0:1024], PAap[:, 1024:2048]]
    oos = [B["oo"], B["x1"].ap()[:, 0:1024]]
    l2base = [P.v("ldl2_0"), P.v("ldl2_1")]
    P.dma("sync", lambda sy: sy.dma_start(
        out=B["l2"][0][:], in_=cfg["ln2_dram"][0]), inc="ldl2_0")
    for t in range(NT):
        par = t % 2
        if t + 1 < NT:
            P.wait("sync", "te_xwo", max(P.v("te_xwo") - 2, 0))
            P.dma("sync", lambda sy, t=t: sy.dma_start(
                out=B["l2"][(t + 1) % 2][:], in_=cfg["ln2_dram"][t + 1]),
                inc=f"ldl2_{(t + 1) % 2}")
        P.wait("tensor", f"ldl2_{par}", l2base[par] + 16 * (t // 2 + 1))
        for st in range(2):
            po = 1024 * st
            P.wait("tensor", "se_h", max(P.v("se_h") - (1 - st), 0))
            for ic in range(8):
                for j in range(2):
                    P.op("tensor", lambda t_, ic=ic, j=j, st=st, po=po, par=par:
                         t_.matmul(
                             PB[:, po + 512 * j:po + 512 * (j + 1)],
                             B["l2"][par][:, 256 * ic + 128 * st:
                                           256 * ic + 128 * (st + 1)],
                             B["woT"][:, 1024 * ic + 512 * j:
                                      1024 * ic + 512 * (j + 1)],
                             start=(ic == 0), stop=(ic == 7),
                             skip_group_check=True),
                         inc="te_xwo" if (ic == 7 and j == 1) else None)
        for st in range(2):
            po = 1024 * st
            P.wait("vector", "te_xwo", P.v("te_xwo") - (1 - st))
            P.op("vector", lambda v, st=st, po=po: v.tensor_add(
                PB[:, po:po + 1024], PB[:, po:po + 1024],
                B["cuo"][:, 1024 * st:1024 * (st + 1)]), inc="ve_ph")
        for st in range(2):
            po = 1024 * st
            P.wait("scalar", "ve_ph", P.v("ve_ph") - (1 - st))
            P.op("scalar", lambda s, po=po, st=st: s.activation(
                hhs[st], PB[:, po:po + 1024], AF.Tanh), inc="se_h")
        for st in range(2):
            P.wait("sync", "ve_o", max(P.v("ve_o") - (1 - st), 0))
            P.dma("sync", lambda sy, t=t, st=st: sy.dma_start(
                out=B["alt"][st][:], in_=cfg["aln_src"](t, 128 * st)),
                inc=f"ldo{st}")
            P.dma("sync", lambda sy, t=t, st=st: sy.dma_start(
                out=B["xt"][st][:], in_=xm[t, 128 * st:128 * (st + 1), :]),
                inc=f"ldo{st}")
        nmrs = []
        for st in range(2):
            P.wait("vector", "se_h", P.v("se_h") - (1 - st))
            nmrs.append(emit_small_ln_dve(P, B, st,
                                          lambda st=st: hhs[st]))
        for st in range(2):
            P.wait("gpsimd", f"ldo{st}")
            P.wait("gpsimd", "ve_o", max(P.v("ve_o") - (1 - st), 0))
            P.op("gpsimd", lambda g, st=st: g.tensor_add(
                B["xt"][st][:], B["xt"][st][:], B["alt"][st][:]),
                inc="gp_ax")
        for st in range(2):
            nm, rs = nmrs[st]
            P.wait("vector", f"st_y{st}")
            P.op("vector", lambda v, nm=nm, rs=rs, st=st: v.tensor_scalar(
                oos[st][:], hhs[st], nm[:, 0:1], rs[:, 0:1],
                ALU.add, ALU.mult), inc="ve_oc")
            P.wait("vector", "ve_oc")
            P.wait("vector", "gp_ax", P.v("gp_ax") - (1 - st))
            P.op("vector", lambda v, st=st: v.tensor_add(
                oos[st][:], oos[st][:], B["xt"][st][:]), inc="ve_o")
            P.wait("sync", "ve_o")
            P.dma("sync", lambda sy, t=t, st=st: sy.dma_start(
                out=cfg["y"][t, 128 * st:128 * (st + 1), :], in_=oos[st][:]),
                inc=f"st_y{st}")
    P.wait("sync", "st_y0")
    P.wait("sync", "st_y1")



def make_units(version):
    if version == "fused":
        ktiles = [(i, 16 * i, 16 * i, 16) for i in range(16)]
        return [{"q_off": 0, "row0": 0, "ktiles": ktiles}]
    units = []
    for u in range(8):
        kt = [(i, 0, 0, 0) for i in range(2 * u)]
        kt.append((2 * u, 0, 0, 128))
        kt.append((2 * u + 1, 128, 128, 128))
        units.append({"q_off": 256 * u, "row0": 256 * u, "ktiles": kt})
    return units


def build_attn_program(version):
    from contextlib import ExitStack
    nc = bass.Bass("TRN2", target_bir_lowering=False)
    ctx = ExitStack()
    NB = 2 if version == "unfused" else 16
    NQ = 2048 if version == "unfused" else 256
    mask_w = 128 if version == "unfused" else 16

    xg = nc.dram_tensor("xg", [NB, SEQ, D], F32, kind="ExternalInput").ap()
    wq = nc.dram_tensor("wq", [D, D], F32, kind="ExternalInput").ap()
    wk = nc.dram_tensor("wk", [D, D], F32, kind="ExternalInput").ap()
    wv = nc.dram_tensor("wv", [D, D], F32, kind="ExternalInput").ap()
    wu = nc.dram_tensor("wu", [D, D], F32, kind="ExternalInput").ap()
    bu = nc.dram_tensor("bu", [D], F32, kind="ExternalInput").ap()
    masks_dram = nc.dram_tensor("masks", [128, 8 * mask_w], BF16,
                                kind="ExternalInput").ap()
    aln = nc.dram_tensor("aln", [NB, NQ, D], BF16, kind="ExternalOutput").ap()

    with ctx:
        B = alloc_common(nc, ctx, need_attn=True, need_ffn=False)
        B["masks"] = ctx.enter_context(
            nc.sbuf_tensor("masks_sb", [128, 8 * mask_w], BF16))
        P = Plan(nc, ctx)
        blk = ctx.enter_context(nc.Block())
        cfg = {
            "NB": NB, "NQ": NQ, "units": make_units(version),
            "xg": xg, "xm": None, "lnm_is_lnf": True,
            "wq": wq, "wk": wk, "wv": wv, "wu": wu, "bu": bu,
            "masks_dram": masks_dram, "mask_w": mask_w,
            "aln_dst": lambda b, r0: aln[b, r0:r0 + 128, :],
        }
        emit_setup(P, B)
        emit_attention(P, B, cfg)
        P.run(blk)
    return nc


def build_ffn_program():
    from contextlib import ExitStack
    nc = bass.Bass("TRN2", target_bir_lowering=False)
    ctx = ExitStack()
    NT = 16
    xm = nc.dram_tensor("xm", [NT, 256, D], F32, kind="ExternalInput").ap()
    aln = nc.dram_tensor("aln", [NT, 256, D], BF16, kind="ExternalInput").ap()
    drams = {n: nc.dram_tensor(n, [D, D], F32, kind="ExternalInput").ap()
             for n in ("wf", "wi", "wc", "wo", "uf", "ui", "uc", "uo")}
    bias = {n: nc.dram_tensor(n, [D], F32, kind="ExternalInput").ap()
            for n in ("bf", "bi", "bc", "bo", "in_c")}
    y = nc.dram_tensor("y", [NT, 256, D], F32, kind="ExternalOutput").ap()
    ln2_dram = nc.dram_tensor("ln2s", [NT, 128, 2048], BF16, kind="Internal")

    with ctx:
        B = alloc_common(nc, ctx, need_attn=False, need_ffn=True)
        P = Plan(nc, ctx)
        blk = ctx.enter_context(nc.Block())
        cfg = dict(drams)
        cfg.update(bias)
        cfg.update({
            "xm_ffn": xm, "y": y,
            "aln_src": lambda t, r0: aln[t, r0:r0 + 128, :],
            "ln2_dram": ln2_dram.ap(),
        })
        emit_setup(P, B)
        emit_ffn(P, B, cfg)
        P.run(blk)
    return nc


def build_fused_program():
    from contextlib import ExitStack
    nc = bass.Bass("TRN2", target_bir_lowering=False)
    ctx = ExitStack()
    NB, NQ, mask_w, NT = 16, 256, 16, 16

    xg = nc.dram_tensor("xg", [NB, SEQ, D], F32, kind="ExternalInput").ap()
    xm = nc.dram_tensor("xm", [NT, 256, D], F32, kind="ExternalInput").ap()
    attnw = {n: nc.dram_tensor(n, [D, D], F32, kind="ExternalInput").ap()
             for n in ("wq", "wk", "wv", "wu")}
    bu = nc.dram_tensor("bu", [D], F32, kind="ExternalInput").ap()
    ffnw = {n: nc.dram_tensor(n, [D, D], F32, kind="ExternalInput").ap()
            for n in ("wf", "wi", "wc", "wo", "uf", "ui", "uc", "uo")}
    bias = {n: nc.dram_tensor(n, [D], F32, kind="ExternalInput").ap()
            for n in ("bf", "bi", "bc", "bo", "in_c")}
    masks_dram = nc.dram_tensor("masks", [128, 8 * mask_w], BF16,
                                kind="ExternalInput").ap()
    y = nc.dram_tensor("y", [NT, 256, D], F32, kind="ExternalOutput").ap()
    aln_dram = nc.dram_tensor("alns", [NB, NQ, D], BF16, kind="Internal")
    ln2_dram = nc.dram_tensor("ln2s", [NT, 128, 2048], BF16, kind="Internal")

    with ctx:
        B = alloc_common(nc, ctx, need_attn=True, need_ffn="alias")
        B["lnm"] = B["of"].ap()[:, :]
        B["masks"] = ctx.enter_context(
            nc.sbuf_tensor("masks_sb", [128, 8 * mask_w], BF16))
        exap = B["ex"].ap()
        B["al"] = [exap[:, 0:1024], exap[:, 1024:2048]]
        B["vt"] = exap[:, 0:1024]
        ln_slots = [B["xt"][0].ap()[:, :], B["xt"][1].ap()[:, :],
                    B["ex"].ap().bitcast(F32),
                    B["of"].ap().bitcast(F32)]
        P = Plan(nc, ctx)
        blk = ctx.enter_context(nc.Block())
        cfg_a = {
            "NB": NB, "NQ": NQ, "units": make_units("fused"),
            "xg": xg, "xm": xm, "lnm_is_lnf": False,
            "wq": attnw["wq"], "wk": attnw["wk"], "wv": attnw["wv"],
            "wu": attnw["wu"], "bu": bu,
            "masks_dram": masks_dram, "mask_w": mask_w,
            "aln_dst": lambda b, r0: aln_dram.ap()[b, r0:r0 + 128, :],
            "al_on_ex": True, "ln_slots": ln_slots,
        }
        cfg_f = dict(ffnw)
        cfg_f.update(bias)
        cfg_f.update({
            "xm_ffn": xm, "y": y,
            "aln_src": lambda t, r0: aln_dram.ap()[t, r0:r0 + 128, :],
            "ln2_dram": ln2_dram.ap(),
        })
        emit_setup(P, B)
        emit_attention(P, B, cfg_a)
        for eng in ENGINES:
            P.wait(eng, "st_al0")
            P.wait(eng, "st_al1")
        if not SKIP_FFN:
            emit_ffn(P, B, cfg_f)
        P.run(blk)
    return nc


def rows_for_core(c):
    return np.array([16 * (8 * k + c) + r for k in range(16)
                     for r in range(16)], dtype=np.int64)


def make_masks(version, core):
    mask_w = 128 if version == "unfused" else 16
    rk = np.arange(128)[:, None]
    jj = np.arange(mask_w)[None, :]
    if version == "unfused":
        m = np.where(rk <= jj, 0.0, MASKVAL)
    else:
        m = np.where(rk <= 16 * core + jj, 0.0, MASKVAL)
    return np.ascontiguousarray(np.tile(m, (1, 8)).astype(ml_dtypes.bfloat16))


_CACHE = {}


def fused_in_map(x, w, c):
    rows = rows_for_core(c)
    return {
        "xg": x,
        "xm": np.ascontiguousarray(x[:, rows, :]),
        "wq": w["L0_attn_w_q"], "wk": w["L0_attn_w_k"],
        "wv": w["L0_attn_w_v"], "wu": w["L0_attn_w_u"],
        "bu": w["L0_attn_b_u"],
        "wf": w["L0_ffn_wf"], "wi": w["L0_ffn_wi"], "wc": w["L0_ffn_wc"],
        "wo": w["L0_ffn_wo"], "uf": w["L0_ffn_uf"], "ui": w["L0_ffn_ui"],
        "uc": w["L0_ffn_uc"], "uo": w["L0_ffn_uo"],
        "bf": w["L0_ffn_bf"], "bi": w["L0_ffn_bi"], "bc": w["L0_ffn_bc"],
        "bo": w["L0_ffn_bo"], "in_c": w["L0_ffn_in_c"],
        "masks": make_masks("fused", c),
    }


def kernel_fused(**inp):
    x = np.ascontiguousarray(inp["L0_attn_x"])
    if "fused" not in _CACHE:
        _CACHE["fused"] = build_fused_program()
    in_maps = [fused_in_map(x, inp, c) for c in range(NCORES)]
    res = run_bass_kernel_spmd(_CACHE["fused"], in_maps,
                               core_ids=list(range(NCORES)))
    out = np.empty((16, SEQ, D), dtype=np.float32)
    for c in range(NCORES):
        out[:, rows_for_core(c), :] = np.asarray(res.results[c]["y"],
                                                 dtype=np.float32)
    return out


USE_FUSED = True


def kernel(**inp):
    if USE_FUSED:
        return kernel_fused(**inp)
    x = np.ascontiguousarray(inp["L0_attn_x"])
    w = inp

    version = "unfused"
    if "attn" not in _CACHE:
        _CACHE["attn"] = build_attn_program(version)
    in_maps = []
    for c in range(NCORES):
        in_maps.append({
            "xg": x[2 * c:2 * c + 2],
            "wq": w["L0_attn_w_q"], "wk": w["L0_attn_w_k"],
            "wv": w["L0_attn_w_v"], "wu": w["L0_attn_w_u"],
            "bu": w["L0_attn_b_u"],
            "masks": make_masks(version, c),
        })
    res = run_bass_kernel_spmd(_CACHE["attn"], in_maps,
                               core_ids=list(range(NCORES)))
    aln_full = np.concatenate([r["aln"] for r in res.results], axis=0)

    if "ffn" not in _CACHE:
        _CACHE["ffn"] = build_ffn_program()
    in_maps = []
    for c in range(NCORES):
        rows = slice(256 * c, 256 * (c + 1))
        in_maps.append({
            "xm": np.ascontiguousarray(x[:, rows, :]),
            "aln": np.ascontiguousarray(aln_full[:, rows, :]),
            "wf": w["L0_ffn_wf"], "wi": w["L0_ffn_wi"], "wc": w["L0_ffn_wc"],
            "wo": w["L0_ffn_wo"], "uf": w["L0_ffn_uf"], "ui": w["L0_ffn_ui"],
            "uc": w["L0_ffn_uc"], "uo": w["L0_ffn_uo"],
            "bf": w["L0_ffn_bf"], "bi": w["L0_ffn_bi"], "bc": w["L0_ffn_bc"],
            "bo": w["L0_ffn_bo"], "in_c": w["L0_ffn_in_c"],
        })
    res = run_bass_kernel_spmd(_CACHE["ffn"], in_maps,
                               core_ids=list(range(NCORES)))
    y = np.concatenate([r["y"] for r in res.results], axis=1)
    return np.ascontiguousarray(y.astype(np.float32))


# version 21
# speedup vs baseline: 1.1021x; 1.1021x over previous
import numpy as np
import ml_dtypes
import concourse.bass as bass
import concourse.mybir as mybir
from concourse.bass_utils import run_bass_kernel_spmd

F32 = mybir.dt.float32
BF16 = mybir.dt.bfloat16
AF = mybir.ActivationFunctionType
ALU = mybir.AluOpType

NCORES = 8
SEQ = 2048
D = 1024
NKT = SEQ // 128
EPS = 1e-5
MASKVAL = -30000.0

ENGINES = ("sync", "scalar", "vector", "tensor", "gpsimd")
DEBUG_NB = None
DEBUG_STAGE = "full"
SKIP_FFN = False


class Plan:

    def __init__(self, nc, ctx):
        self.nc = nc
        self.ctx = ctx
        self.ops = {e: [] for e in ENGINES}
        self.sems = {}
        self.val = {}

    def sem(self, name):
        if name not in self.sems:
            self.sems[name] = self.ctx.enter_context(self.nc.semaphore(name))
            self.val[name] = 0
        return self.sems[name]

    def op(self, eng, fn, inc=None, inc_amt=1):
        semh = self.sem(inc) if inc else None
        self.ops[eng].append(("op", fn, semh, inc_amt))
        if inc:
            self.val[inc] += inc_amt

    def dma(self, eng, fn, inc):
        self.op(eng, fn, inc=inc, inc_amt=16)

    def wait(self, eng, sem, value=None):
        if value is None:
            value = self.val.get(sem, 0)
        if value > 0:
            self.ops[eng].append(("wait", self.sem(sem), value))

    def v(self, sem):
        return self.val.setdefault(sem, 0)

    def run(self, block):
        def make(name):
            items = self.ops[name]

            def body(eng):
                for it in items:
                    if it[0] == "wait":
                        eng.wait_ge(it[1], it[2])
                    else:
                        inst = it[1](eng)
                        if it[2] is not None:
                            inst.then_inc(it[2], it[3])
            return body
        block.sync(make("sync"))
        block.scalar(make("scalar"))
        block.vector(make("vector"))
        block.tensor(make("tensor"))
        block.gpsimd(make("gpsimd"))




def _apof(buf):
    return buf.ap() if hasattr(buf, "ap") and callable(getattr(buf, "ap")) \
        else buf


def sub3(buf, off, d1, d2):
    a = _apof(buf)
    return bass.AP(a.tensor, a.offset + off, [[a.ap[0][0], a.ap[0][1]], d1, d2])


def sub2(buf, off, d1):
    a = _apof(buf)
    return bass.AP(a.tensor, a.offset + off, [[a.ap[0][0], a.ap[0][1]], d1])


def rowstride(buf):
    return _apof(buf).ap[0][0]



def alloc_common(nc, ctx, need_attn, need_ffn):
    E = ctx.enter_context
    B = {}
    B["PA"] = E(nc.psum_tensor("PA", [128, 2048], F32))
    B["PB"] = E(nc.psum_tensor("PB", [128, 2048], F32))
    B["bnb"] = [E(nc.sbuf_tensor(f"bnb{i}", [128, 12], F32)) for i in range(4)]
    B["mv"] = [E(nc.sbuf_tensor(f"mv{i}", [128, 2], F32)) for i in range(4)]
    B["nm"] = [E(nc.sbuf_tensor(f"nm{i}", [128, 1], F32)) for i in range(4)]
    B["rs"] = [E(nc.sbuf_tensor(f"rs{i}", [128, 1], F32)) for i in range(4)]
    B["sq1"] = [E(nc.sbuf_tensor(f"sq1_{i}", [128, 1], F32)) for i in range(4)]
    B["sq2"] = [E(nc.sbuf_tensor(f"sq2_{i}", [128, 1], F32)) for i in range(4)]
    B["eps"] = E(nc.sbuf_tensor("eps", [128, 1], F32))
    B["xt"] = [E(nc.sbuf_tensor(f"xt{i}", [128, 1024], F32)) for i in range(2)]
    B["lt"] = E(nc.sbuf_tensor("lt", [128, 1024], F32))
    B["identf"] = E(nc.sbuf_tensor("identf", [128, 128], F32))
    B["identb"] = E(nc.sbuf_tensor("identb", [128, 128], BF16))
    B["ones1"] = E(nc.sbuf_tensor("ones1", [1, 128], BF16))
    if need_attn:
        B["lnf"] = E(nc.sbuf_tensor("lnf", [128, 8 * 2048], BF16))
        B["kf"] = E(nc.sbuf_tensor("kf", [128, 8 * 2048], BF16))
        B["va"] = E(nc.sbuf_tensor("va", [128, 16 * 1040], BF16))
        B["vt"] = E(nc.sbuf_tensor("vt", [128, 1024], BF16))
        B["qf"] = E(nc.sbuf_tensor("qf", [128, 2 * 8 * 256], BF16))
        B["ex"] = E(nc.sbuf_tensor("ex", [128, 2048], BF16))
        B["of"] = E(nc.sbuf_tensor("of", [128, 8 * 256], BF16))
        B["al"] = [E(nc.sbuf_tensor(f"al{i}", [128, 1024], BF16)) for i in range(2)]
        B["wq"] = E(nc.sbuf_tensor("wq_sb", [128, 8192], BF16))
        B["wk"] = E(nc.sbuf_tensor("wk_sb", [128, 8192], BF16))
        B["wv"] = E(nc.sbuf_tensor("wv_sb", [128, 8192], BF16))
        B["wu"] = E(nc.sbuf_tensor("wu_sb", [128, 8192], BF16))
        B["bu_row"] = E(nc.sbuf_tensor("bu_row", [1, 1024], BF16))
        B["rb"] = E(nc.sbuf_tensor("rb", [64, 1024], BF16))
    if need_ffn == "alias":
        lnf, kf, va = B["lnf"].ap(), B["kf"].ap(), B["va"].ap()
        B["wfT"] = lnf[:, 0:8192]
        B["wiT"] = lnf[:, 8192:16384]
        B["wcT"] = kf[:, 0:8192]
        B["ufT"] = kf[:, 8192:16384]
        B["uiT"] = B["wq"].ap()[:, :]
        B["ucT"] = B["wk"].ap()[:, :]
        B["woT"] = B["wv"].ap()[:, :]
        B["uoT"] = B["wu"].ap()[:, :]
        B["cb"] = [va[:, 0:2048], va[:, 2048:4096]]
        B["l2"] = [va[:, 4096:6144], va[:, 6144:8192]]
        B["alt"] = [va[:, 8192:9216], va[:, 9216:10240]]
        B["cuo"] = va[:, 10240:12288]
        B["bo_row"] = va[0:1, 12288:13312]
        cap = B["PA"].ap()[:, 0:2048]
        B["c"] = [cap, cap]
        B["c_in_psum"] = True
        B["hh"] = B["PA"].ap()[:, 0:1024]
        ltap = B["lt"].ap()
        B["oo"] = ltap[:, 0:1024]
        B["gates"] = [ltap[:, 0:256], ltap[:, 256:512], ltap[:, 512:768]]
        B["x1"] = E(nc.sbuf_tensor("x1_", [128, 1024], F32))
        B["bcol"] = E(nc.sbuf_tensor("bcol", [128, 24], F32))
        B["inccol"] = E(nc.sbuf_tensor("inccol", [128, 8], F32))
    elif need_ffn:
        for w in ("wfT", "wiT", "wcT", "ufT", "uiT", "ucT", "woT", "uoT"):
            B[w] = E(nc.sbuf_tensor(w, [128, 8192], BF16))
        B["c"] = [E(nc.sbuf_tensor(f"c{i}", [128, 2048], F32)) for i in range(2)]
        B["cb"] = [E(nc.sbuf_tensor(f"cb{i}", [128, 2048], BF16)) for i in range(2)]
        B["l2"] = [E(nc.sbuf_tensor(f"l2{i}", [128, 2048], BF16)) for i in range(2)]
        B["gates"] = [E(nc.sbuf_tensor(f"gate{i}", [128, 256], F32)) for i in range(3)]
        B["x1"] = E(nc.sbuf_tensor("x1_", [128, 1024], F32))
        B["alt"] = [E(nc.sbuf_tensor(f"alt{i}", [128, 1024], BF16)) for i in range(2)]
        B["cuo"] = E(nc.sbuf_tensor("cuo", [128, 2048], BF16))
        B["hh"] = E(nc.sbuf_tensor("hh", [128, 1024], F32))
        B["oo"] = E(nc.sbuf_tensor("oo", [128, 1024], F32))
        B["bcol"] = E(nc.sbuf_tensor("bcol", [128, 24], F32))
        B["inccol"] = E(nc.sbuf_tensor("inccol", [128, 8], F32))
        B["bo_row"] = E(nc.sbuf_tensor("bo_row", [1, 1024], BF16))
    return B


def emit_setup(P, B):
    P.op("gpsimd", lambda g: g.memset(B["eps"][:], EPS))
    P.op("gpsimd", lambda g: g.memset(B["identf"][:], 0.0), inc="gp_setup")
    P.wait("gpsimd", "gp_setup")
    P.op("gpsimd", lambda g: g.affine_select(
        out=B["identf"][:], in_=B["identf"][:], compare_op=ALU.not_equal,
        fill=1.0, base=0, pattern=[[-1, 128]], channel_multiplier=1))
    P.op("gpsimd", lambda g: g.memset(B["ones1"][:], 1.0), inc="gp_setup")
    P.op("gpsimd", lambda g: g.memset(B["identb"][:], 0.0), inc="gp_setup")
    P.wait("gpsimd", "gp_setup")
    P.op("gpsimd", lambda g: g.affine_select(
        out=B["identb"][:], in_=B["identb"][:], compare_op=ALU.not_equal,
        fill=1.0, base=0, pattern=[[-1, 128]], channel_multiplier=1),
        inc="gp_setup")
    if "va" in B:
        P.op("gpsimd", lambda g: g.memset(B["qf"][:], 0.0), inc="gp_setup")
        P.op("gpsimd", lambda g: g.memset(B["va"][:], 0.0), inc="gp_setup")
        P.wait("gpsimd", "gp_setup")
        P.op("gpsimd", lambda g: g.memset(
            sub3(B["va"], 64, [1040, 16], [65, 16]), 1.0),
            inc="gp_setup")
    P.wait("tensor", "gp_setup")
    P.wait("vector", "gp_setup")
    P.wait("scalar", "gp_setup")


def emit_small_ln(P, B, par, src_fn, width=1024):
    bnb, mv, nm, rs = B["bnb"][par], B["mv"][par], B["nm"][par], B["rs"][par]

    def stats(v):
        src = src_fn()
        nh = width // 512
        for i in range(nh - 1):
            v.bn_stats(bnb[:, 6 * i:6 * (i + 1)], src[:, 512 * i:512 * (i + 1)])
        i = nh - 1
        return v.bn_stats(bnb[:, 6 * i:6 * (i + 1)],
                          src[:, 512 * i:512 * (i + 1)])
    P.op("vector", stats, inc="ve")
    P.wait("vector", "ve")
    P.op("vector", lambda v: v.bn_aggr(mv[:], bnb[:, 0:6 * (width // 512)]),
         inc="ve")
    P.wait("vector", "ve")
    P.op("vector", lambda v: v.tensor_scalar(nm[:], mv[:, 0:1], -1.0, None,
                                             ALU.mult), inc="ve")
    P.wait("scalar", "ve")
    P.op("scalar", lambda s: s.activation(rs[:], mv[:, 1:2], AF.Ln,
                                          bias=B["eps"][:, 0:1]), inc="se")
    P.wait("scalar", "se")
    P.op("scalar", lambda s: s.activation(rs[:], rs[:], AF.Exp, scale=-0.5),
         inc="ve_rs")
    return nm, rs


I32 = mybir.dt.int32
RSQRT_K = 0x5f3759df


def emit_rsqrt_dve(P, B, par):
    mv, nm, rs = B["mv"][par], B["nm"][par], B["rs"][par]
    q1, q2 = B["sq1"][par], B["sq2"][par]
    q1i = q1.ap().bitcast(I32)
    q2i = q2.ap().bitcast(I32)
    P.op("vector", lambda v: v.tensor_scalar(nm[:], mv[:, 0:1], -1.0, None,
                                             ALU.mult), inc="ve")
    P.op("vector", lambda v: v.tensor_single_scalar(
        q1[:], mv[:, 1:2], EPS, ALU.add), inc="ve")
    P.wait("vector", "ve")
    P.op("vector", lambda v: v.tensor_single_scalar(
        q2i, q1i, 1, ALU.arith_shift_right), inc="ve")
    P.wait("vector", "ve")
    P.op("vector", lambda v: v.tensor_scalar(
        q2i, q2i, -1, RSQRT_K, ALU.mult, ALU.add), inc="ve")
    for it in range(2):
        dst = q2 if it == 0 else rs
        P.wait("vector", "ve")
        P.op("vector", lambda v: v.tensor_mul(rs[:], q1[:], q2[:]), inc="ve")
        P.wait("vector", "ve")
        P.op("vector", lambda v: v.tensor_mul(rs[:], rs[:], q2[:]), inc="ve")
        P.wait("vector", "ve")
        P.op("vector", lambda v: v.tensor_scalar(
            rs[:], rs[:], -0.5, 1.5, ALU.mult, ALU.add), inc="ve")
        P.wait("vector", "ve")
        P.op("vector", lambda v, dst=dst: v.tensor_mul(dst[:], q2[:], rs[:]),
             inc="ve" if it == 0 else "ve_rs")
    return nm, rs


def emit_small_ln_dve(P, B, par, src_fn, width=1024):
    bnb, mv = B["bnb"][par], B["mv"][par]

    def stats(v):
        src = src_fn()
        nh = width // 512
        for i in range(nh - 1):
            v.bn_stats(bnb[:, 6 * i:6 * (i + 1)], src[:, 512 * i:512 * (i + 1)])
        i = nh - 1
        return v.bn_stats(bnb[:, 6 * i:6 * (i + 1)],
                          src[:, 512 * i:512 * (i + 1)])
    P.op("vector", stats, inc="ve")
    P.wait("vector", "ve")
    P.op("vector", lambda v: v.bn_aggr(mv[:], bnb[:, 0:6 * (width // 512)]),
         inc="ve")
    P.wait("vector", "ve")
    return emit_rsqrt_dve(P, B, par)


def emit_ln_tile_bf(P, B, src_dram_ap, k, dst_fm, dst_col, slots, pend=None):
    nslot = len(slots)
    par4 = k % nslot
    par2 = k % 2
    xt = slots[par4]
    lt = (B["lt"] if par2 == 0 else B["x1"]).ap().bitcast(BF16)[:, 0:1024]
    PA = B["PA"]
    P.wait("sync", "gp_lnapply", max(P.v("gp_lnapply") - (nslot - 1), 0))
    P.dma("sync", lambda sy: sy.dma_start(out=xt, in_=src_dram_ap),
          inc=f"ldx{par4}")
    P.wait("vector", "gp_lnapply", max(P.v("gp_lnapply") - (nslot - 1), 0))
    P.wait("vector", f"ldx{par4}")
    bnb, mv, nm, rs = (B["bnb"][par4], B["mv"][par4], B["nm"][par4],
                       B["rs"][par4])

    def stats(v):
        v.bn_stats(bnb[:, 0:6], xt[:, 0:512])
        return v.bn_stats(bnb[:, 6:12], xt[:, 512:1024])
    P.op("vector", stats, inc="ve")
    P.wait("vector", "ve")
    P.op("vector", lambda v: v.bn_aggr(mv[:], bnb[:]), inc="ve")
    P.wait("vector", "ve")
    P.op("vector", lambda v: v.tensor_scalar(nm[:], mv[:, 0:1], -1.0, None,
                                             ALU.mult), inc="ve")
    P.wait("scalar", "ve")
    P.op("scalar", lambda sc: sc.activation(rs[:], mv[:, 1:2], AF.Ln,
                                            bias=B["eps"][:, 0:1]), inc="se")
    P.wait("scalar", "se")
    P.op("scalar", lambda sc: sc.activation(rs[:], rs[:], AF.Exp, scale=-0.5),
         inc="ve_rs")
    if pend is not None:
        pend()
    b2 = B["sq1"][par4]
    P.wait("scalar", "ve_rs")
    P.op("scalar", lambda sc, b2=b2: sc.activation(
        b2[:], nm[:], AF.Identity, scale=rs[:, 0:1]), inc="se")
    P.wait("scalar", "se")
    P.wait("scalar", "te_T", max(P.v("te_T") - 1, 0))
    P.op("scalar", lambda sc, lt=lt, b2=b2: sc.activation(
        lt, xt, AF.Identity, bias=b2[:, 0:1], scale=rs[:, 0:1]),
        inc="gp_lnapply")
    P.wait("tensor", "gp_lnapply")
    P.wait("tensor", "ve_Tevac", max(P.v("ve_Tevac") - 2, 0))
    pslot = k % 4
    pbuf = (PA if pslot < 2 else B["PB"]).ap().bitcast(BF16)
    po = 1024 * (pslot % 2)
    for ch in range(8):
        P.op("tensor", lambda t, ch=ch, po=po, lt=lt, pbuf=pbuf: t.transpose(
            pbuf[:, po + 128 * ch: po + 128 * (ch + 1)],
            lt[:, 128 * ch:128 * (ch + 1)], B["identb"][:]),
            inc="te_T" if ch == 7 else None)

    def evac():
        P.wait("scalar", "te_T")
        stride = dst_fm.shape[1] // 8
        P.op("scalar",
             lambda sc, po=po, dst_col=dst_col, dst_fm=dst_fm, stride=stride,
             pbuf=pbuf:
             sc.copy(
                 sub3(dst_fm, dst_col, [stride, 8], [1, 128]),
                 sub3(pbuf, po, [128, 8], [1, 128])),
             inc="ve_Tevac")
    return evac


def emit_ln_tile(P, B, src_dram_ap, par, dst_fm, dst_col, lt2=False):
    xt = B["xt"][par]
    lt = (B["lt"] if (par == 0 or not lt2) else B["x1"])
    PA = B["PA"]
    P.wait("sync", "gp_lnapply", max(P.v("gp_lnapply") - 1, 0))
    P.dma("sync", lambda sy: sy.dma_start(out=xt[:], in_=src_dram_ap),
          inc=f"ldx{par}")
    P.wait("vector", "gp_lnapply", max(P.v("gp_lnapply") - 1, 0))
    P.wait("vector", f"ldx{par}")
    nm, rs = emit_small_ln(P, B, par, lambda: xt[:])
    P.wait("gpsimd", "ve_rs")
    if lt2:
        P.wait("gpsimd", "te_T", max(P.v("te_T") - 1, 0))
    else:
        P.wait("gpsimd", "te_T")
    P.op("gpsimd", lambda g, lt=lt: g.tensor_scalar(
        lt[:], xt[:], nm[:, 0:1], rs[:, 0:1], ALU.add, ALU.mult),
        inc="gp_lnapply")
    P.wait("tensor", "gp_lnapply")
    P.wait("tensor", "ve_Tevac", max(P.v("ve_Tevac") - 1, 0))
    po = 1024 * par
    for ch in range(8):
        P.op("tensor", lambda t, ch=ch, po=po, lt=lt: t.transpose(
            PA[:, po + 128 * ch: po + 128 * (ch + 1)],
            lt[:, 128 * ch:128 * (ch + 1)], B["identf"][:]),
            inc="te_T" if ch == 7 else None)
    P.wait("scalar", "te_T")
    stride = dst_fm.shape[1] // 8
    P.op("scalar", lambda sc, po=po, dst_col=dst_col, dst_fm=dst_fm, stride=stride:
         sc.copy(
             sub3(dst_fm, dst_col, [stride, 8], [1, 128]),
             sub3(PA, po, [128, 8], [1, 128])),
         inc="ve_Tevac")



def emit_attention(P, B, cfg):
    NB = DEBUG_NB or cfg["NB"]
    units = cfg["units"]
    xg, xm = cfg["xg"], cfg["xm"]
    lnf, kf, va, qf, ex, of = (B["lnf"], B["kf"], B["va"], B["qf"], B["ex"],
                               B["of"])
    wq, wk, wv, wu = B["wq"], B["wk"], B["wv"], B["wu"]
    PA, PB = B["PA"], B["PB"]
    lnm = lnf if cfg["lnm_is_lnf"] else B["lnm"]
    masks = B["masks"]

    for name, arena in (("wq", wq), ("wk", wk), ("wv", wv), ("wu", wu)):
        dram = cfg[name]
        for dmc in range(8):
            P.dma("gpsimd", lambda g, dram=dram, arena=arena, dmc=dmc: g.dma_start(
                out=arena[:, 1024 * dmc:1024 * (dmc + 1)],
                in_=dram[128 * dmc:128 * (dmc + 1), :]), inc="ldw")
    P.dma("gpsimd", lambda g: g.dma_start(out=B["bu_row"][:],
                                          in_=cfg["bu"][None, :]), inc="ldw")
    P.dma("sync", lambda sy: sy.dma_start(out=masks[:], in_=cfg["masks_dram"]),
          inc="ld_mask")
    n_ldw = P.v("ldw")
    P.wait("tensor", "ldw", n_ldw)

    for b in range(NB):
        P.wait("vector", "te_v")
        P.wait("vector", "te_s")
        P.wait("vector", "te_q")
        P.wait("vector", "te_up")
        P.wait("gpsimd", "te_av")
        P.wait("tensor", "ve_al")
        P.wait("tensor", "ve_norm")
        if cfg.get("al_on_ex"):
            P.wait("scalar", "st_al0")
            P.wait("scalar", "st_al1")
            P.wait("vector", "st_al0")
            P.wait("vector", "st_al1")
            P.wait("gpsimd", "st_al0")
            P.wait("gpsimd", "st_al1")
            P.wait("sync", "st_al0")
            P.wait("sync", "st_al1")
        slots = cfg["ln_slots"]
        pends = []
        for tt in range(NKT):
            pd = pends.pop(0) if len(pends) >= 2 else None
            pends.append(emit_ln_tile_bf(
                P, B, xg[b, 128 * tt:128 * (tt + 1), :],
                tt, lnf, 128 * tt, slots, pd))
        if not cfg["lnm_is_lnf"]:
            for tt in range(cfg["NQ"] // 128):
                pd = pends.pop(0) if len(pends) >= 2 else None
                pends.append(emit_ln_tile_bf(
                    P, B, xm[b, 128 * tt:128 * (tt + 1), :],
                    tt, lnm, 128 * tt, slots[0:2], pd))
        for pd in pends:
            pd()
        P.wait("tensor", "ve_Tevac")
        P.wait("vector", "gp_lnapply")
        if DEBUG_STAGE == "ln1":
            continue

        for g4 in range(4):
            for hp in range(8):
                par = (g4 * 8 + hp) % 2
                po = 512 * par
                P.wait("tensor", "ve_kevac", max(P.v("ve_kevac") - 1, 0))
                for dmc in range(8):
                    P.op("tensor", lambda t, dmc=dmc, hp=hp, g4=g4, po=po: t.matmul(
                        PA[:, po:po + 512],
                        wk[:, 1024 * dmc + 128 * hp: 1024 * dmc + 128 * (hp + 1)],
                        lnf[:, 2048 * dmc + 512 * g4: 2048 * dmc + 512 * (g4 + 1)],
                        start=(dmc == 0), stop=(dmc == 7), skip_group_check=True),
                        inc="te_k" if dmc == 7 else None)
                P.wait("vector", "te_k")
                P.op("vector", lambda v, hp=hp, g4=g4, po=po: v.tensor_copy(
                    kf[:, 2048 * hp + 512 * g4: 2048 * hp + 512 * (g4 + 1)],
                    PA[:, po:po + 512]), inc="ve_kevac")

        for kt in range(NKT):
            par = kt % 2
            po = 1024 * par
            P.wait("tensor", "gp_vrepack", max(P.v("gp_vrepack") - 1, 0))
            for dmc in range(8):
                for hf in range(2):
                    P.op("tensor", lambda t, dmc=dmc, kt=kt, hf=hf, po=po: t.matmul(
                        PB[:, po + 512 * hf: po + 512 * (hf + 1)],
                        lnf[:, 2048 * dmc + 128 * kt: 2048 * dmc + 128 * (kt + 1)],
                        wv[:, 1024 * dmc + 512 * hf: 1024 * dmc + 512 * (hf + 1)],
                        start=(dmc == 0), stop=(dmc == 7),
                        skip_group_check=True),
                        inc="te_v" if (dmc == 7 and hf == 1) else None)
            P.wait("vector", "te_v")
            P.op("vector", lambda v, po=po, kt=kt: v.tensor_copy(
                sub3(va, 1040 * kt, [65, 16], [1, 64]),
                sub3(B["PB"], po, [64, 16], [1, 64])),
                inc="gp_vrepack")

        if DEBUG_STAGE == "kv":
            continue
        P.wait("tensor", "ve_kevac")
        P.wait("tensor", "gp_vrepack")
        P.wait("vector", "ld_mask", 16)
        for unit in units:
            q_off = unit["q_off"]
            src_stride = lnm.shape[1] // 8
            P.wait("tensor", "ve_norm")
            for hp in range(8):
                P.wait("tensor", "ve_qevac")
                for dmc in range(8):
                    P.op("tensor", lambda t, hp=hp, dmc=dmc, q_off=q_off,
                         src_stride=src_stride: t.matmul(
                             PB[:, 1792:2048],
                             wq[:, 1024 * dmc + 128 * hp: 1024 * dmc + 128 * (hp + 1)],
                             lnm[:, src_stride * dmc + q_off:
                                 src_stride * dmc + q_off + 256],
                             start=(dmc == 0), stop=(dmc == 7),
                             skip_group_check=True),
                         inc="te_q" if dmc == 7 else None)
                P.wait("vector", "te_q")
                P.op("vector", lambda v, hp=hp: v.tensor_copy(
                    qf[0:64, 256 * hp:256 * (hp + 1)], PB[0:64, 1792:2048]))
                P.op("vector", lambda v, hp=hp: v.tensor_copy(
                    qf[64:128, 2048 + 256 * hp:2048 + 256 * (hp + 1)],
                    PB[64:128, 1792:2048]), inc="ve_qevac")
            P.wait("tensor", "ve_qevac")
            if DEBUG_STAGE == "q":
                continue

            for quad in range(4):
                ktl = unit["ktiles"]
                nkt_u = len(ktl)
                P.wait("tensor", "ve_al")

                exp_base = P.v("se_exp")

                def emit_av(idx, quad=quad, ktl=ktl, nkt_u=nkt_u,
                            exp_base=exp_base):
                    i, vs, d_lo, d_w = ktl[idx]
                    P.wait("tensor", "se_exp", exp_base + idx + 1)
                    if idx == 0:
                        P.wait("tensor", "ve_norm")
                    for hq in range(4):
                        h = 4 * quad + hq
                        P.op("tensor", lambda t, i=i, vs=vs, hq=hq, h=h, idx=idx,
                             fb=(idx == 0 and hq % 2 == 0),
                             last=(idx == nkt_u - 1): t.matmul(
                                 PB[0:65, 256 * hq + vs: 256 * (hq + 1)],
                                 va[:, 1040 * i + 65 * h: 1040 * i + 65 * (h + 1)],
                                 sub2(ex, 1024 * (idx % 2) + 256 * hq + vs,
                                      [1, 256 - vs]),
                                 start=fb, stop=last, skip_group_check=True),
                             inc="te_av" if hq == 3 else None)

                for idx, (i, vs, d_lo, d_w) in enumerate(ktl):
                    slot = 1024 * (idx % 2)
                    P.wait("tensor", "se_exp", max(P.v("se_exp") - 1, 0))
                    for hq in range(4):
                        h = 4 * quad + hq
                        hp = h // 2
                        qo = 2048 * (h % 2) + 256 * hp
                        P.op("tensor",
                             lambda t, i=i, vs=vs, hq=hq, hp=hp, qo=qo, slot=slot: t.matmul(
                                 PA[:, slot + 256 * hq + vs: slot + 256 * (hq + 1)],
                                 kf[:, 2048 * hp + 128 * i: 2048 * hp + 128 * (i + 1)],
                                 qf[:, qo + vs: qo + 256],
                                 start=(hq % 2 == 0), stop=False,
                                 skip_group_check=True),
                             inc="te_s" if hq == 3 else None)
                    if d_w:
                        for hq in range(4):
                            P.op("tensor", lambda t, hq=hq, d_lo=d_lo, d_w=d_w,
                                 slot=slot: t.matmul(
                                     PA[:, slot + 256 * hq + d_lo:
                                        slot + 256 * hq + d_lo + d_w],
                                     B["identb"][:],
                                     masks[:, 0:d_w],
                                     start=False, stop=True,
                                     skip_group_check=True),
                                 inc="te_s" if hq == 3 else None)
                    P.wait("scalar", "te_s")
                    P.wait("scalar", "te_av")
                    P.op("scalar", lambda s, vs=vs, slot=slot, idx=idx: s.activation(
                        sub3(ex, 1024 * (idx % 2) + vs, [256, 4], [1, 256 - vs]),
                        sub3(PA, slot + vs, [256, 4], [1, 256 - vs]),
                        AF.Exp, scale=1.0 / 64.0), inc="se_exp")
                    if idx > 0:
                        emit_av(idx - 1)
                emit_av(nkt_u - 1)
                P.wait("scalar", "te_av")
                P.op("scalar", lambda s: s.activation(
                    PB[64:65, 1024:2048], PB[64:65, 0:1024], AF.Ln), inc="se_ln")
                P.wait("scalar", "se_ln")
                P.wait("scalar", "ve_norm")
                P.op("scalar", lambda s: s.activation(
                    B["rb"][0:1, 0:1024], PB[64:65, 1024:2048], AF.Exp,
                    scale=-1.0), inc="se_r2")
                P.wait("tensor", "se_r2")
                P.wait("tensor", "ve_bc")
                for j in range(2):
                    P.op("tensor", lambda t, j=j: t.matmul(
                        PB[0:64, 1024 + 512 * j:1024 + 512 * (j + 1)],
                        B["ones1"][0:1, 0:64],
                        B["rb"][0:1, 512 * j:512 * (j + 1)],
                        start=True, stop=True, skip_group_check=True),
                        inc="te_bc" if j == 1 else None)
                P.wait("vector", "te_bc")
                P.op("vector", lambda v: v.tensor_copy(
                    B["rb"][0:64, 0:1024], PB[0:64, 1024:2048]), inc="ve_bc")
                P.wait("vector", "ve_bc")
                P.wait("vector", "te_up")
                for hq in range(4):
                    h = 4 * quad + hq
                    hp, hlo = h // 2, 64 * (h % 2)
                    P.op("vector", lambda v, hq=hq, hp=hp, hlo=hlo: v.tensor_mul(
                        of[hlo:hlo + 64, 256 * hp:256 * (hp + 1)],
                        PB[0:64, 256 * hq:256 * (hq + 1)],
                        B["rb"][0:64, 256 * hq:256 * (hq + 1)]),
                        inc="ve_norm")
            if DEBUG_STAGE == "norm":
                continue
            P.wait("tensor", "ve_norm")
            for st in range(2):
                po = 1024 * st
                for dc in range(8):
                    for j in range(2):
                        P.op("tensor", lambda t, dc=dc, j=j, st=st, po=po: t.matmul(
                            PA[:, po + 512 * j: po + 512 * (j + 1)],
                            of[:, 256 * dc + 128 * st: 256 * dc + 128 * (st + 1)],
                            wu[:, 1024 * dc + 512 * j: 1024 * dc + 512 * (j + 1)],
                            start=(dc == 0), stop=False,
                            skip_group_check=True))
                for j in range(2):
                    P.op("tensor", lambda t, j=j, po=po: t.matmul(
                        PA[:, po + 512 * j: po + 512 * (j + 1)],
                        B["ones1"][0:1, :],
                        B["bu_row"][0:1, 512 * j:512 * (j + 1)],
                        start=False, stop=True, skip_group_check=True),
                        inc="te_up" if j == 1 else None)
                P.wait("vector", "te_up")
                P.wait("vector", f"st_al{st}")
                nm, rs = emit_small_ln(P, B, st,
                                       lambda po=po: PA[:, po:po + 1024])
                P.wait("vector", "ve_rs")
                P.op("vector", lambda v, st=st, po=po, nm=nm, rs=rs: v.tensor_scalar(
                    B["al"][st][:], PA[:, po:po + 1024], nm[:, 0:1], rs[:, 0:1],
                    ALU.add, ALU.mult), inc="ve_al")
                P.wait("sync", "ve_al")
                dst = cfg["aln_dst"](b, unit["row0"] + 128 * st)
                P.dma("sync", lambda sy, dst=dst, st=st: sy.dma_start(
                    out=dst, in_=B["al"][st][:]), inc=f"st_al{st}")
    P.wait("sync", "st_al0")
    P.wait("sync", "st_al1")



def emit_ffn(P, B, cfg):
    NT = 16
    PA, PB = B["PA"], B["PB"]
    xm = cfg["xm_ffn"]
    identf = B["identf"]

    wlist = [("wf", "wfT"), ("wi", "wiT"), ("wc", "wcT"), ("uf", "ufT"),
             ("ui", "uiT"), ("uc", "ucT"), ("wo", "woT"), ("uo", "uoT")]
    for src_name, dst_name in wlist:
        dram = cfg[src_name]
        dst = B[dst_name]
        for rnd in range(4):
            P.wait("sync", "te_wT")
            for oc in range(2):
                P.dma("sync", lambda sy, dram=dram, oc=oc, rnd=rnd:
                      sy.dma_start(
                          out=B["xt"][oc][:],
                          in_=dram[256 * rnd + 128 * oc:256 * rnd + 128 * (oc + 1), :]),
                      inc="ld_ws")
            P.wait("tensor", "ld_ws")
            for ic in range(8):
                P.wait("tensor", "ve_wevac")
                for oc in range(2):
                    P.op("tensor", lambda t, ic=ic, oc=oc: t.transpose(
                        PA[:, 128 * oc:128 * (oc + 1)],
                        B["xt"][oc][:, 128 * ic: 128 * (ic + 1)],
                        identf[:]),
                        inc="te_wT" if oc == 1 else None)
                P.wait("vector", "te_wT")
                P.op("vector", lambda v, ic=ic, rnd=rnd, dst=dst: v.tensor_copy(
                    dst[:, 1024 * ic + 256 * rnd: 1024 * ic + 256 * rnd + 256],
                    PA[:, 0:256]), inc="ve_wevac")
    P.wait("tensor", "ve_wevac")

    for gi, bname in enumerate(("bf", "bi", "bc")):
        for mc in range(8):
            P.dma("sync", lambda sy, bname=bname, gi=gi, mc=mc: sy.dma_start(
                out=B["bcol"][:, 8 * gi + mc: 8 * gi + mc + 1],
                in_=cfg[bname][128 * mc:128 * (mc + 1), None]), inc="ldb")
    for mc in range(8):
        P.dma("sync", lambda sy, mc=mc: sy.dma_start(
            out=B["inccol"][:, mc:mc + 1],
            in_=cfg["in_c"][128 * mc:128 * (mc + 1), None]), inc="ldb")
    P.dma("gpsimd", lambda g: g.dma_start(out=B["bo_row"][:],
                                          in_=cfg["bo"][None, :]), inc="ldw2")
    P.wait("scalar", "ldb")
    P.wait("scalar", "ldw2")

    P.wait("sync", "te_wT")

    def b1_spill(t):
        P.wait("sync", "ve_l2")
        P.dma("sync", lambda sy, t=t: sy.dma_start(
            out=cfg["ln2_dram"][t], in_=B["l2"][0][:]), inc="st_l2")

    pend = None
    spill_after = None
    for t in range(NT):
        for st in range(2):
            par = st
            P.wait("sync", "ve_x1", max(P.v("ve_x1") - 1, 0))
            P.dma("sync", lambda sy, t=t, st=st, par=par: sy.dma_start(
                out=B["xt"][par][:], in_=xm[t, 128 * st:128 * (st + 1), :]),
                inc=f"ldx{par}")
            P.dma("sync", lambda sy, t=t, st=st, par=par: sy.dma_start(
                out=B["alt"][par][:], in_=cfg["aln_src"](t, 128 * st)),
                inc=f"ldx{par}")
            P.wait("vector", f"ldx{par}")
            P.wait("vector", "gp_lnapply")
            P.op("vector", lambda v, par=par: v.tensor_add(
                B["x1"][:], B["xt"][par][:], B["alt"][par][:]), inc="ve_x1")
            P.wait("vector", "ve_x1")
            nm, rs = emit_small_ln(P, B, par, lambda: B["x1"][:])
            if pend is not None:
                pend()
            P.wait("gpsimd", "ve_rs")
            P.wait("gpsimd", "te_T")
            P.op("gpsimd", lambda g, nm=nm, rs=rs: g.tensor_scalar(
                B["lt"][:], B["x1"][:], nm[:, 0:1], rs[:, 0:1],
                ALU.add, ALU.mult), inc="gp_lnapply")
            P.wait("tensor", "gp_lnapply")
            P.wait("tensor", "ve_l2", max(P.v("ve_l2") - 1, 0))
            po = 1024 * par
            for ch in range(8):
                P.op("tensor", lambda t_, ch=ch, po=po: t_.transpose(
                    PA[:, po + 128 * ch:po + 128 * (ch + 1)],
                    B["lt"][:, 128 * ch:128 * (ch + 1)], identf[:]),
                    inc="te_T" if ch == 7 else None)

            def evac(po=po, st=st, t=t):
                P.wait("scalar", "te_T")
                P.wait("scalar", "st_l2", 16 * t)
                P.op("scalar", lambda sc, po=po, st=st: sc.copy(
                    sub3(B["l2"][0], 128 * st, [256, 8], [1, 128]),
                    sub3(PA, po, [128, 8], [1, 128])),
                    inc="ve_l2")
            pend = evac
            if st == 0 and spill_after is not None:
                b1_spill(spill_after)
                spill_after = None
        spill_after = t
    if pend is not None:
        pend()
    b1_spill(spill_after)

    P.wait("scalar", "ve_x1")
    P.wait("scalar", "ve_l2")
    for mc in range(8):
        P.op("scalar", lambda s, mc=mc: s.activation(
            B["c"][0][:, 256 * mc:256 * (mc + 1)], B["x1"][:, 0:256],
            AF.Identity, bias=B["inccol"][:, mc:mc + 1], scale=0.0),
            inc="se_c0")
    ceng0 = "vector" if B.get("c_in_psum") else "gpsimd"
    P.wait(ceng0, "se_c0")
    P.op(ceng0, lambda g: g.tensor_copy(B["cb"][0][:], B["c"][0][:]),
         inc="gp_cb")
    P.wait("vector", "se_c0")

    P.wait("sync", "st_l2")
    for t in range(NT):
        par = t % 2
        P.wait("sync", "te_g", max(P.v("te_g") - 24, 0))
        P.dma("sync", lambda sy, t=t, par=par: sy.dma_start(
            out=B["l2"][par][:], in_=cfg["ln2_dram"][t]), inc=f"ldl2_{par}")
        P.wait("tensor", f"ldl2_{par}")
        for mc in range(8):
            pgo = 1024 * (mc % 2)
            P.wait("tensor", "se_g", max(P.v("se_g") - 3, 0))
            for gi, wT, uT in ((0, "wfT", "ufT"), (1, "wiT", "uiT"),
                               (2, "wcT", "ucT")):
                doff = pgo + 256 * gi if gi < 2 else pgo + 512
                for ic in range(8):
                    P.op("tensor", lambda t_, wT=wT, ic=ic, mc=mc, doff=doff,
                         par=par, first=(ic == 0 and gi in (0, 2)): t_.matmul(
                             PB[:, doff:doff + 256],
                             B[wT][:, 1024 * ic + 128 * mc: 1024 * ic + 128 * (mc + 1)],
                             B["l2"][par][:, 256 * ic:256 * (ic + 1)],
                             start=first, stop=False, skip_group_check=True))
            P.wait("tensor", "gp_cb", 8 * t + 1)
            for gi, wT, uT in ((0, "wfT", "ufT"), (1, "wiT", "uiT"),
                               (2, "wcT", "ucT")):
                doff = pgo + 256 * gi if gi < 2 else pgo + 512
                for jc in range(8):
                    P.op("tensor", lambda t_, uT=uT, jc=jc, mc=mc, doff=doff,
                         par=par: t_.matmul(
                             PB[:, doff:doff + 256],
                             B[uT][:, 1024 * jc + 128 * mc: 1024 * jc + 128 * (mc + 1)],
                             B["cb"][par][:, 256 * jc:256 * (jc + 1)],
                             start=False, stop=(jc == 7), skip_group_check=True),
                         inc="te_g" if jc == 7 else None)
            P.wait("scalar", "te_g")
            P.wait("scalar", "ve_c")
            for gi, func in ((0, AF.Sigmoid), (1, AF.Sigmoid), (2, AF.Tanh)):
                doff = pgo + 256 * gi if gi < 2 else pgo + 512
                P.op("scalar", lambda s, gi=gi, mc=mc, doff=doff, func=func:
                     s.activation(
                         B["gates"][gi][:], PB[:, doff:doff + 256], func,
                         bias=B["bcol"][:, 8 * gi + mc:8 * gi + mc + 1]),
                     inc="se_g")
            P.wait("vector", "se_g")
            P.wait("vector", "ve_c", max(P.v("ve_c") - 7, 0))
            cs = slice(256 * mc, 256 * (mc + 1))
            P.op("vector", lambda v, cs=cs, par=par: v.tensor_mul(
                B["c"][1 - par][:, cs], B["gates"][0][:], B["c"][par][:, cs]),
                inc="ve_cu")
            P.wait("vector", "ve_cu")
            P.op("vector", lambda v: v.tensor_mul(
                B["gates"][0][:], B["gates"][1][:], B["gates"][2][:]),
                inc="ve_cu")
            P.wait("vector", "ve_cu")
            P.op("vector", lambda v, cs=cs, par=par: v.tensor_add(
                B["c"][1 - par][:, cs], B["c"][1 - par][:, cs],
                B["gates"][0][:]), inc="ve_c")
            ceng = "vector" if B.get("c_in_psum") else "gpsimd"
            P.wait(ceng, "ve_c")
            P.op(ceng, lambda g, cs=cs, par=par: g.tensor_copy(
                B["cb"][1 - par][:, cs], B["c"][1 - par][:, cs]), inc="gp_cb")

    cpar = NT % 2
    P.wait("tensor", "gp_cb", 8 * NT + 1)

    for st in range(2):
        po = 1024 * st
        for mc in range(8):
            for j in range(2):
                P.op("tensor", lambda t, mc=mc, j=j, st=st, po=po: t.matmul(
                    PA[:, po + 512 * j:po + 512 * (j + 1)],
                    B["cb"][cpar][:, 256 * mc + 128 * st: 256 * mc + 128 * (st + 1)],
                    B["uoT"][:, 1024 * mc + 512 * j: 1024 * mc + 512 * (j + 1)],
                    start=(mc == 0), stop=False, skip_group_check=True))
        for j in range(2):
            P.op("tensor", lambda t, j=j, po=po: t.matmul(
                PA[:, po + 512 * j:po + 512 * (j + 1)],
                B["ones1"][0:1, :], B["bo_row"][0:1, 512 * j:512 * (j + 1)],
                start=False, stop=True, skip_group_check=True),
                inc="te_cuo" if j == 1 else None)
    P.wait("vector", "te_cuo")
    P.op("vector", lambda v: v.tensor_copy(B["cuo"][:], PA[:]), inc="ve_cuo")

    P.wait("tensor", "ve_cuo")
    P.wait("vector", "ve_cuo")
    P.wait("scalar", "ve_cuo")
    P.wait("sync", "te_g")
    PAap = B["PA"].ap()
    hhs = [PAap[:, 0:1024], PAap[:, 1024:2048]]
    oos = [B["oo"], B["x1"].ap()[:, 0:1024]]
    l2base = [P.v("ldl2_0"), P.v("ldl2_1")]
    P.dma("sync", lambda sy: sy.dma_start(
        out=B["l2"][0][:], in_=cfg["ln2_dram"][0]), inc="ldl2_0")
    for t in range(NT):
        par = t % 2
        if t + 1 < NT:
            P.wait("sync", "te_xwo", max(P.v("te_xwo") - 2, 0))
            P.dma("sync", lambda sy, t=t: sy.dma_start(
                out=B["l2"][(t + 1) % 2][:], in_=cfg["ln2_dram"][t + 1]),
                inc=f"ldl2_{(t + 1) % 2}")
        P.wait("tensor", f"ldl2_{par}", l2base[par] + 16 * (t // 2 + 1))
        for st in range(2):
            po = 1024 * st
            P.wait("tensor", "se_h", max(P.v("se_h") - (1 - st), 0))
            for ic in range(8):
                for j in range(2):
                    P.op("tensor", lambda t_, ic=ic, j=j, st=st, po=po, par=par:
                         t_.matmul(
                             PB[:, po + 512 * j:po + 512 * (j + 1)],
                             B["l2"][par][:, 256 * ic + 128 * st:
                                           256 * ic + 128 * (st + 1)],
                             B["woT"][:, 1024 * ic + 512 * j:
                                      1024 * ic + 512 * (j + 1)],
                             start=(ic == 0), stop=(ic == 7),
                             skip_group_check=True),
                         inc="te_xwo" if (ic == 7 and j == 1) else None)
        for st in range(2):
            po = 1024 * st
            P.wait("vector", "te_xwo", P.v("te_xwo") - (1 - st))
            P.op("vector", lambda v, st=st, po=po: v.tensor_add(
                PB[:, po:po + 1024], PB[:, po:po + 1024],
                B["cuo"][:, 1024 * st:1024 * (st + 1)]), inc="ve_ph")
        for st in range(2):
            po = 1024 * st
            P.wait("scalar", "ve_ph", P.v("ve_ph") - (1 - st))
            P.op("scalar", lambda s, po=po, st=st: s.activation(
                hhs[st], PB[:, po:po + 1024], AF.Tanh), inc="se_h")
        for st in range(2):
            P.wait("sync", "ve_o", max(P.v("ve_o") - (1 - st), 0))
            P.dma("sync", lambda sy, t=t, st=st: sy.dma_start(
                out=B["alt"][st][:], in_=cfg["aln_src"](t, 128 * st)),
                inc=f"ldo{st}")
            P.dma("sync", lambda sy, t=t, st=st: sy.dma_start(
                out=B["xt"][st][:], in_=xm[t, 128 * st:128 * (st + 1), :]),
                inc=f"ldo{st}")
        nmrs = []
        for st in range(2):
            P.wait("vector", "se_h", P.v("se_h") - (1 - st))
            nmrs.append(emit_small_ln_dve(P, B, st,
                                          lambda st=st: hhs[st]))
        for st in range(2):
            P.wait("gpsimd", f"ldo{st}")
            P.wait("gpsimd", "ve_o", max(P.v("ve_o") - (1 - st), 0))
            P.op("gpsimd", lambda g, st=st: g.tensor_add(
                B["xt"][st][:], B["xt"][st][:], B["alt"][st][:]),
                inc="gp_ax")
        for st in range(2):
            nm, rs = nmrs[st]
            P.wait("vector", f"st_y{st}")
            P.op("vector", lambda v, nm=nm, rs=rs, st=st: v.tensor_scalar(
                oos[st][:], hhs[st], nm[:, 0:1], rs[:, 0:1],
                ALU.add, ALU.mult), inc="ve_oc")
            P.wait("vector", "ve_oc")
            P.wait("vector", "gp_ax", P.v("gp_ax") - (1 - st))
            P.op("vector", lambda v, st=st: v.tensor_add(
                oos[st][:], oos[st][:], B["xt"][st][:]), inc="ve_o")
            P.wait("sync", "ve_o")
            P.dma("sync", lambda sy, t=t, st=st: sy.dma_start(
                out=cfg["y"][t, 128 * st:128 * (st + 1), :], in_=oos[st][:]),
                inc=f"st_y{st}")
    P.wait("sync", "st_y0")
    P.wait("sync", "st_y1")



def make_units(version):
    if version == "fused":
        ktiles = [(i, 16 * i, 16 * i, 16) for i in range(16)]
        return [{"q_off": 0, "row0": 0, "ktiles": ktiles}]
    units = []
    for u in range(8):
        kt = [(i, 0, 0, 0) for i in range(2 * u)]
        kt.append((2 * u, 0, 0, 128))
        kt.append((2 * u + 1, 128, 128, 128))
        units.append({"q_off": 256 * u, "row0": 256 * u, "ktiles": kt})
    return units


def build_attn_program(version):
    from contextlib import ExitStack
    nc = bass.Bass("TRN2", target_bir_lowering=False)
    ctx = ExitStack()
    NB = 2 if version == "unfused" else 16
    NQ = 2048 if version == "unfused" else 256
    mask_w = 128 if version == "unfused" else 16

    xg = nc.dram_tensor("xg", [NB, SEQ, D], F32, kind="ExternalInput").ap()
    wq = nc.dram_tensor("wq", [D, D], F32, kind="ExternalInput").ap()
    wk = nc.dram_tensor("wk", [D, D], F32, kind="ExternalInput").ap()
    wv = nc.dram_tensor("wv", [D, D], F32, kind="ExternalInput").ap()
    wu = nc.dram_tensor("wu", [D, D], F32, kind="ExternalInput").ap()
    bu = nc.dram_tensor("bu", [D], F32, kind="ExternalInput").ap()
    masks_dram = nc.dram_tensor("masks", [128, 8 * mask_w], BF16,
                                kind="ExternalInput").ap()
    aln = nc.dram_tensor("aln", [NB, NQ, D], BF16, kind="ExternalOutput").ap()

    with ctx:
        B = alloc_common(nc, ctx, need_attn=True, need_ffn=False)
        B["masks"] = ctx.enter_context(
            nc.sbuf_tensor("masks_sb", [128, 8 * mask_w], BF16))
        P = Plan(nc, ctx)
        blk = ctx.enter_context(nc.Block())
        cfg = {
            "NB": NB, "NQ": NQ, "units": make_units(version),
            "xg": xg, "xm": None, "lnm_is_lnf": True,
            "wq": wq, "wk": wk, "wv": wv, "wu": wu, "bu": bu,
            "masks_dram": masks_dram, "mask_w": mask_w,
            "aln_dst": lambda b, r0: aln[b, r0:r0 + 128, :],
        }
        emit_setup(P, B)
        emit_attention(P, B, cfg)
        P.run(blk)
    return nc


def build_ffn_program():
    from contextlib import ExitStack
    nc = bass.Bass("TRN2", target_bir_lowering=False)
    ctx = ExitStack()
    NT = 16
    xm = nc.dram_tensor("xm", [NT, 256, D], F32, kind="ExternalInput").ap()
    aln = nc.dram_tensor("aln", [NT, 256, D], BF16, kind="ExternalInput").ap()
    drams = {n: nc.dram_tensor(n, [D, D], F32, kind="ExternalInput").ap()
             for n in ("wf", "wi", "wc", "wo", "uf", "ui", "uc", "uo")}
    bias = {n: nc.dram_tensor(n, [D], F32, kind="ExternalInput").ap()
            for n in ("bf", "bi", "bc", "bo", "in_c")}
    y = nc.dram_tensor("y", [NT, 256, D], F32, kind="ExternalOutput").ap()
    ln2_dram = nc.dram_tensor("ln2s", [NT, 128, 2048], BF16, kind="Internal")

    with ctx:
        B = alloc_common(nc, ctx, need_attn=False, need_ffn=True)
        P = Plan(nc, ctx)
        blk = ctx.enter_context(nc.Block())
        cfg = dict(drams)
        cfg.update(bias)
        cfg.update({
            "xm_ffn": xm, "y": y,
            "aln_src": lambda t, r0: aln[t, r0:r0 + 128, :],
            "ln2_dram": ln2_dram.ap(),
        })
        emit_setup(P, B)
        emit_ffn(P, B, cfg)
        P.run(blk)
    return nc


def build_fused_program():
    from contextlib import ExitStack
    nc = bass.Bass("TRN2", target_bir_lowering=False)
    ctx = ExitStack()
    NB, NQ, mask_w, NT = 16, 256, 16, 16

    xg = nc.dram_tensor("xg", [NB, SEQ, D], F32, kind="ExternalInput").ap()
    xm = nc.dram_tensor("xm", [NT, 256, D], F32, kind="ExternalInput").ap()
    attnw = {n: nc.dram_tensor(n, [D, D], F32, kind="ExternalInput").ap()
             for n in ("wq", "wk", "wv", "wu")}
    bu = nc.dram_tensor("bu", [D], F32, kind="ExternalInput").ap()
    ffnw = {n: nc.dram_tensor(n, [D, D], F32, kind="ExternalInput").ap()
            for n in ("wf", "wi", "wc", "wo", "uf", "ui", "uc", "uo")}
    bias = {n: nc.dram_tensor(n, [D], F32, kind="ExternalInput").ap()
            for n in ("bf", "bi", "bc", "bo", "in_c")}
    masks_dram = nc.dram_tensor("masks", [128, 8 * mask_w], BF16,
                                kind="ExternalInput").ap()
    y = nc.dram_tensor("y", [NT, 256, D], F32, kind="ExternalOutput").ap()
    aln_dram = nc.dram_tensor("alns", [NB, NQ, D], BF16, kind="Internal")
    ln2_dram = nc.dram_tensor("ln2s", [NT, 128, 2048], BF16, kind="Internal")

    with ctx:
        B = alloc_common(nc, ctx, need_attn=True, need_ffn="alias")
        B["lnm"] = B["of"].ap()[:, :]
        B["masks"] = ctx.enter_context(
            nc.sbuf_tensor("masks_sb", [128, 8 * mask_w], BF16))
        exap = B["ex"].ap()
        B["al"] = [exap[:, 0:1024], exap[:, 1024:2048]]
        B["vt"] = exap[:, 0:1024]
        ln_slots = [B["xt"][0].ap()[:, :], B["xt"][1].ap()[:, :],
                    B["ex"].ap().bitcast(F32),
                    B["of"].ap().bitcast(F32)]
        P = Plan(nc, ctx)
        blk = ctx.enter_context(nc.Block())
        cfg_a = {
            "NB": NB, "NQ": NQ, "units": make_units("fused"),
            "xg": xg, "xm": xm, "lnm_is_lnf": False,
            "wq": attnw["wq"], "wk": attnw["wk"], "wv": attnw["wv"],
            "wu": attnw["wu"], "bu": bu,
            "masks_dram": masks_dram, "mask_w": mask_w,
            "aln_dst": lambda b, r0: aln_dram.ap()[b, r0:r0 + 128, :],
            "al_on_ex": True, "ln_slots": ln_slots,
        }
        cfg_f = dict(ffnw)
        cfg_f.update(bias)
        cfg_f.update({
            "xm_ffn": xm, "y": y,
            "aln_src": lambda t, r0: aln_dram.ap()[t, r0:r0 + 128, :],
            "ln2_dram": ln2_dram.ap(),
        })
        emit_setup(P, B)
        emit_attention(P, B, cfg_a)
        for eng in ENGINES:
            P.wait(eng, "st_al0")
            P.wait(eng, "st_al1")
        if not SKIP_FFN:
            emit_ffn(P, B, cfg_f)
        P.run(blk)
    return nc


def rows_for_core(c):
    return np.array([16 * (8 * k + c) + r for k in range(16)
                     for r in range(16)], dtype=np.int64)


def make_masks(version, core):
    mask_w = 128 if version == "unfused" else 16
    rk = np.arange(128)[:, None]
    jj = np.arange(mask_w)[None, :]
    if version == "unfused":
        m = np.where(rk <= jj, 0.0, MASKVAL)
    else:
        m = np.where(rk <= 16 * core + jj, 0.0, MASKVAL)
    return np.ascontiguousarray(np.tile(m, (1, 8)).astype(ml_dtypes.bfloat16))


_CACHE = {}


def fused_in_map(x, w, c):
    rows = rows_for_core(c)
    return {
        "xg": x,
        "xm": np.ascontiguousarray(x[:, rows, :]),
        "wq": w["L0_attn_w_q"], "wk": w["L0_attn_w_k"],
        "wv": w["L0_attn_w_v"], "wu": w["L0_attn_w_u"],
        "bu": w["L0_attn_b_u"],
        "wf": w["L0_ffn_wf"], "wi": w["L0_ffn_wi"], "wc": w["L0_ffn_wc"],
        "wo": w["L0_ffn_wo"], "uf": w["L0_ffn_uf"], "ui": w["L0_ffn_ui"],
        "uc": w["L0_ffn_uc"], "uo": w["L0_ffn_uo"],
        "bf": w["L0_ffn_bf"], "bi": w["L0_ffn_bi"], "bc": w["L0_ffn_bc"],
        "bo": w["L0_ffn_bo"], "in_c": w["L0_ffn_in_c"],
        "masks": make_masks("fused", c),
    }


def kernel_fused(**inp):
    x = np.ascontiguousarray(inp["L0_attn_x"])
    if "fused" not in _CACHE:
        _CACHE["fused"] = build_fused_program()
    in_maps = [fused_in_map(x, inp, c) for c in range(NCORES)]
    res = run_bass_kernel_spmd(_CACHE["fused"], in_maps,
                               core_ids=list(range(NCORES)))
    out = np.empty((16, SEQ, D), dtype=np.float32)
    for c in range(NCORES):
        out[:, rows_for_core(c), :] = np.asarray(res.results[c]["y"],
                                                 dtype=np.float32)
    return out


USE_FUSED = True


def kernel(**inp):
    if USE_FUSED:
        return kernel_fused(**inp)
    x = np.ascontiguousarray(inp["L0_attn_x"])
    w = inp

    version = "unfused"
    if "attn" not in _CACHE:
        _CACHE["attn"] = build_attn_program(version)
    in_maps = []
    for c in range(NCORES):
        in_maps.append({
            "xg": x[2 * c:2 * c + 2],
            "wq": w["L0_attn_w_q"], "wk": w["L0_attn_w_k"],
            "wv": w["L0_attn_w_v"], "wu": w["L0_attn_w_u"],
            "bu": w["L0_attn_b_u"],
            "masks": make_masks(version, c),
        })
    res = run_bass_kernel_spmd(_CACHE["attn"], in_maps,
                               core_ids=list(range(NCORES)))
    aln_full = np.concatenate([r["aln"] for r in res.results], axis=0)

    if "ffn" not in _CACHE:
        _CACHE["ffn"] = build_ffn_program()
    in_maps = []
    for c in range(NCORES):
        rows = slice(256 * c, 256 * (c + 1))
        in_maps.append({
            "xm": np.ascontiguousarray(x[:, rows, :]),
            "aln": np.ascontiguousarray(aln_full[:, rows, :]),
            "wf": w["L0_ffn_wf"], "wi": w["L0_ffn_wi"], "wc": w["L0_ffn_wc"],
            "wo": w["L0_ffn_wo"], "uf": w["L0_ffn_uf"], "ui": w["L0_ffn_ui"],
            "uc": w["L0_ffn_uc"], "uo": w["L0_ffn_uo"],
            "bf": w["L0_ffn_bf"], "bi": w["L0_ffn_bi"], "bc": w["L0_ffn_bc"],
            "bo": w["L0_ffn_bo"], "in_c": w["L0_ffn_in_c"],
        })
    res = run_bass_kernel_spmd(_CACHE["ffn"], in_maps,
                               core_ids=list(range(NCORES)))
    y = np.concatenate([r["y"] for r in res.results], axis=1)
    return np.ascontiguousarray(y.astype(np.float32))


# version 22 (final)
# speedup vs baseline: 1.1919x; 1.0815x over previous
import numpy as np
import ml_dtypes
import concourse.bass as bass
import concourse.mybir as mybir
from concourse.bass_utils import run_bass_kernel_spmd

F32 = mybir.dt.float32
BF16 = mybir.dt.bfloat16
AF = mybir.ActivationFunctionType
ALU = mybir.AluOpType

NCORES = 8
SEQ = 2048
D = 1024
NKT = SEQ // 128
EPS = 1e-5
MASKVAL = -30000.0

ENGINES = ("sync", "scalar", "vector", "tensor", "gpsimd")
DEBUG_NB = None
DEBUG_STAGE = "full"
SKIP_FFN = False


class Plan:

    def __init__(self, nc, ctx):
        self.nc = nc
        self.ctx = ctx
        self.ops = {e: [] for e in ENGINES}
        self.sems = {}
        self.val = {}

    def sem(self, name):
        if name not in self.sems:
            self.sems[name] = self.ctx.enter_context(self.nc.semaphore(name))
            self.val[name] = 0
        return self.sems[name]

    def op(self, eng, fn, inc=None, inc_amt=1):
        semh = self.sem(inc) if inc else None
        self.ops[eng].append(("op", fn, semh, inc_amt))
        if inc:
            self.val[inc] += inc_amt

    def dma(self, eng, fn, inc):
        self.op(eng, fn, inc=inc, inc_amt=16)

    def wait(self, eng, sem, value=None):
        if value is None:
            value = self.val.get(sem, 0)
        if value > 0:
            self.ops[eng].append(("wait", self.sem(sem), value))

    def v(self, sem):
        return self.val.setdefault(sem, 0)

    def run(self, block):
        def make(name):
            items = self.ops[name]

            def body(eng):
                for it in items:
                    if it[0] == "wait":
                        eng.wait_ge(it[1], it[2])
                    else:
                        inst = it[1](eng)
                        if it[2] is not None:
                            inst.then_inc(it[2], it[3])
            return body
        block.sync(make("sync"))
        block.scalar(make("scalar"))
        block.vector(make("vector"))
        block.tensor(make("tensor"))
        block.gpsimd(make("gpsimd"))




def _apof(buf):
    return buf.ap() if hasattr(buf, "ap") and callable(getattr(buf, "ap")) \
        else buf


def sub3(buf, off, d1, d2):
    a = _apof(buf)
    return bass.AP(a.tensor, a.offset + off, [[a.ap[0][0], a.ap[0][1]], d1, d2])


def sub2(buf, off, d1):
    a = _apof(buf)
    return bass.AP(a.tensor, a.offset + off, [[a.ap[0][0], a.ap[0][1]], d1])


def rowstride(buf):
    return _apof(buf).ap[0][0]



def alloc_common(nc, ctx, need_attn, need_ffn):
    E = ctx.enter_context
    B = {}
    B["PA"] = E(nc.psum_tensor("PA", [128, 2048], F32))
    B["PB"] = E(nc.psum_tensor("PB", [128, 2048], F32))
    B["bnb"] = [E(nc.sbuf_tensor(f"bnb{i}", [128, 12], F32)) for i in range(4)]
    B["mv"] = [E(nc.sbuf_tensor(f"mv{i}", [128, 2], F32)) for i in range(4)]
    B["nm"] = [E(nc.sbuf_tensor(f"nm{i}", [128, 1], F32)) for i in range(4)]
    B["rs"] = [E(nc.sbuf_tensor(f"rs{i}", [128, 1], F32)) for i in range(4)]
    B["sq1"] = [E(nc.sbuf_tensor(f"sq1_{i}", [128, 1], F32)) for i in range(4)]
    B["sq2"] = [E(nc.sbuf_tensor(f"sq2_{i}", [128, 1], F32)) for i in range(4)]
    B["eps"] = E(nc.sbuf_tensor("eps", [128, 1], F32))
    B["xt"] = [E(nc.sbuf_tensor(f"xt{i}", [128, 1024], F32)) for i in range(2)]
    B["lt"] = E(nc.sbuf_tensor("lt", [128, 1024], F32))
    B["identf"] = E(nc.sbuf_tensor("identf", [128, 128], F32))
    B["identb"] = E(nc.sbuf_tensor("identb", [128, 128], BF16))
    B["ones1"] = E(nc.sbuf_tensor("ones1", [1, 128], BF16))
    if need_attn:
        B["lnf"] = E(nc.sbuf_tensor("lnf", [128, 8 * 2048], BF16))
        B["kf"] = E(nc.sbuf_tensor("kf", [128, 8 * 2048], BF16))
        B["va"] = E(nc.sbuf_tensor("va", [128, 16 * 1040], BF16))
        B["vt"] = E(nc.sbuf_tensor("vt", [128, 1024], BF16))
        B["qf"] = E(nc.sbuf_tensor("qf", [128, 2 * 8 * 256], BF16))
        B["ex"] = E(nc.sbuf_tensor("ex", [128, 2048], BF16))
        B["of"] = E(nc.sbuf_tensor("of", [128, 8 * 256], BF16))
        B["al"] = [E(nc.sbuf_tensor(f"al{i}", [128, 1024], BF16)) for i in range(2)]
        B["wq"] = E(nc.sbuf_tensor("wq_sb", [128, 8192], BF16))
        B["wk"] = E(nc.sbuf_tensor("wk_sb", [128, 8192], BF16))
        B["wv"] = E(nc.sbuf_tensor("wv_sb", [128, 8192], BF16))
        B["wu"] = E(nc.sbuf_tensor("wu_sb", [128, 8192], BF16))
        B["bu_row"] = E(nc.sbuf_tensor("bu_row", [1, 1024], BF16))
        B["rb"] = E(nc.sbuf_tensor("rb", [64, 1024], BF16))
    if need_ffn == "alias":
        lnf, kf, va = B["lnf"].ap(), B["kf"].ap(), B["va"].ap()
        B["wfT"] = lnf[:, 0:8192]
        B["wiT"] = lnf[:, 8192:16384]
        B["wcT"] = kf[:, 0:8192]
        B["ufT"] = kf[:, 8192:16384]
        B["uiT"] = B["wq"].ap()[:, :]
        B["ucT"] = B["wk"].ap()[:, :]
        B["woT"] = B["wv"].ap()[:, :]
        B["uoT"] = B["wu"].ap()[:, :]
        B["cb"] = [va[:, 0:2048], va[:, 2048:4096]]
        B["l2"] = [va[:, 4096:6144], va[:, 6144:8192]]
        B["alt"] = [va[:, 8192:9216], va[:, 9216:10240]]
        B["cuo"] = va[:, 10240:12288]
        B["bo_row"] = va[0:1, 12288:13312]
        cap = B["PA"].ap()[:, 0:2048]
        B["c"] = [cap, cap]
        B["c_in_psum"] = True
        B["hh"] = B["PA"].ap()[:, 0:1024]
        ltap = B["lt"].ap()
        B["oo"] = ltap[:, 0:1024]
        B["gates"] = [ltap[:, 0:256], ltap[:, 256:512], ltap[:, 512:768]]
        B["x1"] = E(nc.sbuf_tensor("x1_", [128, 1024], F32))
        B["bcol"] = E(nc.sbuf_tensor("bcol", [128, 24], F32))
        B["inccol"] = E(nc.sbuf_tensor("inccol", [128, 8], F32))
    elif need_ffn:
        for w in ("wfT", "wiT", "wcT", "ufT", "uiT", "ucT", "woT", "uoT"):
            B[w] = E(nc.sbuf_tensor(w, [128, 8192], BF16))
        B["c"] = [E(nc.sbuf_tensor(f"c{i}", [128, 2048], F32)) for i in range(2)]
        B["cb"] = [E(nc.sbuf_tensor(f"cb{i}", [128, 2048], BF16)) for i in range(2)]
        B["l2"] = [E(nc.sbuf_tensor(f"l2{i}", [128, 2048], BF16)) for i in range(2)]
        B["gates"] = [E(nc.sbuf_tensor(f"gate{i}", [128, 256], F32)) for i in range(3)]
        B["x1"] = E(nc.sbuf_tensor("x1_", [128, 1024], F32))
        B["alt"] = [E(nc.sbuf_tensor(f"alt{i}", [128, 1024], BF16)) for i in range(2)]
        B["cuo"] = E(nc.sbuf_tensor("cuo", [128, 2048], BF16))
        B["hh"] = E(nc.sbuf_tensor("hh", [128, 1024], F32))
        B["oo"] = E(nc.sbuf_tensor("oo", [128, 1024], F32))
        B["bcol"] = E(nc.sbuf_tensor("bcol", [128, 24], F32))
        B["inccol"] = E(nc.sbuf_tensor("inccol", [128, 8], F32))
        B["bo_row"] = E(nc.sbuf_tensor("bo_row", [1, 1024], BF16))
    return B


def emit_setup(P, B):
    P.op("gpsimd", lambda g: g.memset(B["eps"][:], EPS))
    P.op("gpsimd", lambda g: g.memset(B["identf"][:], 0.0), inc="gp_setup")
    P.wait("gpsimd", "gp_setup")
    P.op("gpsimd", lambda g: g.affine_select(
        out=B["identf"][:], in_=B["identf"][:], compare_op=ALU.not_equal,
        fill=1.0, base=0, pattern=[[-1, 128]], channel_multiplier=1))
    P.op("gpsimd", lambda g: g.memset(B["ones1"][:], 1.0), inc="gp_setup")
    P.op("gpsimd", lambda g: g.memset(B["identb"][:], 0.0), inc="gp_setup")
    P.wait("gpsimd", "gp_setup")
    P.op("gpsimd", lambda g: g.affine_select(
        out=B["identb"][:], in_=B["identb"][:], compare_op=ALU.not_equal,
        fill=1.0, base=0, pattern=[[-1, 128]], channel_multiplier=1),
        inc="gp_setup")
    if "va" in B:
        P.op("gpsimd", lambda g: g.memset(B["qf"][:], 0.0), inc="gp_setup")
        P.op("gpsimd", lambda g: g.memset(B["va"][:], 0.0), inc="gp_setup")
        P.wait("gpsimd", "gp_setup")
        P.op("gpsimd", lambda g: g.memset(
            sub3(B["va"], 64, [1040, 16], [65, 16]), 1.0),
            inc="gp_setup")
    P.wait("tensor", "gp_setup")
    P.wait("vector", "gp_setup")
    P.wait("scalar", "gp_setup")


def emit_small_ln(P, B, par, src_fn, width=1024):
    bnb, mv, nm, rs = B["bnb"][par], B["mv"][par], B["nm"][par], B["rs"][par]

    def stats(v):
        src = src_fn()
        nh = width // 512
        for i in range(nh - 1):
            v.bn_stats(bnb[:, 6 * i:6 * (i + 1)], src[:, 512 * i:512 * (i + 1)])
        i = nh - 1
        return v.bn_stats(bnb[:, 6 * i:6 * (i + 1)],
                          src[:, 512 * i:512 * (i + 1)])
    P.op("vector", stats, inc="ve")
    P.wait("vector", "ve")
    P.op("vector", lambda v: v.bn_aggr(mv[:], bnb[:, 0:6 * (width // 512)]),
         inc="ve")
    P.wait("vector", "ve")
    P.op("vector", lambda v: v.tensor_scalar(nm[:], mv[:, 0:1], -1.0, None,
                                             ALU.mult), inc="ve")
    P.wait("scalar", "ve")
    P.op("scalar", lambda s: s.activation(rs[:], mv[:, 1:2], AF.Ln,
                                          bias=B["eps"][:, 0:1]), inc="se")
    P.wait("scalar", "se")
    P.op("scalar", lambda s: s.activation(rs[:], rs[:], AF.Exp, scale=-0.5),
         inc="ve_rs")
    return nm, rs


I32 = mybir.dt.int32
RSQRT_K = 0x5f3759df


def emit_rsqrt_dve(P, B, par):
    mv, nm, rs = B["mv"][par], B["nm"][par], B["rs"][par]
    q1, q2 = B["sq1"][par], B["sq2"][par]
    q1i = q1.ap().bitcast(I32)
    q2i = q2.ap().bitcast(I32)
    P.op("vector", lambda v: v.tensor_scalar(nm[:], mv[:, 0:1], -1.0, None,
                                             ALU.mult), inc="ve")
    P.op("vector", lambda v: v.tensor_single_scalar(
        q1[:], mv[:, 1:2], EPS, ALU.add), inc="ve")
    P.wait("vector", "ve")
    P.op("vector", lambda v: v.tensor_single_scalar(
        q2i, q1i, 1, ALU.arith_shift_right), inc="ve")
    P.wait("vector", "ve")
    P.op("vector", lambda v: v.tensor_scalar(
        q2i, q2i, -1, RSQRT_K, ALU.mult, ALU.add), inc="ve")
    for it in range(2):
        dst = q2 if it == 0 else rs
        P.wait("vector", "ve")
        P.op("vector", lambda v: v.tensor_mul(rs[:], q1[:], q2[:]), inc="ve")
        P.wait("vector", "ve")
        P.op("vector", lambda v: v.tensor_mul(rs[:], rs[:], q2[:]), inc="ve")
        P.wait("vector", "ve")
        P.op("vector", lambda v: v.tensor_scalar(
            rs[:], rs[:], -0.5, 1.5, ALU.mult, ALU.add), inc="ve")
        P.wait("vector", "ve")
        P.op("vector", lambda v, dst=dst: v.tensor_mul(dst[:], q2[:], rs[:]),
             inc="ve" if it == 0 else "ve_rs")
    return nm, rs


def emit_small_ln_dve(P, B, par, src_fn, width=1024):
    bnb, mv = B["bnb"][par], B["mv"][par]

    def stats(v):
        src = src_fn()
        nh = width // 512
        for i in range(nh - 1):
            v.bn_stats(bnb[:, 6 * i:6 * (i + 1)], src[:, 512 * i:512 * (i + 1)])
        i = nh - 1
        return v.bn_stats(bnb[:, 6 * i:6 * (i + 1)],
                          src[:, 512 * i:512 * (i + 1)])
    P.op("vector", stats, inc="ve")
    P.wait("vector", "ve")
    P.op("vector", lambda v: v.bn_aggr(mv[:], bnb[:, 0:6 * (width // 512)]),
         inc="ve")
    P.wait("vector", "ve")
    return emit_rsqrt_dve(P, B, par)


def emit_ln_tile_bf(P, B, src_dram_ap, k, dst_fm, dst_col, slots, pend=None):
    nslot = len(slots)
    par4 = k % nslot
    par2 = k % 2
    xt = slots[par4]
    lt = (B["lt"] if par2 == 0 else B["x1"]).ap().bitcast(BF16)[:, 0:1024]
    PA = B["PA"]
    P.wait("sync", "gp_lnapply", max(P.v("gp_lnapply") - (nslot - 1), 0))
    P.dma("sync", lambda sy: sy.dma_start(out=xt, in_=src_dram_ap),
          inc=f"ldx{par4}")
    P.wait("vector", "gp_lnapply", max(P.v("gp_lnapply") - (nslot - 1), 0))
    P.wait("vector", f"ldx{par4}")
    bnb, mv, nm, rs = (B["bnb"][par4], B["mv"][par4], B["nm"][par4],
                       B["rs"][par4])

    def stats(v):
        v.bn_stats(bnb[:, 0:6], xt[:, 0:512])
        return v.bn_stats(bnb[:, 6:12], xt[:, 512:1024])
    P.op("vector", stats, inc="ve")
    P.wait("vector", "ve")
    P.op("vector", lambda v: v.bn_aggr(mv[:], bnb[:]), inc="ve")
    P.wait("vector", "ve")
    P.op("vector", lambda v: v.tensor_scalar(nm[:], mv[:, 0:1], -1.0, None,
                                             ALU.mult), inc="ve")
    P.wait("scalar", "ve")
    P.op("scalar", lambda sc: sc.activation(rs[:], mv[:, 1:2], AF.Ln,
                                            bias=B["eps"][:, 0:1]), inc="se")
    P.wait("scalar", "se")
    P.op("scalar", lambda sc: sc.activation(rs[:], rs[:], AF.Exp, scale=-0.5),
         inc="ve_rs")
    if pend is not None:
        pend()
    P.wait("gpsimd", "ve_rs")
    P.wait("gpsimd", "te_T", max(P.v("te_T") - 1, 0))
    P.op("gpsimd", lambda g, lt=lt: g.tensor_scalar(
        lt, xt, nm[:, 0:1], rs[:, 0:1], ALU.add, ALU.mult),
        inc="gp_lnapply")
    P.wait("tensor", "gp_lnapply")
    P.wait("tensor", "ve_Tevac", max(P.v("ve_Tevac") - 2, 0))
    pslot = k % 4
    pbuf = (PA if pslot < 2 else B["PB"]).ap().bitcast(BF16)
    po = 1024 * (pslot % 2)
    for ch in range(8):
        P.op("tensor", lambda t, ch=ch, po=po, lt=lt, pbuf=pbuf: t.transpose(
            pbuf[:, po + 128 * ch: po + 128 * (ch + 1)],
            lt[:, 128 * ch:128 * (ch + 1)], B["identb"][:]),
            inc="te_T" if ch == 7 else None)

    def evac():
        P.wait("scalar", "te_T")
        stride = dst_fm.shape[1] // 8
        P.op("scalar",
             lambda sc, po=po, dst_col=dst_col, dst_fm=dst_fm, stride=stride,
             pbuf=pbuf:
             sc.copy(
                 sub3(dst_fm, dst_col, [stride, 8], [1, 128]),
                 sub3(pbuf, po, [128, 8], [1, 128])),
             inc="ve_Tevac")
    return evac


def emit_ln_tile(P, B, src_dram_ap, par, dst_fm, dst_col, lt2=False):
    xt = B["xt"][par]
    lt = (B["lt"] if (par == 0 or not lt2) else B["x1"])
    PA = B["PA"]
    P.wait("sync", "gp_lnapply", max(P.v("gp_lnapply") - 1, 0))
    P.dma("sync", lambda sy: sy.dma_start(out=xt[:], in_=src_dram_ap),
          inc=f"ldx{par}")
    P.wait("vector", "gp_lnapply", max(P.v("gp_lnapply") - 1, 0))
    P.wait("vector", f"ldx{par}")
    nm, rs = emit_small_ln(P, B, par, lambda: xt[:])
    P.wait("gpsimd", "ve_rs")
    if lt2:
        P.wait("gpsimd", "te_T", max(P.v("te_T") - 1, 0))
    else:
        P.wait("gpsimd", "te_T")
    P.op("gpsimd", lambda g, lt=lt: g.tensor_scalar(
        lt[:], xt[:], nm[:, 0:1], rs[:, 0:1], ALU.add, ALU.mult),
        inc="gp_lnapply")
    P.wait("tensor", "gp_lnapply")
    P.wait("tensor", "ve_Tevac", max(P.v("ve_Tevac") - 1, 0))
    po = 1024 * par
    for ch in range(8):
        P.op("tensor", lambda t, ch=ch, po=po, lt=lt: t.transpose(
            PA[:, po + 128 * ch: po + 128 * (ch + 1)],
            lt[:, 128 * ch:128 * (ch + 1)], B["identf"][:]),
            inc="te_T" if ch == 7 else None)
    P.wait("scalar", "te_T")
    stride = dst_fm.shape[1] // 8
    P.op("scalar", lambda sc, po=po, dst_col=dst_col, dst_fm=dst_fm, stride=stride:
         sc.copy(
             sub3(dst_fm, dst_col, [stride, 8], [1, 128]),
             sub3(PA, po, [128, 8], [1, 128])),
         inc="ve_Tevac")



def emit_attention(P, B, cfg):
    NB = DEBUG_NB or cfg["NB"]
    units = cfg["units"]
    xg, xm = cfg["xg"], cfg["xm"]
    lnf, kf, va, qf, ex, of = (B["lnf"], B["kf"], B["va"], B["qf"], B["ex"],
                               B["of"])
    wq, wk, wv, wu = B["wq"], B["wk"], B["wv"], B["wu"]
    PA, PB = B["PA"], B["PB"]
    lnm = lnf if cfg["lnm_is_lnf"] else B["lnm"]
    masks = B["masks"]

    for name, arena in (("wq", wq), ("wk", wk), ("wv", wv), ("wu", wu)):
        dram = cfg[name]
        for dmc in range(8):
            P.dma("gpsimd", lambda g, dram=dram, arena=arena, dmc=dmc: g.dma_start(
                out=arena[:, 1024 * dmc:1024 * (dmc + 1)],
                in_=dram[128 * dmc:128 * (dmc + 1), :]), inc="ldw")
    P.dma("gpsimd", lambda g: g.dma_start(out=B["bu_row"][:],
                                          in_=cfg["bu"][None, :]), inc="ldw")
    P.dma("sync", lambda sy: sy.dma_start(out=masks[:], in_=cfg["masks_dram"]),
          inc="ld_mask")
    n_ldw = P.v("ldw")
    P.wait("tensor", "ldw", n_ldw)

    for b in range(NB):
        P.wait("vector", "te_v")
        P.wait("vector", "te_s")
        P.wait("vector", "te_q")
        P.wait("vector", "te_up")
        P.wait("gpsimd", "te_av")
        P.wait("tensor", "ve_al")
        P.wait("tensor", "ve_norm")
        if cfg.get("al_on_ex"):
            P.wait("scalar", "st_al0")
            P.wait("scalar", "st_al1")
            P.wait("vector", "st_al0")
            P.wait("vector", "st_al1")
            P.wait("gpsimd", "st_al0")
            P.wait("gpsimd", "st_al1")
            P.wait("sync", "st_al0")
            P.wait("sync", "st_al1")
        slots = cfg["ln_slots"]
        pends = []
        for tt in range(NKT):
            pd = pends.pop(0) if len(pends) >= 2 else None
            pends.append(emit_ln_tile_bf(
                P, B, xg[b, 128 * tt:128 * (tt + 1), :],
                tt, lnf, 128 * tt, slots, pd))
        if not cfg["lnm_is_lnf"]:
            for tt in range(cfg["NQ"] // 128):
                pd = pends.pop(0) if len(pends) >= 2 else None
                pends.append(emit_ln_tile_bf(
                    P, B, xm[b, 128 * tt:128 * (tt + 1), :],
                    tt, lnm, 128 * tt, slots[0:2], pd))
        for pd in pends:
            pd()
        P.wait("tensor", "ve_Tevac")
        P.wait("vector", "gp_lnapply")
        if DEBUG_STAGE == "ln1":
            continue

        for g4 in range(4):
            for hp in range(8):
                par = (g4 * 8 + hp) % 2
                po = 512 * par
                P.wait("tensor", "ve_kevac", max(P.v("ve_kevac") - 1, 0))
                for dmc in range(8):
                    P.op("tensor", lambda t, dmc=dmc, hp=hp, g4=g4, po=po: t.matmul(
                        PA[:, po:po + 512],
                        wk[:, 1024 * dmc + 128 * hp: 1024 * dmc + 128 * (hp + 1)],
                        lnf[:, 2048 * dmc + 512 * g4: 2048 * dmc + 512 * (g4 + 1)],
                        start=(dmc == 0), stop=(dmc == 7), skip_group_check=True),
                        inc="te_k" if dmc == 7 else None)
                P.wait("vector", "te_k")
                P.op("vector", lambda v, hp=hp, g4=g4, po=po: v.tensor_copy(
                    kf[:, 2048 * hp + 512 * g4: 2048 * hp + 512 * (g4 + 1)],
                    PA[:, po:po + 512]), inc="ve_kevac")

        for kt in range(NKT):
            par = kt % 2
            po = 1024 * par
            P.wait("tensor", "gp_vrepack", max(P.v("gp_vrepack") - 1, 0))
            for dmc in range(8):
                for hf in range(2):
                    P.op("tensor", lambda t, dmc=dmc, kt=kt, hf=hf, po=po: t.matmul(
                        PB[:, po + 512 * hf: po + 512 * (hf + 1)],
                        lnf[:, 2048 * dmc + 128 * kt: 2048 * dmc + 128 * (kt + 1)],
                        wv[:, 1024 * dmc + 512 * hf: 1024 * dmc + 512 * (hf + 1)],
                        start=(dmc == 0), stop=(dmc == 7),
                        skip_group_check=True),
                        inc="te_v" if (dmc == 7 and hf == 1) else None)
            P.wait("vector", "te_v")
            P.op("vector", lambda v, po=po, kt=kt: v.tensor_copy(
                sub3(va, 1040 * kt, [65, 16], [1, 64]),
                sub3(B["PB"], po, [64, 16], [1, 64])),
                inc="gp_vrepack")

        if DEBUG_STAGE == "kv":
            continue
        P.wait("tensor", "ve_kevac")
        P.wait("tensor", "gp_vrepack")
        P.wait("vector", "ld_mask", 16)
        for unit in units:
            q_off = unit["q_off"]
            src_stride = lnm.shape[1] // 8
            P.wait("tensor", "ve_norm")
            for hp in range(8):
                P.wait("tensor", "ve_qevac")
                for dmc in range(8):
                    P.op("tensor", lambda t, hp=hp, dmc=dmc, q_off=q_off,
                         src_stride=src_stride: t.matmul(
                             PB[:, 1792:2048],
                             wq[:, 1024 * dmc + 128 * hp: 1024 * dmc + 128 * (hp + 1)],
                             lnm[:, src_stride * dmc + q_off:
                                 src_stride * dmc + q_off + 256],
                             start=(dmc == 0), stop=(dmc == 7),
                             skip_group_check=True),
                         inc="te_q" if dmc == 7 else None)
                P.wait("vector", "te_q")
                P.op("vector", lambda v, hp=hp: v.tensor_copy(
                    qf[0:64, 256 * hp:256 * (hp + 1)], PB[0:64, 1792:2048]))
                P.op("vector", lambda v, hp=hp: v.tensor_copy(
                    qf[64:128, 2048 + 256 * hp:2048 + 256 * (hp + 1)],
                    PB[64:128, 1792:2048]), inc="ve_qevac")
            P.wait("tensor", "ve_qevac")
            if DEBUG_STAGE == "q":
                continue

            for quad in range(4):
                ktl = unit["ktiles"]
                nkt_u = len(ktl)
                P.wait("tensor", "ve_al")

                exp_base = P.v("se_exp")

                def emit_av(idx, quad=quad, ktl=ktl, nkt_u=nkt_u,
                            exp_base=exp_base):
                    i, vs, d_lo, d_w = ktl[idx]
                    P.wait("tensor", "se_exp", exp_base + idx + 1)
                    if idx == 0:
                        P.wait("tensor", "ve_norm")
                    for hq in range(4):
                        h = 4 * quad + hq
                        P.op("tensor", lambda t, i=i, vs=vs, hq=hq, h=h, idx=idx,
                             fb=(idx == 0 and hq % 2 == 0),
                             last=(idx == nkt_u - 1): t.matmul(
                                 PB[0:65, 256 * hq + vs: 256 * (hq + 1)],
                                 va[:, 1040 * i + 65 * h: 1040 * i + 65 * (h + 1)],
                                 sub2(ex, 1024 * (idx % 2) + 256 * hq + vs,
                                      [1, 256 - vs]),
                                 start=fb, stop=last, skip_group_check=True),
                             inc="te_av" if hq == 3 else None)

                for idx, (i, vs, d_lo, d_w) in enumerate(ktl):
                    slot = 1024 * (idx % 2)
                    P.wait("tensor", "se_exp", max(P.v("se_exp") - 1, 0))
                    for hq in range(4):
                        h = 4 * quad + hq
                        hp = h // 2
                        qo = 2048 * (h % 2) + 256 * hp
                        P.op("tensor",
                             lambda t, i=i, vs=vs, hq=hq, hp=hp, qo=qo, slot=slot: t.matmul(
                                 PA[:, slot + 256 * hq + vs: slot + 256 * (hq + 1)],
                                 kf[:, 2048 * hp + 128 * i: 2048 * hp + 128 * (i + 1)],
                                 qf[:, qo + vs: qo + 256],
                                 start=(hq % 2 == 0), stop=False,
                                 skip_group_check=True),
                             inc="te_s" if hq == 3 else None)
                    if d_w:
                        for hq in range(4):
                            P.op("tensor", lambda t, hq=hq, d_lo=d_lo, d_w=d_w,
                                 slot=slot: t.matmul(
                                     PA[:, slot + 256 * hq + d_lo:
                                        slot + 256 * hq + d_lo + d_w],
                                     B["identb"][:],
                                     masks[:, 0:d_w],
                                     start=False, stop=True,
                                     skip_group_check=True),
                                 inc="te_s" if hq == 3 else None)
                    P.wait("scalar", "te_s")
                    P.wait("scalar", "te_av")
                    P.op("scalar", lambda s, vs=vs, slot=slot, idx=idx: s.activation(
                        sub3(ex, 1024 * (idx % 2) + vs, [256, 4], [1, 256 - vs]),
                        sub3(PA, slot + vs, [256, 4], [1, 256 - vs]),
                        AF.Exp, scale=1.0 / 64.0), inc="se_exp")
                    if idx > 0:
                        emit_av(idx - 1)
                emit_av(nkt_u - 1)
                P.wait("scalar", "te_av")
                P.op("scalar", lambda s: s.activation(
                    PB[64:65, 1024:2048], PB[64:65, 0:1024], AF.Ln), inc="se_ln")
                P.wait("scalar", "se_ln")
                P.wait("scalar", "ve_norm")
                P.op("scalar", lambda s: s.activation(
                    B["rb"][0:1, 0:1024], PB[64:65, 1024:2048], AF.Exp,
                    scale=-1.0), inc="se_r2")
                P.wait("tensor", "se_r2")
                P.wait("tensor", "ve_bc")
                for j in range(2):
                    P.op("tensor", lambda t, j=j: t.matmul(
                        PB[0:64, 1024 + 512 * j:1024 + 512 * (j + 1)],
                        B["ones1"][0:1, 0:64],
                        B["rb"][0:1, 512 * j:512 * (j + 1)],
                        start=True, stop=True, skip_group_check=True),
                        inc="te_bc" if j == 1 else None)
                P.wait("vector", "te_bc")
                P.op("vector", lambda v: v.tensor_copy(
                    B["rb"][0:64, 0:1024], PB[0:64, 1024:2048]), inc="ve_bc")
                P.wait("vector", "ve_bc")
                P.wait("vector", "te_up")
                for hq in range(4):
                    h = 4 * quad + hq
                    hp, hlo = h // 2, 64 * (h % 2)
                    P.op("vector", lambda v, hq=hq, hp=hp, hlo=hlo: v.tensor_mul(
                        of[hlo:hlo + 64, 256 * hp:256 * (hp + 1)],
                        PB[0:64, 256 * hq:256 * (hq + 1)],
                        B["rb"][0:64, 256 * hq:256 * (hq + 1)]),
                        inc="ve_norm")
            if DEBUG_STAGE == "norm":
                continue
            P.wait("tensor", "ve_norm")
            for st in range(2):
                po = 1024 * st
                for dc in range(8):
                    for j in range(2):
                        P.op("tensor", lambda t, dc=dc, j=j, st=st, po=po: t.matmul(
                            PA[:, po + 512 * j: po + 512 * (j + 1)],
                            of[:, 256 * dc + 128 * st: 256 * dc + 128 * (st + 1)],
                            wu[:, 1024 * dc + 512 * j: 1024 * dc + 512 * (j + 1)],
                            start=(dc == 0), stop=False,
                            skip_group_check=True))
                for j in range(2):
                    P.op("tensor", lambda t, j=j, po=po: t.matmul(
                        PA[:, po + 512 * j: po + 512 * (j + 1)],
                        B["ones1"][0:1, :],
                        B["bu_row"][0:1, 512 * j:512 * (j + 1)],
                        start=False, stop=True, skip_group_check=True),
                        inc="te_up" if j == 1 else None)
                P.wait("vector", "te_up")
                P.wait("vector", f"st_al{st}")
                nm, rs = emit_small_ln(P, B, st,
                                       lambda po=po: PA[:, po:po + 1024])
                P.wait("vector", "ve_rs")
                P.op("vector", lambda v, st=st, po=po, nm=nm, rs=rs: v.tensor_scalar(
                    B["al"][st][:], PA[:, po:po + 1024], nm[:, 0:1], rs[:, 0:1],
                    ALU.add, ALU.mult), inc="ve_al")
                P.wait("sync", "ve_al")
                dst = cfg["aln_dst"](b, unit["row0"] + 128 * st)
                P.dma("sync", lambda sy, dst=dst, st=st: sy.dma_start(
                    out=dst, in_=B["al"][st][:]), inc=f"st_al{st}")
    P.wait("sync", "st_al0")
    P.wait("sync", "st_al1")



def emit_ffn(P, B, cfg):
    NT = 16
    PA, PB = B["PA"], B["PB"]
    xm = cfg["xm_ffn"]
    identf = B["identf"]

    wlist = [("wf", "wfT"), ("wi", "wiT"), ("wc", "wcT"), ("uf", "ufT"),
             ("ui", "uiT"), ("uc", "ucT"), ("wo", "woT"), ("uo", "uoT")]
    for src_name, dst_name in wlist:
        dram = cfg[src_name]
        dst = B[dst_name]
        for rnd in range(4):
            P.wait("sync", "te_wT")
            for oc in range(2):
                P.dma("sync", lambda sy, dram=dram, oc=oc, rnd=rnd:
                      sy.dma_start(
                          out=B["xt"][oc][:],
                          in_=dram[256 * rnd + 128 * oc:256 * rnd + 128 * (oc + 1), :]),
                      inc="ld_ws")
            P.wait("tensor", "ld_ws")
            for ic in range(8):
                P.wait("tensor", "ve_wevac")
                for oc in range(2):
                    P.op("tensor", lambda t, ic=ic, oc=oc: t.transpose(
                        PA[:, 128 * oc:128 * (oc + 1)],
                        B["xt"][oc][:, 128 * ic: 128 * (ic + 1)],
                        identf[:]),
                        inc="te_wT" if oc == 1 else None)
                P.wait("vector", "te_wT")
                P.op("vector", lambda v, ic=ic, rnd=rnd, dst=dst: v.tensor_copy(
                    dst[:, 1024 * ic + 256 * rnd: 1024 * ic + 256 * rnd + 256],
                    PA[:, 0:256]), inc="ve_wevac")
    P.wait("tensor", "ve_wevac")

    for gi, bname in enumerate(("bf", "bi", "bc")):
        for mc in range(8):
            P.dma("sync", lambda sy, bname=bname, gi=gi, mc=mc: sy.dma_start(
                out=B["bcol"][:, 8 * gi + mc: 8 * gi + mc + 1],
                in_=cfg[bname][128 * mc:128 * (mc + 1), None]), inc="ldb")
    for mc in range(8):
        P.dma("sync", lambda sy, mc=mc: sy.dma_start(
            out=B["inccol"][:, mc:mc + 1],
            in_=cfg["in_c"][128 * mc:128 * (mc + 1), None]), inc="ldb")
    P.dma("gpsimd", lambda g: g.dma_start(out=B["bo_row"][:],
                                          in_=cfg["bo"][None, :]), inc="ldw2")
    P.wait("scalar", "ldb")
    P.wait("scalar", "ldw2")

    P.wait("sync", "te_wT")

    def b1_spill(t):
        P.wait("sync", "ve_l2")
        P.dma("sync", lambda sy, t=t: sy.dma_start(
            out=cfg["ln2_dram"][t], in_=B["l2"][0][:]), inc="st_l2")

    pend = None
    spill_after = None
    for t in range(NT):
        for st in range(2):
            par = st
            P.wait("sync", "ve_x1", max(P.v("ve_x1") - 1, 0))
            P.dma("sync", lambda sy, t=t, st=st, par=par: sy.dma_start(
                out=B["xt"][par][:], in_=xm[t, 128 * st:128 * (st + 1), :]),
                inc=f"ldx{par}")
            P.dma("sync", lambda sy, t=t, st=st, par=par: sy.dma_start(
                out=B["alt"][par][:], in_=cfg["aln_src"](t, 128 * st)),
                inc=f"ldx{par}")
            P.wait("vector", f"ldx{par}")
            P.wait("vector", "gp_lnapply")
            P.op("vector", lambda v, par=par: v.tensor_add(
                B["x1"][:], B["xt"][par][:], B["alt"][par][:]), inc="ve_x1")
            P.wait("vector", "ve_x1")
            nm, rs = emit_small_ln(P, B, par, lambda: B["x1"][:])
            if pend is not None:
                pend()
            P.wait("gpsimd", "ve_rs")
            P.wait("gpsimd", "te_T")
            P.op("gpsimd", lambda g, nm=nm, rs=rs: g.tensor_scalar(
                B["lt"][:], B["x1"][:], nm[:, 0:1], rs[:, 0:1],
                ALU.add, ALU.mult), inc="gp_lnapply")
            P.wait("tensor", "gp_lnapply")
            P.wait("tensor", "ve_l2", max(P.v("ve_l2") - 1, 0))
            po = 1024 * par
            for ch in range(8):
                P.op("tensor", lambda t_, ch=ch, po=po: t_.transpose(
                    PA[:, po + 128 * ch:po + 128 * (ch + 1)],
                    B["lt"][:, 128 * ch:128 * (ch + 1)], identf[:]),
                    inc="te_T" if ch == 7 else None)

            def evac(po=po, st=st, t=t):
                P.wait("scalar", "te_T")
                P.wait("scalar", "st_l2", 16 * t)
                P.op("scalar", lambda sc, po=po, st=st: sc.copy(
                    sub3(B["l2"][0], 128 * st, [256, 8], [1, 128]),
                    sub3(PA, po, [128, 8], [1, 128])),
                    inc="ve_l2")
            pend = evac
            if st == 0 and spill_after is not None:
                b1_spill(spill_after)
                spill_after = None
        spill_after = t
    if pend is not None:
        pend()
    b1_spill(spill_after)

    P.wait("scalar", "ve_x1")
    P.wait("scalar", "ve_l2")
    for mc in range(8):
        P.op("scalar", lambda s, mc=mc: s.activation(
            B["c"][0][:, 256 * mc:256 * (mc + 1)], B["x1"][:, 0:256],
            AF.Identity, bias=B["inccol"][:, mc:mc + 1], scale=0.0),
            inc="se_c0")
    ceng0 = "vector" if B.get("c_in_psum") else "gpsimd"
    P.wait(ceng0, "se_c0")
    P.op(ceng0, lambda g: g.tensor_copy(B["cb"][0][:], B["c"][0][:]),
         inc="gp_cb")
    P.wait("vector", "se_c0")

    P.wait("sync", "st_l2")
    for t in range(NT):
        par = t % 2
        P.wait("sync", "te_g", max(P.v("te_g") - 24, 0))
        P.dma("sync", lambda sy, t=t, par=par: sy.dma_start(
            out=B["l2"][par][:], in_=cfg["ln2_dram"][t]), inc=f"ldl2_{par}")
        P.wait("tensor", f"ldl2_{par}")
        for mc in range(8):
            pgo = 1024 * (mc % 2)
            P.wait("tensor", "se_g", max(P.v("se_g") - 3, 0))
            for gi, wT, uT in ((0, "wfT", "ufT"), (1, "wiT", "uiT"),
                               (2, "wcT", "ucT")):
                doff = pgo + 256 * gi if gi < 2 else pgo + 512
                for ic in range(8):
                    P.op("tensor", lambda t_, wT=wT, ic=ic, mc=mc, doff=doff,
                         par=par, first=(ic == 0 and gi in (0, 2)): t_.matmul(
                             PB[:, doff:doff + 256],
                             B[wT][:, 1024 * ic + 128 * mc: 1024 * ic + 128 * (mc + 1)],
                             B["l2"][par][:, 256 * ic:256 * (ic + 1)],
                             start=first, stop=False, skip_group_check=True))
            P.wait("tensor", "gp_cb", 8 * t + 1)
            for gi, wT, uT in ((0, "wfT", "ufT"), (1, "wiT", "uiT"),
                               (2, "wcT", "ucT")):
                doff = pgo + 256 * gi if gi < 2 else pgo + 512
                for jc in range(8):
                    P.op("tensor", lambda t_, uT=uT, jc=jc, mc=mc, doff=doff,
                         par=par: t_.matmul(
                             PB[:, doff:doff + 256],
                             B[uT][:, 1024 * jc + 128 * mc: 1024 * jc + 128 * (mc + 1)],
                             B["cb"][par][:, 256 * jc:256 * (jc + 1)],
                             start=False, stop=(jc == 7), skip_group_check=True),
                         inc="te_g" if jc == 7 else None)
            P.wait("scalar", "te_g")
            P.wait("scalar", "ve_c")
            for gi, func in ((0, AF.Sigmoid), (1, AF.Sigmoid), (2, AF.Tanh)):
                doff = pgo + 256 * gi if gi < 2 else pgo + 512
                P.op("scalar", lambda s, gi=gi, mc=mc, doff=doff, func=func:
                     s.activation(
                         B["gates"][gi][:], PB[:, doff:doff + 256], func,
                         bias=B["bcol"][:, 8 * gi + mc:8 * gi + mc + 1]),
                     inc="se_g")
            P.wait("vector", "se_g")
            P.wait("vector", "ve_c", max(P.v("ve_c") - 7, 0))
            cs = slice(256 * mc, 256 * (mc + 1))
            P.op("vector", lambda v, cs=cs, par=par: v.tensor_mul(
                B["c"][1 - par][:, cs], B["gates"][0][:], B["c"][par][:, cs]),
                inc="ve_cu")
            P.wait("vector", "ve_cu")
            P.op("vector", lambda v: v.tensor_mul(
                B["gates"][0][:], B["gates"][1][:], B["gates"][2][:]),
                inc="ve_cu")
            P.wait("vector", "ve_cu")
            P.op("vector", lambda v, cs=cs, par=par: v.tensor_add(
                B["c"][1 - par][:, cs], B["c"][1 - par][:, cs],
                B["gates"][0][:]), inc="ve_c")
            ceng = "vector" if B.get("c_in_psum") else "gpsimd"
            P.wait(ceng, "ve_c")
            P.op(ceng, lambda g, cs=cs, par=par: g.tensor_copy(
                B["cb"][1 - par][:, cs], B["c"][1 - par][:, cs]), inc="gp_cb")

    cpar = NT % 2
    P.wait("tensor", "gp_cb", 8 * NT + 1)

    for st in range(2):
        po = 1024 * st
        for mc in range(8):
            for j in range(2):
                P.op("tensor", lambda t, mc=mc, j=j, st=st, po=po: t.matmul(
                    PA[:, po + 512 * j:po + 512 * (j + 1)],
                    B["cb"][cpar][:, 256 * mc + 128 * st: 256 * mc + 128 * (st + 1)],
                    B["uoT"][:, 1024 * mc + 512 * j: 1024 * mc + 512 * (j + 1)],
                    start=(mc == 0), stop=False, skip_group_check=True))
        for j in range(2):
            P.op("tensor", lambda t, j=j, po=po: t.matmul(
                PA[:, po + 512 * j:po + 512 * (j + 1)],
                B["ones1"][0:1, :], B["bo_row"][0:1, 512 * j:512 * (j + 1)],
                start=False, stop=True, skip_group_check=True),
                inc="te_cuo" if j == 1 else None)
    P.wait("vector", "te_cuo")
    P.op("vector", lambda v: v.tensor_copy(B["cuo"][:], PA[:]), inc="ve_cuo")

    P.wait("tensor", "ve_cuo")
    P.wait("vector", "ve_cuo")
    P.wait("scalar", "ve_cuo")
    P.wait("sync", "te_g")
    PAap = B["PA"].ap()
    hhs = [PAap[:, 0:1024], PAap[:, 1024:2048]]
    oos = [B["oo"], B["x1"].ap()[:, 0:1024]]
    l2base = [P.v("ldl2_0"), P.v("ldl2_1")]
    P.dma("sync", lambda sy: sy.dma_start(
        out=B["l2"][0][:], in_=cfg["ln2_dram"][0]), inc="ldl2_0")
    for t in range(NT):
        par = t % 2
        if t + 1 < NT:
            P.wait("sync", "te_xwo", max(P.v("te_xwo") - 2, 0))
            P.dma("sync", lambda sy, t=t: sy.dma_start(
                out=B["l2"][(t + 1) % 2][:], in_=cfg["ln2_dram"][t + 1]),
                inc=f"ldl2_{(t + 1) % 2}")
        P.wait("tensor", f"ldl2_{par}", l2base[par] + 16 * (t // 2 + 1))
        for st in range(2):
            po = 1024 * st
            P.wait("tensor", "se_h", max(P.v("se_h") - (1 - st), 0))
            for j in range(2):
                P.op("tensor", lambda t_, j=j, st=st, po=po: t_.matmul(
                    PB[:, po + 512 * j:po + 512 * (j + 1)],
                    B["identb"][:],
                    B["cuo"][:, 1024 * st + 512 * j:1024 * st + 512 * (j + 1)],
                    start=True, stop=False, skip_group_check=True))
            for ic in range(8):
                for j in range(2):
                    P.op("tensor", lambda t_, ic=ic, j=j, st=st, po=po, par=par:
                         t_.matmul(
                             PB[:, po + 512 * j:po + 512 * (j + 1)],
                             B["l2"][par][:, 256 * ic + 128 * st:
                                           256 * ic + 128 * (st + 1)],
                             B["woT"][:, 1024 * ic + 512 * j:
                                      1024 * ic + 512 * (j + 1)],
                             start=False, stop=(ic == 7),
                             skip_group_check=True),
                         inc="te_xwo" if (ic == 7 and j == 1) else None)
        for st in range(2):
            po = 1024 * st
            P.wait("scalar", "te_xwo", P.v("te_xwo") - (1 - st))
            P.wait("scalar", "ve_oc", max(P.v("ve_oc") - (1 - st), 0))
            P.op("scalar", lambda s, po=po, st=st: s.activation(
                hhs[st], PB[:, po:po + 1024], AF.Tanh), inc="se_h")
        for st in range(2):
            P.wait("sync", "ve_o", max(P.v("ve_o") - (1 - st), 0))
            P.dma("sync", lambda sy, t=t, st=st: sy.dma_start(
                out=B["alt"][st][:], in_=cfg["aln_src"](t, 128 * st)),
                inc=f"ldo{st}")
            P.dma("sync", lambda sy, t=t, st=st: sy.dma_start(
                out=B["xt"][st][:], in_=xm[t, 128 * st:128 * (st + 1), :]),
                inc=f"ldo{st}")
        nmrs = []
        for st in range(2):
            P.wait("vector", "se_h", P.v("se_h") - (1 - st))
            nmrs.append(emit_small_ln_dve(P, B, st,
                                          lambda st=st: hhs[st]))
        for st in range(2):
            P.wait("gpsimd", f"ldo{st}")
            P.wait("gpsimd", "ve_o", max(P.v("ve_o") - (1 - st), 0))
            P.op("gpsimd", lambda g, st=st: g.tensor_add(
                B["xt"][st][:], B["xt"][st][:], B["alt"][st][:]),
                inc="gp_ax")
        for st in range(2):
            nm, rs = nmrs[st]
            P.wait("vector", f"st_y{st}")
            P.op("vector", lambda v, nm=nm, rs=rs, st=st: v.tensor_scalar(
                oos[st][:], hhs[st], nm[:, 0:1], rs[:, 0:1],
                ALU.add, ALU.mult), inc="ve_oc")
            P.wait("vector", "ve_oc")
            P.wait("vector", "gp_ax", P.v("gp_ax") - (1 - st))
            P.op("vector", lambda v, st=st: v.tensor_add(
                oos[st][:], oos[st][:], B["xt"][st][:]), inc="ve_o")
            P.wait("sync", "ve_o")
            P.dma("sync", lambda sy, t=t, st=st: sy.dma_start(
                out=cfg["y"][t, 128 * st:128 * (st + 1), :], in_=oos[st][:]),
                inc=f"st_y{st}")
    P.wait("sync", "st_y0")
    P.wait("sync", "st_y1")



def make_units(version):
    if version == "fused":
        ktiles = [(i, 16 * i, 16 * i, 16) for i in range(16)]
        return [{"q_off": 0, "row0": 0, "ktiles": ktiles}]
    units = []
    for u in range(8):
        kt = [(i, 0, 0, 0) for i in range(2 * u)]
        kt.append((2 * u, 0, 0, 128))
        kt.append((2 * u + 1, 128, 128, 128))
        units.append({"q_off": 256 * u, "row0": 256 * u, "ktiles": kt})
    return units


def build_attn_program(version):
    from contextlib import ExitStack
    nc = bass.Bass("TRN2", target_bir_lowering=False)
    ctx = ExitStack()
    NB = 2 if version == "unfused" else 16
    NQ = 2048 if version == "unfused" else 256
    mask_w = 128 if version == "unfused" else 16

    xg = nc.dram_tensor("xg", [NB, SEQ, D], F32, kind="ExternalInput").ap()
    wq = nc.dram_tensor("wq", [D, D], F32, kind="ExternalInput").ap()
    wk = nc.dram_tensor("wk", [D, D], F32, kind="ExternalInput").ap()
    wv = nc.dram_tensor("wv", [D, D], F32, kind="ExternalInput").ap()
    wu = nc.dram_tensor("wu", [D, D], F32, kind="ExternalInput").ap()
    bu = nc.dram_tensor("bu", [D], F32, kind="ExternalInput").ap()
    masks_dram = nc.dram_tensor("masks", [128, 8 * mask_w], BF16,
                                kind="ExternalInput").ap()
    aln = nc.dram_tensor("aln", [NB, NQ, D], BF16, kind="ExternalOutput").ap()

    with ctx:
        B = alloc_common(nc, ctx, need_attn=True, need_ffn=False)
        B["masks"] = ctx.enter_context(
            nc.sbuf_tensor("masks_sb", [128, 8 * mask_w], BF16))
        P = Plan(nc, ctx)
        blk = ctx.enter_context(nc.Block())
        cfg = {
            "NB": NB, "NQ": NQ, "units": make_units(version),
            "xg": xg, "xm": None, "lnm_is_lnf": True,
            "wq": wq, "wk": wk, "wv": wv, "wu": wu, "bu": bu,
            "masks_dram": masks_dram, "mask_w": mask_w,
            "aln_dst": lambda b, r0: aln[b, r0:r0 + 128, :],
        }
        emit_setup(P, B)
        emit_attention(P, B, cfg)
        P.run(blk)
    return nc


def build_ffn_program():
    from contextlib import ExitStack
    nc = bass.Bass("TRN2", target_bir_lowering=False)
    ctx = ExitStack()
    NT = 16
    xm = nc.dram_tensor("xm", [NT, 256, D], F32, kind="ExternalInput").ap()
    aln = nc.dram_tensor("aln", [NT, 256, D], BF16, kind="ExternalInput").ap()
    drams = {n: nc.dram_tensor(n, [D, D], F32, kind="ExternalInput").ap()
             for n in ("wf", "wi", "wc", "wo", "uf", "ui", "uc", "uo")}
    bias = {n: nc.dram_tensor(n, [D], F32, kind="ExternalInput").ap()
            for n in ("bf", "bi", "bc", "bo", "in_c")}
    y = nc.dram_tensor("y", [NT, 256, D], F32, kind="ExternalOutput").ap()
    ln2_dram = nc.dram_tensor("ln2s", [NT, 128, 2048], BF16, kind="Internal")

    with ctx:
        B = alloc_common(nc, ctx, need_attn=False, need_ffn=True)
        P = Plan(nc, ctx)
        blk = ctx.enter_context(nc.Block())
        cfg = dict(drams)
        cfg.update(bias)
        cfg.update({
            "xm_ffn": xm, "y": y,
            "aln_src": lambda t, r0: aln[t, r0:r0 + 128, :],
            "ln2_dram": ln2_dram.ap(),
        })
        emit_setup(P, B)
        emit_ffn(P, B, cfg)
        P.run(blk)
    return nc


def build_fused_program():
    from contextlib import ExitStack
    nc = bass.Bass("TRN2", target_bir_lowering=False)
    ctx = ExitStack()
    NB, NQ, mask_w, NT = 16, 256, 16, 16

    xg = nc.dram_tensor("xg", [NB, SEQ, D], F32, kind="ExternalInput").ap()
    xm = nc.dram_tensor("xm", [NT, 256, D], F32, kind="ExternalInput").ap()
    attnw = {n: nc.dram_tensor(n, [D, D], F32, kind="ExternalInput").ap()
             for n in ("wq", "wk", "wv", "wu")}
    bu = nc.dram_tensor("bu", [D], F32, kind="ExternalInput").ap()
    ffnw = {n: nc.dram_tensor(n, [D, D], F32, kind="ExternalInput").ap()
            for n in ("wf", "wi", "wc", "wo", "uf", "ui", "uc", "uo")}
    bias = {n: nc.dram_tensor(n, [D], F32, kind="ExternalInput").ap()
            for n in ("bf", "bi", "bc", "bo", "in_c")}
    masks_dram = nc.dram_tensor("masks", [128, 8 * mask_w], BF16,
                                kind="ExternalInput").ap()
    y = nc.dram_tensor("y", [NT, 256, D], F32, kind="ExternalOutput").ap()
    aln_dram = nc.dram_tensor("alns", [NB, NQ, D], BF16, kind="Internal")
    ln2_dram = nc.dram_tensor("ln2s", [NT, 128, 2048], BF16, kind="Internal")

    with ctx:
        B = alloc_common(nc, ctx, need_attn=True, need_ffn="alias")
        B["lnm"] = B["of"].ap()[:, :]
        B["masks"] = ctx.enter_context(
            nc.sbuf_tensor("masks_sb", [128, 8 * mask_w], BF16))
        exap = B["ex"].ap()
        B["al"] = [exap[:, 0:1024], exap[:, 1024:2048]]
        B["vt"] = exap[:, 0:1024]
        ln_slots = [B["xt"][0].ap()[:, :], B["xt"][1].ap()[:, :],
                    B["ex"].ap().bitcast(F32),
                    B["of"].ap().bitcast(F32)]
        P = Plan(nc, ctx)
        blk = ctx.enter_context(nc.Block())
        cfg_a = {
            "NB": NB, "NQ": NQ, "units": make_units("fused"),
            "xg": xg, "xm": xm, "lnm_is_lnf": False,
            "wq": attnw["wq"], "wk": attnw["wk"], "wv": attnw["wv"],
            "wu": attnw["wu"], "bu": bu,
            "masks_dram": masks_dram, "mask_w": mask_w,
            "aln_dst": lambda b, r0: aln_dram.ap()[b, r0:r0 + 128, :],
            "al_on_ex": True, "ln_slots": ln_slots,
        }
        cfg_f = dict(ffnw)
        cfg_f.update(bias)
        cfg_f.update({
            "xm_ffn": xm, "y": y,
            "aln_src": lambda t, r0: aln_dram.ap()[t, r0:r0 + 128, :],
            "ln2_dram": ln2_dram.ap(),
        })
        emit_setup(P, B)
        emit_attention(P, B, cfg_a)
        for eng in ENGINES:
            P.wait(eng, "st_al0")
            P.wait(eng, "st_al1")
        if not SKIP_FFN:
            emit_ffn(P, B, cfg_f)
        P.run(blk)
    return nc


def rows_for_core(c):
    return np.array([16 * (8 * k + c) + r for k in range(16)
                     for r in range(16)], dtype=np.int64)


def make_masks(version, core):
    mask_w = 128 if version == "unfused" else 16
    rk = np.arange(128)[:, None]
    jj = np.arange(mask_w)[None, :]
    if version == "unfused":
        m = np.where(rk <= jj, 0.0, MASKVAL)
    else:
        m = np.where(rk <= 16 * core + jj, 0.0, MASKVAL)
    return np.ascontiguousarray(np.tile(m, (1, 8)).astype(ml_dtypes.bfloat16))


_CACHE = {}


def fused_in_map(x, w, c):
    rows = rows_for_core(c)
    return {
        "xg": x,
        "xm": np.ascontiguousarray(x[:, rows, :]),
        "wq": w["L0_attn_w_q"], "wk": w["L0_attn_w_k"],
        "wv": w["L0_attn_w_v"], "wu": w["L0_attn_w_u"],
        "bu": w["L0_attn_b_u"],
        "wf": w["L0_ffn_wf"], "wi": w["L0_ffn_wi"], "wc": w["L0_ffn_wc"],
        "wo": w["L0_ffn_wo"], "uf": w["L0_ffn_uf"], "ui": w["L0_ffn_ui"],
        "uc": w["L0_ffn_uc"], "uo": w["L0_ffn_uo"],
        "bf": w["L0_ffn_bf"], "bi": w["L0_ffn_bi"], "bc": w["L0_ffn_bc"],
        "bo": w["L0_ffn_bo"], "in_c": w["L0_ffn_in_c"],
        "masks": make_masks("fused", c),
    }


def kernel_fused(**inp):
    x = np.ascontiguousarray(inp["L0_attn_x"])
    if "fused" not in _CACHE:
        _CACHE["fused"] = build_fused_program()
    in_maps = [fused_in_map(x, inp, c) for c in range(NCORES)]
    res = run_bass_kernel_spmd(_CACHE["fused"], in_maps,
                               core_ids=list(range(NCORES)))
    out = np.empty((16, SEQ, D), dtype=np.float32)
    for c in range(NCORES):
        out[:, rows_for_core(c), :] = np.asarray(res.results[c]["y"],
                                                 dtype=np.float32)
    return out


USE_FUSED = True


def kernel(**inp):
    if USE_FUSED:
        return kernel_fused(**inp)
    x = np.ascontiguousarray(inp["L0_attn_x"])
    w = inp

    version = "unfused"
    if "attn" not in _CACHE:
        _CACHE["attn"] = build_attn_program(version)
    in_maps = []
    for c in range(NCORES):
        in_maps.append({
            "xg": x[2 * c:2 * c + 2],
            "wq": w["L0_attn_w_q"], "wk": w["L0_attn_w_k"],
            "wv": w["L0_attn_w_v"], "wu": w["L0_attn_w_u"],
            "bu": w["L0_attn_b_u"],
            "masks": make_masks(version, c),
        })
    res = run_bass_kernel_spmd(_CACHE["attn"], in_maps,
                               core_ids=list(range(NCORES)))
    aln_full = np.concatenate([r["aln"] for r in res.results], axis=0)

    if "ffn" not in _CACHE:
        _CACHE["ffn"] = build_ffn_program()
    in_maps = []
    for c in range(NCORES):
        rows = slice(256 * c, 256 * (c + 1))
        in_maps.append({
            "xm": np.ascontiguousarray(x[:, rows, :]),
            "aln": np.ascontiguousarray(aln_full[:, rows, :]),
            "wf": w["L0_ffn_wf"], "wi": w["L0_ffn_wi"], "wc": w["L0_ffn_wc"],
            "wo": w["L0_ffn_wo"], "uf": w["L0_ffn_uf"], "ui": w["L0_ffn_ui"],
            "uc": w["L0_ffn_uc"], "uo": w["L0_ffn_uo"],
            "bf": w["L0_ffn_bf"], "bi": w["L0_ffn_bi"], "bc": w["L0_ffn_bc"],
            "bo": w["L0_ffn_bo"], "in_c": w["L0_ffn_in_c"],
        })
    res = run_bass_kernel_spmd(_CACHE["ffn"], in_maps,
                               core_ids=list(range(NCORES)))
    y = np.concatenate([r["y"] for r in res.results], axis=1)
    return np.ascontiguousarray(y.astype(np.float32))


# version 23
# speedup vs baseline: 1.1954x; 1.0029x over previous
import numpy as np
import ml_dtypes
import concourse.bass as bass
import concourse.mybir as mybir
from concourse.bass_utils import run_bass_kernel_spmd

F32 = mybir.dt.float32
BF16 = mybir.dt.bfloat16
AF = mybir.ActivationFunctionType
ALU = mybir.AluOpType

NCORES = 8
SEQ = 2048
D = 1024
NKT = SEQ // 128
EPS = 1e-5
MASKVAL = -30000.0

ENGINES = ("sync", "scalar", "vector", "tensor", "gpsimd")
DEBUG_NB = None
DEBUG_STAGE = "full"
SKIP_FFN = False


class Plan:

    def __init__(self, nc, ctx):
        self.nc = nc
        self.ctx = ctx
        self.ops = {e: [] for e in ENGINES}
        self.sems = {}
        self.val = {}

    def sem(self, name):
        if name not in self.sems:
            self.sems[name] = self.ctx.enter_context(self.nc.semaphore(name))
            self.val[name] = 0
        return self.sems[name]

    def op(self, eng, fn, inc=None, inc_amt=1):
        semh = self.sem(inc) if inc else None
        self.ops[eng].append(("op", fn, semh, inc_amt))
        if inc:
            self.val[inc] += inc_amt

    def dma(self, eng, fn, inc):
        self.op(eng, fn, inc=inc, inc_amt=16)

    def wait(self, eng, sem, value=None):
        if value is None:
            value = self.val.get(sem, 0)
        if value > 0:
            self.ops[eng].append(("wait", self.sem(sem), value))

    def v(self, sem):
        return self.val.setdefault(sem, 0)

    def run(self, block):
        def make(name):
            items = self.ops[name]

            def body(eng):
                for it in items:
                    if it[0] == "wait":
                        eng.wait_ge(it[1], it[2])
                    else:
                        inst = it[1](eng)
                        if it[2] is not None:
                            inst.then_inc(it[2], it[3])
            return body
        block.sync(make("sync"))
        block.scalar(make("scalar"))
        block.vector(make("vector"))
        block.tensor(make("tensor"))
        block.gpsimd(make("gpsimd"))




def _apof(buf):
    return buf.ap() if hasattr(buf, "ap") and callable(getattr(buf, "ap")) \
        else buf


def sub3(buf, off, d1, d2):
    a = _apof(buf)
    return bass.AP(a.tensor, a.offset + off, [[a.ap[0][0], a.ap[0][1]], d1, d2])


def sub2(buf, off, d1):
    a = _apof(buf)
    return bass.AP(a.tensor, a.offset + off, [[a.ap[0][0], a.ap[0][1]], d1])


def rowstride(buf):
    return _apof(buf).ap[0][0]



def alloc_common(nc, ctx, need_attn, need_ffn):
    E = ctx.enter_context
    B = {}
    B["PA"] = E(nc.psum_tensor("PA", [128, 2048], F32))
    B["PB"] = E(nc.psum_tensor("PB", [128, 2048], F32))
    B["bnb"] = [E(nc.sbuf_tensor(f"bnb{i}", [128, 12], F32)) for i in range(4)]
    B["mv"] = [E(nc.sbuf_tensor(f"mv{i}", [128, 2], F32)) for i in range(4)]
    B["nm"] = [E(nc.sbuf_tensor(f"nm{i}", [128, 1], F32)) for i in range(4)]
    B["rs"] = [E(nc.sbuf_tensor(f"rs{i}", [128, 1], F32)) for i in range(4)]
    B["sq1"] = [E(nc.sbuf_tensor(f"sq1_{i}", [128, 1], F32)) for i in range(4)]
    B["sq2"] = [E(nc.sbuf_tensor(f"sq2_{i}", [128, 1], F32)) for i in range(4)]
    B["eps"] = E(nc.sbuf_tensor("eps", [128, 1], F32))
    B["xt"] = [E(nc.sbuf_tensor(f"xt{i}", [128, 1024], F32)) for i in range(2)]
    B["lt"] = E(nc.sbuf_tensor("lt", [128, 1024], F32))
    B["identf"] = E(nc.sbuf_tensor("identf", [128, 128], F32))
    B["identb"] = E(nc.sbuf_tensor("identb", [128, 128], BF16))
    B["ones1"] = E(nc.sbuf_tensor("ones1", [1, 128], BF16))
    if need_attn:
        B["lnf"] = E(nc.sbuf_tensor("lnf", [128, 8 * 2048], BF16))
        B["kf"] = E(nc.sbuf_tensor("kf", [128, 8 * 2048], BF16))
        B["va"] = E(nc.sbuf_tensor("va", [128, 16 * 1040], BF16))
        B["vt"] = E(nc.sbuf_tensor("vt", [128, 1024], BF16))
        B["qf"] = E(nc.sbuf_tensor("qf", [128, 2 * 8 * 256], BF16))
        B["ex"] = E(nc.sbuf_tensor("ex", [128, 2048], BF16))
        B["of"] = E(nc.sbuf_tensor("of", [128, 8 * 256], BF16))
        B["al"] = [E(nc.sbuf_tensor(f"al{i}", [128, 1024], BF16)) for i in range(2)]
        B["wq"] = E(nc.sbuf_tensor("wq_sb", [128, 8192], BF16))
        B["wk"] = E(nc.sbuf_tensor("wk_sb", [128, 8192], BF16))
        B["wv"] = E(nc.sbuf_tensor("wv_sb", [128, 8192], BF16))
        B["wu"] = E(nc.sbuf_tensor("wu_sb", [128, 8192], BF16))
        B["bu_row"] = E(nc.sbuf_tensor("bu_row", [1, 1024], BF16))
        B["rb"] = E(nc.sbuf_tensor("rb", [64, 1024], BF16))
    if need_ffn == "alias":
        lnf, kf, va = B["lnf"].ap(), B["kf"].ap(), B["va"].ap()
        B["wfT"] = lnf[:, 0:8192]
        B["wiT"] = lnf[:, 8192:16384]
        B["wcT"] = kf[:, 0:8192]
        B["ufT"] = kf[:, 8192:16384]
        B["uiT"] = B["wq"].ap()[:, :]
        B["ucT"] = B["wk"].ap()[:, :]
        B["woT"] = B["wv"].ap()[:, :]
        B["uoT"] = B["wu"].ap()[:, :]
        B["cb"] = [va[:, 0:2048], va[:, 2048:4096]]
        B["l2"] = [va[:, 4096:6144], va[:, 6144:8192]]
        B["alt"] = [va[:, 8192:9216], va[:, 9216:10240]]
        B["cuo"] = va[:, 10240:12288]
        B["bo_row"] = va[0:1, 12288:13312]
        cap = B["PA"].ap()[:, 0:2048]
        B["c"] = [cap, cap]
        B["c_in_psum"] = True
        B["hh"] = B["PA"].ap()[:, 0:1024]
        ltap = B["lt"].ap()
        B["oo"] = ltap[:, 0:1024]
        B["gates"] = [ltap[:, 0:256], ltap[:, 256:512], ltap[:, 512:768]]
        B["x1"] = E(nc.sbuf_tensor("x1_", [128, 1024], F32))
        B["bcol"] = E(nc.sbuf_tensor("bcol", [128, 24], F32))
        B["inccol"] = E(nc.sbuf_tensor("inccol", [128, 8], F32))
    elif need_ffn:
        for w in ("wfT", "wiT", "wcT", "ufT", "uiT", "ucT", "woT", "uoT"):
            B[w] = E(nc.sbuf_tensor(w, [128, 8192], BF16))
        B["c"] = [E(nc.sbuf_tensor(f"c{i}", [128, 2048], F32)) for i in range(2)]
        B["cb"] = [E(nc.sbuf_tensor(f"cb{i}", [128, 2048], BF16)) for i in range(2)]
        B["l2"] = [E(nc.sbuf_tensor(f"l2{i}", [128, 2048], BF16)) for i in range(2)]
        B["gates"] = [E(nc.sbuf_tensor(f"gate{i}", [128, 256], F32)) for i in range(3)]
        B["x1"] = E(nc.sbuf_tensor("x1_", [128, 1024], F32))
        B["alt"] = [E(nc.sbuf_tensor(f"alt{i}", [128, 1024], BF16)) for i in range(2)]
        B["cuo"] = E(nc.sbuf_tensor("cuo", [128, 2048], BF16))
        B["hh"] = E(nc.sbuf_tensor("hh", [128, 1024], F32))
        B["oo"] = E(nc.sbuf_tensor("oo", [128, 1024], F32))
        B["bcol"] = E(nc.sbuf_tensor("bcol", [128, 24], F32))
        B["inccol"] = E(nc.sbuf_tensor("inccol", [128, 8], F32))
        B["bo_row"] = E(nc.sbuf_tensor("bo_row", [1, 1024], BF16))
    return B


def emit_setup(P, B):
    P.op("gpsimd", lambda g: g.memset(B["eps"][:], EPS))
    P.op("gpsimd", lambda g: g.memset(B["identf"][:], 0.0), inc="gp_setup")
    P.wait("gpsimd", "gp_setup")
    P.op("gpsimd", lambda g: g.affine_select(
        out=B["identf"][:], in_=B["identf"][:], compare_op=ALU.not_equal,
        fill=1.0, base=0, pattern=[[-1, 128]], channel_multiplier=1))
    P.op("gpsimd", lambda g: g.memset(B["ones1"][:], 1.0), inc="gp_setup")
    P.op("gpsimd", lambda g: g.memset(B["identb"][:], 0.0), inc="gp_setup")
    P.wait("gpsimd", "gp_setup")
    P.op("gpsimd", lambda g: g.affine_select(
        out=B["identb"][:], in_=B["identb"][:], compare_op=ALU.not_equal,
        fill=1.0, base=0, pattern=[[-1, 128]], channel_multiplier=1),
        inc="gp_setup")
    if "va" in B:
        P.op("gpsimd", lambda g: g.memset(B["qf"][:], 0.0), inc="gp_setup")
        P.op("gpsimd", lambda g: g.memset(B["va"][:], 0.0), inc="gp_setup")
        P.wait("gpsimd", "gp_setup")
        P.op("gpsimd", lambda g: g.memset(
            sub3(B["va"], 64, [1040, 16], [65, 16]), 1.0),
            inc="gp_setup")
    P.wait("tensor", "gp_setup")
    P.wait("vector", "gp_setup")
    P.wait("scalar", "gp_setup")


def emit_small_ln(P, B, par, src_fn, width=1024):
    bnb, mv, nm, rs = B["bnb"][par], B["mv"][par], B["nm"][par], B["rs"][par]

    def stats(v):
        src = src_fn()
        nh = width // 512
        for i in range(nh - 1):
            v.bn_stats(bnb[:, 6 * i:6 * (i + 1)], src[:, 512 * i:512 * (i + 1)])
        i = nh - 1
        return v.bn_stats(bnb[:, 6 * i:6 * (i + 1)],
                          src[:, 512 * i:512 * (i + 1)])
    P.op("vector", stats, inc="ve")
    P.wait("vector", "ve")
    P.op("vector", lambda v: v.bn_aggr(mv[:], bnb[:, 0:6 * (width // 512)]),
         inc="ve")
    P.wait("vector", "ve")
    P.op("vector", lambda v: v.tensor_scalar(nm[:], mv[:, 0:1], -1.0, None,
                                             ALU.mult), inc="ve")
    P.wait("scalar", "ve")
    P.op("scalar", lambda s: s.activation(rs[:], mv[:, 1:2], AF.Ln,
                                          bias=B["eps"][:, 0:1]), inc="se")
    P.wait("scalar", "se")
    P.op("scalar", lambda s: s.activation(rs[:], rs[:], AF.Exp, scale=-0.5),
         inc="ve_rs")
    return nm, rs


I32 = mybir.dt.int32
RSQRT_K = 0x5f3759df


def emit_rsqrt_dve(P, B, par):
    mv, nm, rs = B["mv"][par], B["nm"][par], B["rs"][par]
    q1, q2 = B["sq1"][par], B["sq2"][par]
    q1i = q1.ap().bitcast(I32)
    q2i = q2.ap().bitcast(I32)
    P.op("vector", lambda v: v.tensor_scalar(nm[:], mv[:, 0:1], -1.0, None,
                                             ALU.mult), inc="ve")
    P.op("vector", lambda v: v.tensor_single_scalar(
        q1[:], mv[:, 1:2], EPS, ALU.add), inc="ve")
    P.wait("vector", "ve")
    P.op("vector", lambda v: v.tensor_single_scalar(
        q2i, q1i, 1, ALU.arith_shift_right), inc="ve")
    P.wait("vector", "ve")
    P.op("vector", lambda v: v.tensor_scalar(
        q2i, q2i, -1, RSQRT_K, ALU.mult, ALU.add), inc="ve")
    for it in range(2):
        dst = q2 if it == 0 else rs
        P.wait("vector", "ve")
        P.op("vector", lambda v: v.tensor_mul(rs[:], q1[:], q2[:]), inc="ve")
        P.wait("vector", "ve")
        P.op("vector", lambda v: v.tensor_mul(rs[:], rs[:], q2[:]), inc="ve")
        P.wait("vector", "ve")
        P.op("vector", lambda v: v.tensor_scalar(
            rs[:], rs[:], -0.5, 1.5, ALU.mult, ALU.add), inc="ve")
        P.wait("vector", "ve")
        P.op("vector", lambda v, dst=dst: v.tensor_mul(dst[:], q2[:], rs[:]),
             inc="ve" if it == 0 else "ve_rs")
    return nm, rs


def emit_small_ln_dve(P, B, par, src_fn, width=1024):
    bnb, mv = B["bnb"][par], B["mv"][par]

    def stats(v):
        src = src_fn()
        nh = width // 512
        for i in range(nh - 1):
            v.bn_stats(bnb[:, 6 * i:6 * (i + 1)], src[:, 512 * i:512 * (i + 1)])
        i = nh - 1
        return v.bn_stats(bnb[:, 6 * i:6 * (i + 1)],
                          src[:, 512 * i:512 * (i + 1)])
    P.op("vector", stats, inc="ve")
    P.wait("vector", "ve")
    P.op("vector", lambda v: v.bn_aggr(mv[:], bnb[:, 0:6 * (width // 512)]),
         inc="ve")
    P.wait("vector", "ve")
    return emit_rsqrt_dve(P, B, par)


def emit_ln_tile_bf(P, B, src_dram_ap, k, dst_fm, dst_col, slots, pend=None):
    nslot = len(slots)
    par4 = k % nslot
    par2 = k % 2
    xt = slots[par4]
    lt = (B["lt"] if par2 == 0 else B["x1"]).ap().bitcast(BF16)[:, 0:1024]
    PA = B["PA"]
    P.wait("sync", "gp_lnapply", max(P.v("gp_lnapply") - (nslot - 1), 0))
    P.dma("sync", lambda sy: sy.dma_start(out=xt, in_=src_dram_ap),
          inc=f"ldx{par4}")
    P.wait("vector", "gp_lnapply", max(P.v("gp_lnapply") - (nslot - 1), 0))
    P.wait("vector", f"ldx{par4}")
    bnb, mv, nm, rs = (B["bnb"][par4], B["mv"][par4], B["nm"][par4],
                       B["rs"][par4])

    def stats(v):
        v.bn_stats(bnb[:, 0:6], xt[:, 0:512])
        return v.bn_stats(bnb[:, 6:12], xt[:, 512:1024])
    P.op("vector", stats, inc="ve")
    P.wait("vector", "ve")
    P.op("vector", lambda v: v.bn_aggr(mv[:], bnb[:]), inc="ve")
    P.wait("vector", "ve")
    P.op("vector", lambda v: v.tensor_scalar(nm[:], mv[:, 0:1], -1.0, None,
                                             ALU.mult), inc="ve")
    P.wait("scalar", "ve")
    P.op("scalar", lambda sc: sc.activation(rs[:], mv[:, 1:2], AF.Ln,
                                            bias=B["eps"][:, 0:1]), inc="se")
    P.wait("scalar", "se")
    P.op("scalar", lambda sc: sc.activation(rs[:], rs[:], AF.Exp, scale=-0.5),
         inc="ve_rs")
    if pend is not None:
        pend()
    P.wait("gpsimd", "ve_rs")
    P.wait("gpsimd", "te_T", max(P.v("te_T") - 1, 0))
    P.op("gpsimd", lambda g, lt=lt: g.tensor_scalar(
        lt, xt, nm[:, 0:1], rs[:, 0:1], ALU.add, ALU.mult),
        inc="gp_lnapply")
    P.wait("tensor", "gp_lnapply")
    P.wait("tensor", "ve_Tevac", max(P.v("ve_Tevac") - 2, 0))
    pslot = k % 4
    pbuf = (PA if pslot < 2 else B["PB"]).ap().bitcast(BF16)
    po = 1024 * (pslot % 2)
    for ch in range(8):
        P.op("tensor", lambda t, ch=ch, po=po, lt=lt, pbuf=pbuf: t.transpose(
            pbuf[:, po + 128 * ch: po + 128 * (ch + 1)],
            lt[:, 128 * ch:128 * (ch + 1)], B["identb"][:]),
            inc="te_T" if ch == 7 else None)

    def evac():
        P.wait("scalar", "te_T")
        stride = dst_fm.shape[1] // 8
        P.op("scalar",
             lambda sc, po=po, dst_col=dst_col, dst_fm=dst_fm, stride=stride,
             pbuf=pbuf:
             sc.copy(
                 sub3(dst_fm, dst_col, [stride, 8], [1, 128]),
                 sub3(pbuf, po, [128, 8], [1, 128])),
             inc="ve_Tevac")
    return evac


def emit_ln_tile(P, B, src_dram_ap, par, dst_fm, dst_col, lt2=False):
    xt = B["xt"][par]
    lt = (B["lt"] if (par == 0 or not lt2) else B["x1"])
    PA = B["PA"]
    P.wait("sync", "gp_lnapply", max(P.v("gp_lnapply") - 1, 0))
    P.dma("sync", lambda sy: sy.dma_start(out=xt[:], in_=src_dram_ap),
          inc=f"ldx{par}")
    P.wait("vector", "gp_lnapply", max(P.v("gp_lnapply") - 1, 0))
    P.wait("vector", f"ldx{par}")
    nm, rs = emit_small_ln(P, B, par, lambda: xt[:])
    P.wait("gpsimd", "ve_rs")
    if lt2:
        P.wait("gpsimd", "te_T", max(P.v("te_T") - 1, 0))
    else:
        P.wait("gpsimd", "te_T")
    P.op("gpsimd", lambda g, lt=lt: g.tensor_scalar(
        lt[:], xt[:], nm[:, 0:1], rs[:, 0:1], ALU.add, ALU.mult),
        inc="gp_lnapply")
    P.wait("tensor", "gp_lnapply")
    P.wait("tensor", "ve_Tevac", max(P.v("ve_Tevac") - 1, 0))
    po = 1024 * par
    for ch in range(8):
        P.op("tensor", lambda t, ch=ch, po=po, lt=lt: t.transpose(
            PA[:, po + 128 * ch: po + 128 * (ch + 1)],
            lt[:, 128 * ch:128 * (ch + 1)], B["identf"][:]),
            inc="te_T" if ch == 7 else None)
    P.wait("scalar", "te_T")
    stride = dst_fm.shape[1] // 8
    P.op("scalar", lambda sc, po=po, dst_col=dst_col, dst_fm=dst_fm, stride=stride:
         sc.copy(
             sub3(dst_fm, dst_col, [stride, 8], [1, 128]),
             sub3(PA, po, [128, 8], [1, 128])),
         inc="ve_Tevac")



def emit_attention(P, B, cfg):
    NB = DEBUG_NB or cfg["NB"]
    units = cfg["units"]
    xg, xm = cfg["xg"], cfg["xm"]
    lnf, kf, va, qf, ex, of = (B["lnf"], B["kf"], B["va"], B["qf"], B["ex"],
                               B["of"])
    wq, wk, wv, wu = B["wq"], B["wk"], B["wv"], B["wu"]
    PA, PB = B["PA"], B["PB"]
    lnm = lnf if cfg["lnm_is_lnf"] else B["lnm"]
    masks = B["masks"]

    for name, arena in (("wq", wq), ("wk", wk), ("wv", wv), ("wu", wu)):
        dram = cfg[name]
        for dmc in range(8):
            P.dma("gpsimd", lambda g, dram=dram, arena=arena, dmc=dmc: g.dma_start(
                out=arena[:, 1024 * dmc:1024 * (dmc + 1)],
                in_=dram[128 * dmc:128 * (dmc + 1), :]), inc="ldw")
    P.dma("gpsimd", lambda g: g.dma_start(out=B["bu_row"][:],
                                          in_=cfg["bu"][None, :]), inc="ldw")
    P.dma("sync", lambda sy: sy.dma_start(out=masks[:], in_=cfg["masks_dram"]),
          inc="ld_mask")
    n_ldw = P.v("ldw")
    P.wait("tensor", "ldw", n_ldw)

    for b in range(NB):
        P.wait("vector", "te_v")
        P.wait("vector", "te_s")
        P.wait("vector", "te_q")
        P.wait("vector", "te_up")
        P.wait("gpsimd", "te_av")
        P.wait("tensor", "ve_al")
        P.wait("tensor", "ve_norm")
        if cfg.get("al_on_ex"):
            P.wait("scalar", "st_al0")
            P.wait("scalar", "st_al1")
            P.wait("vector", "st_al0")
            P.wait("vector", "st_al1")
            P.wait("gpsimd", "st_al0")
            P.wait("gpsimd", "st_al1")
            P.wait("sync", "st_al0")
            P.wait("sync", "st_al1")
        slots = cfg["ln_slots"]
        pends = []
        for tt in range(NKT):
            pd = pends.pop(0) if len(pends) >= 2 else None
            pends.append(emit_ln_tile_bf(
                P, B, xg[b, 128 * tt:128 * (tt + 1), :],
                tt, lnf, 128 * tt, slots, pd))
        if not cfg["lnm_is_lnf"]:
            for tt in range(cfg["NQ"] // 128):
                pd = pends.pop(0) if len(pends) >= 2 else None
                pends.append(emit_ln_tile_bf(
                    P, B, xm[b, 128 * tt:128 * (tt + 1), :],
                    tt, lnm, 128 * tt, slots[0:2], pd))
        for pd in pends:
            pd()
        P.wait("tensor", "ve_Tevac")
        P.wait("vector", "gp_lnapply")
        if DEBUG_STAGE == "ln1":
            continue

        for g4 in range(4):
            for hp in range(8):
                par = (g4 * 8 + hp) % 2
                po = 512 * par
                P.wait("tensor", "ve_kevac", max(P.v("ve_kevac") - 1, 0))
                for dmc in range(8):
                    P.op("tensor", lambda t, dmc=dmc, hp=hp, g4=g4, po=po: t.matmul(
                        PA[:, po:po + 512],
                        wk[:, 1024 * dmc + 128 * hp: 1024 * dmc + 128 * (hp + 1)],
                        lnf[:, 2048 * dmc + 512 * g4: 2048 * dmc + 512 * (g4 + 1)],
                        start=(dmc == 0), stop=(dmc == 7), skip_group_check=True),
                        inc="te_k" if dmc == 7 else None)
                P.wait("vector", "te_k")
                P.op("vector", lambda v, hp=hp, g4=g4, po=po: v.tensor_copy(
                    kf[:, 2048 * hp + 512 * g4: 2048 * hp + 512 * (g4 + 1)],
                    PA[:, po:po + 512]), inc="ve_kevac")

        for kt in range(NKT):
            par = kt % 2
            po = 1024 * par
            P.wait("tensor", "gp_vrepack", max(P.v("gp_vrepack") - 1, 0))
            for dmc in range(8):
                for hf in range(2):
                    P.op("tensor", lambda t, dmc=dmc, kt=kt, hf=hf, po=po: t.matmul(
                        PB[:, po + 512 * hf: po + 512 * (hf + 1)],
                        lnf[:, 2048 * dmc + 128 * kt: 2048 * dmc + 128 * (kt + 1)],
                        wv[:, 1024 * dmc + 512 * hf: 1024 * dmc + 512 * (hf + 1)],
                        start=(dmc == 0), stop=(dmc == 7),
                        skip_group_check=True),
                        inc="te_v" if (dmc == 7 and hf == 1) else None)
            P.wait("vector", "te_v")
            P.op("vector", lambda v, po=po, kt=kt: v.tensor_copy(
                sub3(va, 1040 * kt, [65, 16], [1, 64]),
                sub3(B["PB"], po, [64, 16], [1, 64])),
                inc="gp_vrepack")

        if DEBUG_STAGE == "kv":
            continue
        P.wait("tensor", "ve_kevac")
        P.wait("tensor", "gp_vrepack")
        P.wait("vector", "ld_mask", 16)
        for unit in units:
            q_off = unit["q_off"]
            src_stride = lnm.shape[1] // 8
            P.wait("tensor", "ve_norm")
            for hp in range(8):
                P.wait("tensor", "ve_qevac")
                for dmc in range(8):
                    P.op("tensor", lambda t, hp=hp, dmc=dmc, q_off=q_off,
                         src_stride=src_stride: t.matmul(
                             PB[:, 1792:2048],
                             wq[:, 1024 * dmc + 128 * hp: 1024 * dmc + 128 * (hp + 1)],
                             lnm[:, src_stride * dmc + q_off:
                                 src_stride * dmc + q_off + 256],
                             start=(dmc == 0), stop=(dmc == 7),
                             skip_group_check=True),
                         inc="te_q" if dmc == 7 else None)
                P.wait("vector", "te_q")
                P.op("vector", lambda v, hp=hp: v.tensor_copy(
                    qf[0:64, 256 * hp:256 * (hp + 1)], PB[0:64, 1792:2048]))
                P.op("vector", lambda v, hp=hp: v.tensor_copy(
                    qf[64:128, 2048 + 256 * hp:2048 + 256 * (hp + 1)],
                    PB[64:128, 1792:2048]), inc="ve_qevac")
            P.wait("tensor", "ve_qevac")
            if DEBUG_STAGE == "q":
                continue

            for quad in range(4):
                ktl = unit["ktiles"]
                nkt_u = len(ktl)
                P.wait("tensor", "ve_al")

                exp_base = P.v("se_exp")

                def emit_av(idx, quad=quad, ktl=ktl, nkt_u=nkt_u,
                            exp_base=exp_base):
                    i, vs, d_lo, d_w = ktl[idx]
                    P.wait("tensor", "se_exp", exp_base + idx + 1)
                    if idx == 0:
                        P.wait("tensor", "ve_norm")
                    for hq in range(4):
                        h = 4 * quad + hq
                        P.op("tensor", lambda t, i=i, vs=vs, hq=hq, h=h, idx=idx,
                             fb=(idx == 0 and hq % 2 == 0),
                             last=(idx == nkt_u - 1): t.matmul(
                                 PB[0:65, 256 * hq + vs: 256 * (hq + 1)],
                                 va[:, 1040 * i + 65 * h: 1040 * i + 65 * (h + 1)],
                                 sub2(ex, 1024 * (idx % 2) + 256 * hq + vs,
                                      [1, 256 - vs]),
                                 start=fb, stop=last, skip_group_check=True),
                             inc="te_av" if hq == 3 else None)

                for idx, (i, vs, d_lo, d_w) in enumerate(ktl):
                    slot = 1024 * (idx % 2)
                    P.wait("tensor", "se_exp", max(P.v("se_exp") - 1, 0))
                    for hq in range(4):
                        h = 4 * quad + hq
                        hp = h // 2
                        qo = 2048 * (h % 2) + 256 * hp
                        P.op("tensor",
                             lambda t, i=i, vs=vs, hq=hq, hp=hp, qo=qo, slot=slot: t.matmul(
                                 PA[:, slot + 256 * hq + vs: slot + 256 * (hq + 1)],
                                 kf[:, 2048 * hp + 128 * i: 2048 * hp + 128 * (i + 1)],
                                 qf[:, qo + vs: qo + 256],
                                 start=(hq % 2 == 0), stop=False,
                                 skip_group_check=True),
                             inc="te_s" if hq == 3 else None)
                    if d_w:
                        for hq in range(4):
                            P.op("tensor", lambda t, hq=hq, d_lo=d_lo, d_w=d_w,
                                 slot=slot: t.matmul(
                                     PA[:, slot + 256 * hq + d_lo:
                                        slot + 256 * hq + d_lo + d_w],
                                     B["identb"][:],
                                     masks[:, 0:d_w],
                                     start=False, stop=True,
                                     skip_group_check=True),
                                 inc="te_s" if hq == 3 else None)
                    P.wait("scalar", "te_s")
                    P.wait("scalar", "te_av")
                    P.op("scalar", lambda s, vs=vs, slot=slot, idx=idx: s.activation(
                        sub3(ex, 1024 * (idx % 2) + vs, [256, 4], [1, 256 - vs]),
                        sub3(PA, slot + vs, [256, 4], [1, 256 - vs]),
                        AF.Exp, scale=1.0 / 64.0), inc="se_exp")
                    if idx > 0:
                        emit_av(idx - 1)
                emit_av(nkt_u - 1)
                P.wait("scalar", "te_av")
                P.op("scalar", lambda s: s.activation(
                    PB[64:65, 1024:2048], PB[64:65, 0:1024], AF.Ln), inc="se_ln")
                P.wait("scalar", "se_ln")
                P.wait("scalar", "ve_norm")
                P.op("scalar", lambda s: s.activation(
                    B["rb"][0:1, 0:1024], PB[64:65, 1024:2048], AF.Exp,
                    scale=-1.0), inc="se_r2")
                P.wait("tensor", "se_r2")
                P.wait("tensor", "ve_bc")
                for j in range(2):
                    P.op("tensor", lambda t, j=j: t.matmul(
                        PB[0:64, 1024 + 512 * j:1024 + 512 * (j + 1)],
                        B["ones1"][0:1, 0:64],
                        B["rb"][0:1, 512 * j:512 * (j + 1)],
                        start=True, stop=True, skip_group_check=True),
                        inc="te_bc" if j == 1 else None)
                P.wait("vector", "te_bc")
                P.op("vector", lambda v: v.tensor_copy(
                    B["rb"][0:64, 0:1024], PB[0:64, 1024:2048]), inc="ve_bc")
                P.wait("vector", "ve_bc")
                P.wait("vector", "te_up")
                for hq in range(4):
                    h = 4 * quad + hq
                    hp, hlo = h // 2, 64 * (h % 2)
                    P.op("vector", lambda v, hq=hq, hp=hp, hlo=hlo: v.tensor_mul(
                        of[hlo:hlo + 64, 256 * hp:256 * (hp + 1)],
                        PB[0:64, 256 * hq:256 * (hq + 1)],
                        B["rb"][0:64, 256 * hq:256 * (hq + 1)]),
                        inc="ve_norm")
            if DEBUG_STAGE == "norm":
                continue
            P.wait("tensor", "ve_norm")
            for st in range(2):
                po = 1024 * st
                for dc in range(8):
                    for j in range(2):
                        P.op("tensor", lambda t, dc=dc, j=j, st=st, po=po: t.matmul(
                            PA[:, po + 512 * j: po + 512 * (j + 1)],
                            of[:, 256 * dc + 128 * st: 256 * dc + 128 * (st + 1)],
                            wu[:, 1024 * dc + 512 * j: 1024 * dc + 512 * (j + 1)],
                            start=(dc == 0), stop=False,
                            skip_group_check=True))
                for j in range(2):
                    P.op("tensor", lambda t, j=j, po=po: t.matmul(
                        PA[:, po + 512 * j: po + 512 * (j + 1)],
                        B["ones1"][0:1, :],
                        B["bu_row"][0:1, 512 * j:512 * (j + 1)],
                        start=False, stop=True, skip_group_check=True),
                        inc="te_up" if j == 1 else None)
                P.wait("vector", "te_up")
                P.wait("vector", f"st_al{st}")
                nm, rs = emit_small_ln(P, B, st,
                                       lambda po=po: PA[:, po:po + 1024])
                P.wait("vector", "ve_rs")
                P.op("vector", lambda v, st=st, po=po, nm=nm, rs=rs: v.tensor_scalar(
                    B["al"][st][:], PA[:, po:po + 1024], nm[:, 0:1], rs[:, 0:1],
                    ALU.add, ALU.mult), inc="ve_al")
                P.wait("sync", "ve_al")
                dst = cfg["aln_dst"](b, unit["row0"] + 128 * st)
                P.dma("sync", lambda sy, dst=dst, st=st: sy.dma_start(
                    out=dst, in_=B["al"][st][:]), inc=f"st_al{st}")
    P.wait("sync", "st_al0")
    P.wait("sync", "st_al1")



def emit_ffn(P, B, cfg):
    NT = 16
    PA, PB = B["PA"], B["PB"]
    xm = cfg["xm_ffn"]
    identf = B["identf"]

    wlist = [("wf", "wfT"), ("wi", "wiT"), ("wc", "wcT"), ("uf", "ufT"),
             ("ui", "uiT"), ("uc", "ucT"), ("wo", "woT"), ("uo", "uoT")]
    for src_name, dst_name in wlist:
        dram = cfg[src_name]
        dst = B[dst_name]
        for rnd in range(4):
            P.wait("sync", "te_wT")
            for oc in range(2):
                P.dma("sync", lambda sy, dram=dram, oc=oc, rnd=rnd:
                      sy.dma_start(
                          out=B["xt"][oc][:],
                          in_=dram[256 * rnd + 128 * oc:256 * rnd + 128 * (oc + 1), :]),
                      inc="ld_ws")
            P.wait("tensor", "ld_ws")
            for ic in range(8):
                P.wait("tensor", "ve_wevac")
                for oc in range(2):
                    P.op("tensor", lambda t, ic=ic, oc=oc: t.transpose(
                        PA[:, 128 * oc:128 * (oc + 1)],
                        B["xt"][oc][:, 128 * ic: 128 * (ic + 1)],
                        identf[:]),
                        inc="te_wT" if oc == 1 else None)
                P.wait("vector", "te_wT")
                P.op("vector", lambda v, ic=ic, rnd=rnd, dst=dst: v.tensor_copy(
                    dst[:, 1024 * ic + 256 * rnd: 1024 * ic + 256 * rnd + 256],
                    PA[:, 0:256]), inc="ve_wevac")
    P.wait("tensor", "ve_wevac")

    for gi, bname in enumerate(("bf", "bi", "bc")):
        for mc in range(8):
            P.dma("sync", lambda sy, bname=bname, gi=gi, mc=mc: sy.dma_start(
                out=B["bcol"][:, 8 * gi + mc: 8 * gi + mc + 1],
                in_=cfg[bname][128 * mc:128 * (mc + 1), None]), inc="ldb")
    for mc in range(8):
        P.dma("sync", lambda sy, mc=mc: sy.dma_start(
            out=B["inccol"][:, mc:mc + 1],
            in_=cfg["in_c"][128 * mc:128 * (mc + 1), None]), inc="ldb")
    P.dma("gpsimd", lambda g: g.dma_start(out=B["bo_row"][:],
                                          in_=cfg["bo"][None, :]), inc="ldw2")
    P.wait("scalar", "ldb")
    P.wait("scalar", "ldw2")

    P.wait("sync", "te_wT")

    def b1_spill(t):
        P.wait("sync", "ve_l2")
        P.dma("sync", lambda sy, t=t: sy.dma_start(
            out=cfg["ln2_dram"][t], in_=B["l2"][0][:]), inc="st_l2")

    pend = None
    spill_after = None
    for t in range(NT):
        for st in range(2):
            par = st
            P.wait("sync", "ve_x1", max(P.v("ve_x1") - 1, 0))
            P.dma("sync", lambda sy, t=t, st=st, par=par: sy.dma_start(
                out=B["xt"][par][:], in_=xm[t, 128 * st:128 * (st + 1), :]),
                inc=f"ldx{par}")
            P.dma("sync", lambda sy, t=t, st=st, par=par: sy.dma_start(
                out=B["alt"][par][:], in_=cfg["aln_src"](t, 128 * st)),
                inc=f"ldx{par}")
            P.wait("vector", f"ldx{par}")
            P.wait("vector", "gp_lnapply")
            P.op("vector", lambda v, par=par: v.tensor_add(
                B["x1"][:], B["xt"][par][:], B["alt"][par][:]), inc="ve_x1")
            P.wait("vector", "ve_x1")
            nm, rs = emit_small_ln(P, B, par, lambda: B["x1"][:])
            if pend is not None:
                pend()
            P.wait("gpsimd", "ve_rs")
            P.wait("gpsimd", "te_T")
            P.op("gpsimd", lambda g, nm=nm, rs=rs: g.tensor_scalar(
                B["lt"][:], B["x1"][:], nm[:, 0:1], rs[:, 0:1],
                ALU.add, ALU.mult), inc="gp_lnapply")
            P.wait("tensor", "gp_lnapply")
            P.wait("tensor", "ve_l2", max(P.v("ve_l2") - 1, 0))
            po = 1024 * par
            for ch in range(8):
                P.op("tensor", lambda t_, ch=ch, po=po: t_.transpose(
                    PA[:, po + 128 * ch:po + 128 * (ch + 1)],
                    B["lt"][:, 128 * ch:128 * (ch + 1)], identf[:]),
                    inc="te_T" if ch == 7 else None)

            def evac(po=po, st=st, t=t):
                P.wait("scalar", "te_T")
                P.wait("scalar", "st_l2", 16 * t)
                P.op("scalar", lambda sc, po=po, st=st: sc.copy(
                    sub3(B["l2"][0], 128 * st, [256, 8], [1, 128]),
                    sub3(PA, po, [128, 8], [1, 128])),
                    inc="ve_l2")
            pend = evac
            if st == 0 and spill_after is not None:
                b1_spill(spill_after)
                spill_after = None
        spill_after = t
    if pend is not None:
        pend()
    b1_spill(spill_after)

    P.wait("scalar", "ve_x1")
    P.wait("scalar", "ve_l2")
    for mc in range(8):
        P.op("scalar", lambda s, mc=mc: s.activation(
            B["c"][0][:, 256 * mc:256 * (mc + 1)], B["x1"][:, 0:256],
            AF.Identity, bias=B["inccol"][:, mc:mc + 1], scale=0.0),
            inc="se_c0")
    ceng0 = "vector" if B.get("c_in_psum") else "gpsimd"
    P.wait(ceng0, "se_c0")
    P.op(ceng0, lambda g: g.tensor_copy(B["cb"][0][:], B["c"][0][:]),
         inc="gp_cb")
    P.wait("vector", "se_c0")

    P.wait("sync", "st_l2")
    for t in range(NT):
        par = t % 2
        P.wait("sync", "te_g", max(P.v("te_g") - 24, 0))
        P.dma("sync", lambda sy, t=t, par=par: sy.dma_start(
            out=B["l2"][par][:], in_=cfg["ln2_dram"][t]), inc=f"ldl2_{par}")
        P.wait("tensor", f"ldl2_{par}")
        for mc in range(8):
            pgo = 1024 * (mc % 2)
            P.wait("tensor", "se_g", max(P.v("se_g") - 3, 0))
            for gi, wT, uT in ((0, "wfT", "ufT"), (1, "wiT", "uiT"),
                               (2, "wcT", "ucT")):
                doff = pgo + 256 * gi if gi < 2 else pgo + 512
                for ic in range(8):
                    P.op("tensor", lambda t_, wT=wT, ic=ic, mc=mc, doff=doff,
                         par=par, first=(ic == 0 and gi in (0, 2)): t_.matmul(
                             PB[:, doff:doff + 256],
                             B[wT][:, 1024 * ic + 128 * mc: 1024 * ic + 128 * (mc + 1)],
                             B["l2"][par][:, 256 * ic:256 * (ic + 1)],
                             start=first, stop=False, skip_group_check=True))
            P.wait("tensor", "gp_cb", 8 * t + 1)
            for gi, wT, uT in ((0, "wfT", "ufT"), (1, "wiT", "uiT"),
                               (2, "wcT", "ucT")):
                doff = pgo + 256 * gi if gi < 2 else pgo + 512
                for jc in range(8):
                    P.op("tensor", lambda t_, uT=uT, jc=jc, mc=mc, doff=doff,
                         par=par: t_.matmul(
                             PB[:, doff:doff + 256],
                             B[uT][:, 1024 * jc + 128 * mc: 1024 * jc + 128 * (mc + 1)],
                             B["cb"][par][:, 256 * jc:256 * (jc + 1)],
                             start=False, stop=(jc == 7), skip_group_check=True),
                         inc="te_g" if jc == 7 else None)
            P.wait("scalar", "te_g")
            P.wait("scalar", "ve_c")
            for gi, func in ((0, AF.Sigmoid), (1, AF.Sigmoid), (2, AF.Tanh)):
                doff = pgo + 256 * gi if gi < 2 else pgo + 512
                P.op("scalar", lambda s, gi=gi, mc=mc, doff=doff, func=func:
                     s.activation(
                         B["gates"][gi][:], PB[:, doff:doff + 256], func,
                         bias=B["bcol"][:, 8 * gi + mc:8 * gi + mc + 1]),
                     inc="se_g")
            P.wait("vector", "se_g")
            P.wait("vector", "ve_c", max(P.v("ve_c") - 7, 0))
            cs = slice(256 * mc, 256 * (mc + 1))
            P.op("vector", lambda v, cs=cs, par=par: v.tensor_mul(
                B["c"][1 - par][:, cs], B["gates"][0][:], B["c"][par][:, cs]),
                inc="ve_cu")
            P.wait("vector", "ve_cu")
            P.op("vector", lambda v: v.tensor_mul(
                B["gates"][0][:], B["gates"][1][:], B["gates"][2][:]),
                inc="ve_cu")
            P.wait("vector", "ve_cu")
            P.op("vector", lambda v, cs=cs, par=par: v.tensor_add(
                B["c"][1 - par][:, cs], B["c"][1 - par][:, cs],
                B["gates"][0][:]), inc="ve_c")
            ceng = "vector" if B.get("c_in_psum") else "gpsimd"
            P.wait(ceng, "ve_c")
            P.op(ceng, lambda g, cs=cs, par=par: g.tensor_copy(
                B["cb"][1 - par][:, cs], B["c"][1 - par][:, cs]), inc="gp_cb")

    cpar = NT % 2
    P.wait("tensor", "gp_cb", 8 * NT + 1)

    for st in range(2):
        po = 1024 * st
        for mc in range(8):
            for j in range(2):
                P.op("tensor", lambda t, mc=mc, j=j, st=st, po=po: t.matmul(
                    PA[:, po + 512 * j:po + 512 * (j + 1)],
                    B["cb"][cpar][:, 256 * mc + 128 * st: 256 * mc + 128 * (st + 1)],
                    B["uoT"][:, 1024 * mc + 512 * j: 1024 * mc + 512 * (j + 1)],
                    start=(mc == 0), stop=False, skip_group_check=True))
        for j in range(2):
            P.op("tensor", lambda t, j=j, po=po: t.matmul(
                PA[:, po + 512 * j:po + 512 * (j + 1)],
                B["ones1"][0:1, :], B["bo_row"][0:1, 512 * j:512 * (j + 1)],
                start=False, stop=True, skip_group_check=True),
                inc="te_cuo" if j == 1 else None)
    P.wait("vector", "te_cuo")
    P.op("vector", lambda v: v.tensor_copy(B["cuo"][:], PA[:]), inc="ve_cuo")

    P.wait("tensor", "ve_cuo")
    P.wait("vector", "ve_cuo")
    P.wait("scalar", "ve_cuo")
    P.wait("sync", "te_g")
    PAap = B["PA"].ap()
    hhs = [PAap[:, 0:1024], PAap[:, 1024:2048]]
    oos = [B["oo"], B["x1"].ap()[:, 0:1024]]
    l2base = [P.v("ldl2_0"), P.v("ldl2_1")]
    P.dma("sync", lambda sy: sy.dma_start(
        out=B["l2"][0][:], in_=cfg["ln2_dram"][0]), inc="ldl2_0")
    for t in range(NT):
        par = t % 2
        if t + 1 < NT:
            P.wait("sync", "te_xwo", max(P.v("te_xwo") - 2, 0))
            P.dma("sync", lambda sy, t=t: sy.dma_start(
                out=B["l2"][(t + 1) % 2][:], in_=cfg["ln2_dram"][t + 1]),
                inc=f"ldl2_{(t + 1) % 2}")
        P.wait("tensor", f"ldl2_{par}", l2base[par] + 16 * (t // 2 + 1))
        for st in range(2):
            po = 1024 * st
            P.wait("tensor", "se_h", max(P.v("se_h") - (1 - st), 0))
            for j in range(2):
                P.op("tensor", lambda t_, j=j, st=st, po=po: t_.matmul(
                    PB[:, po + 512 * j:po + 512 * (j + 1)],
                    B["identb"][:],
                    B["cuo"][:, 1024 * st + 512 * j:1024 * st + 512 * (j + 1)],
                    start=True, stop=False, skip_group_check=True))
            for ic in range(8):
                for j in range(2):
                    P.op("tensor", lambda t_, ic=ic, j=j, st=st, po=po, par=par:
                         t_.matmul(
                             PB[:, po + 512 * j:po + 512 * (j + 1)],
                             B["l2"][par][:, 256 * ic + 128 * st:
                                           256 * ic + 128 * (st + 1)],
                             B["woT"][:, 1024 * ic + 512 * j:
                                      1024 * ic + 512 * (j + 1)],
                             start=False, stop=(ic == 7),
                             skip_group_check=True),
                         inc="te_xwo" if (ic == 7 and j == 1) else None)
        for st in range(2):
            po = 1024 * st
            P.wait("scalar", "te_xwo", P.v("te_xwo") - (1 - st))
            P.wait("scalar", "ve_oc", max(P.v("ve_oc") - (1 - st), 0))
            P.op("scalar", lambda s, po=po, st=st: s.activation(
                hhs[st], PB[:, po:po + 1024], AF.Tanh), inc="se_h")
        for st in range(2):
            P.wait("sync", "ve_o", max(P.v("ve_o") - (1 - st), 0))
            P.dma("sync", lambda sy, t=t, st=st: sy.dma_start(
                out=B["alt"][st][:], in_=cfg["aln_src"](t, 128 * st)),
                inc=f"ldo{st}")
            P.dma("sync", lambda sy, t=t, st=st: sy.dma_start(
                out=B["xt"][st][:], in_=xm[t, 128 * st:128 * (st + 1), :]),
                inc=f"ldo{st}")
        quad = B["bnb"][2].ap()
        scr = B["bnb"][3].ap()
        nm2, q1 = quad[:, 4:6], quad[:, 6:8]
        q2, rr2 = scr[:, 0:2], scr[:, 2:4]
        means = bass.AP(quad.tensor, quad.offset, [list(quad.ap[0]), [2, 2]])
        varis = bass.AP(quad.tensor, quad.offset + 1, [list(quad.ap[0]), [2, 2]])
        q1i = q1.bitcast(I32)
        q2i = q2.bitcast(I32)
        for st in range(2):
            P.wait("vector", "se_h", P.v("se_h") - (1 - st))
            bnb = B["bnb"][st]
            P.op("vector", lambda v, st=st, bnb=bnb: (
                v.bn_stats(bnb[:, 0:6], hhs[st][:, 0:512]),
                v.bn_stats(bnb[:, 6:12], hhs[st][:, 512:1024]))[-1], inc="ve")
            P.wait("vector", "ve")
            P.op("vector", lambda v, st=st, bnb=bnb: v.bn_aggr(
                quad[:, 2 * st:2 * st + 2], bnb[:]), inc="ve")
        P.wait("vector", "ve")
        P.op("vector", lambda v: v.tensor_scalar(
            nm2, means, -1.0, None, ALU.mult), inc="ve")
        P.op("vector", lambda v: v.tensor_single_scalar(
            q1, varis, EPS, ALU.add), inc="ve")
        P.wait("vector", "ve")
        P.op("vector", lambda v: v.tensor_single_scalar(
            q2i, q1i, 1, ALU.arith_shift_right), inc="ve")
        P.wait("vector", "ve")
        P.op("vector", lambda v: v.tensor_scalar(
            q2i, q2i, -1, RSQRT_K, ALU.mult, ALU.add), inc="ve")
        for it in range(2):
            dst = q2 if it == 0 else rr2
            P.wait("vector", "ve")
            P.op("vector", lambda v: v.tensor_mul(rr2, q1, q2), inc="ve")
            P.wait("vector", "ve")
            P.op("vector", lambda v: v.tensor_mul(rr2, rr2, q2), inc="ve")
            P.wait("vector", "ve")
            P.op("vector", lambda v: v.tensor_scalar(
                rr2, rr2, -0.5, 1.5, ALU.mult, ALU.add), inc="ve")
            P.wait("vector", "ve")
            P.op("vector", lambda v, dst=dst: v.tensor_mul(dst, q2, rr2),
                 inc="ve" if it == 0 else "ve_rs")
        nmrs = [(nm2[:, 0:1], rr2[:, 0:1]), (nm2[:, 1:2], rr2[:, 1:2])]
        for st in range(2):
            P.wait("gpsimd", f"ldo{st}")
            P.wait("gpsimd", "ve_o", max(P.v("ve_o") - (1 - st), 0))
            P.op("gpsimd", lambda g, st=st: g.tensor_add(
                B["xt"][st][:], B["xt"][st][:], B["alt"][st][:]),
                inc="gp_ax")
        for st in range(2):
            nm, rs = nmrs[st]
            P.wait("vector", f"st_y{st}")
            P.wait("vector", "ve_rs")
            P.op("vector", lambda v, nm=nm, rs=rs, st=st: v.tensor_scalar(
                oos[st][:], hhs[st], nm[:, 0:1], rs[:, 0:1],
                ALU.add, ALU.mult), inc="ve_oc")
            P.wait("vector", "ve_oc")
            P.wait("vector", "gp_ax", P.v("gp_ax") - (1 - st))
            P.op("vector", lambda v, st=st: v.tensor_add(
                oos[st][:], oos[st][:], B["xt"][st][:]), inc="ve_o")
            P.wait("sync", "ve_o")
            P.dma("sync", lambda sy, t=t, st=st: sy.dma_start(
                out=cfg["y"][t, 128 * st:128 * (st + 1), :], in_=oos[st][:]),
                inc=f"st_y{st}")
    P.wait("sync", "st_y0")
    P.wait("sync", "st_y1")



def make_units(version):
    if version == "fused":
        ktiles = [(i, 16 * i, 16 * i, 16) for i in range(16)]
        return [{"q_off": 0, "row0": 0, "ktiles": ktiles}]
    units = []
    for u in range(8):
        kt = [(i, 0, 0, 0) for i in range(2 * u)]
        kt.append((2 * u, 0, 0, 128))
        kt.append((2 * u + 1, 128, 128, 128))
        units.append({"q_off": 256 * u, "row0": 256 * u, "ktiles": kt})
    return units


def build_attn_program(version):
    from contextlib import ExitStack
    nc = bass.Bass("TRN2", target_bir_lowering=False)
    ctx = ExitStack()
    NB = 2 if version == "unfused" else 16
    NQ = 2048 if version == "unfused" else 256
    mask_w = 128 if version == "unfused" else 16

    xg = nc.dram_tensor("xg", [NB, SEQ, D], F32, kind="ExternalInput").ap()
    wq = nc.dram_tensor("wq", [D, D], F32, kind="ExternalInput").ap()
    wk = nc.dram_tensor("wk", [D, D], F32, kind="ExternalInput").ap()
    wv = nc.dram_tensor("wv", [D, D], F32, kind="ExternalInput").ap()
    wu = nc.dram_tensor("wu", [D, D], F32, kind="ExternalInput").ap()
    bu = nc.dram_tensor("bu", [D], F32, kind="ExternalInput").ap()
    masks_dram = nc.dram_tensor("masks", [128, 8 * mask_w], BF16,
                                kind="ExternalInput").ap()
    aln = nc.dram_tensor("aln", [NB, NQ, D], BF16, kind="ExternalOutput").ap()

    with ctx:
        B = alloc_common(nc, ctx, need_attn=True, need_ffn=False)
        B["masks"] = ctx.enter_context(
            nc.sbuf_tensor("masks_sb", [128, 8 * mask_w], BF16))
        P = Plan(nc, ctx)
        blk = ctx.enter_context(nc.Block())
        cfg = {
            "NB": NB, "NQ": NQ, "units": make_units(version),
            "xg": xg, "xm": None, "lnm_is_lnf": True,
            "wq": wq, "wk": wk, "wv": wv, "wu": wu, "bu": bu,
            "masks_dram": masks_dram, "mask_w": mask_w,
            "aln_dst": lambda b, r0: aln[b, r0:r0 + 128, :],
        }
        emit_setup(P, B)
        emit_attention(P, B, cfg)
        P.run(blk)
    return nc


def build_ffn_program():
    from contextlib import ExitStack
    nc = bass.Bass("TRN2", target_bir_lowering=False)
    ctx = ExitStack()
    NT = 16
    xm = nc.dram_tensor("xm", [NT, 256, D], F32, kind="ExternalInput").ap()
    aln = nc.dram_tensor("aln", [NT, 256, D], BF16, kind="ExternalInput").ap()
    drams = {n: nc.dram_tensor(n, [D, D], F32, kind="ExternalInput").ap()
             for n in ("wf", "wi", "wc", "wo", "uf", "ui", "uc", "uo")}
    bias = {n: nc.dram_tensor(n, [D], F32, kind="ExternalInput").ap()
            for n in ("bf", "bi", "bc", "bo", "in_c")}
    y = nc.dram_tensor("y", [NT, 256, D], F32, kind="ExternalOutput").ap()
    ln2_dram = nc.dram_tensor("ln2s", [NT, 128, 2048], BF16, kind="Internal")

    with ctx:
        B = alloc_common(nc, ctx, need_attn=False, need_ffn=True)
        P = Plan(nc, ctx)
        blk = ctx.enter_context(nc.Block())
        cfg = dict(drams)
        cfg.update(bias)
        cfg.update({
            "xm_ffn": xm, "y": y,
            "aln_src": lambda t, r0: aln[t, r0:r0 + 128, :],
            "ln2_dram": ln2_dram.ap(),
        })
        emit_setup(P, B)
        emit_ffn(P, B, cfg)
        P.run(blk)
    return nc


def build_fused_program():
    from contextlib import ExitStack
    nc = bass.Bass("TRN2", target_bir_lowering=False)
    ctx = ExitStack()
    NB, NQ, mask_w, NT = 16, 256, 16, 16

    xg = nc.dram_tensor("xg", [NB, SEQ, D], F32, kind="ExternalInput").ap()
    xm = nc.dram_tensor("xm", [NT, 256, D], F32, kind="ExternalInput").ap()
    attnw = {n: nc.dram_tensor(n, [D, D], F32, kind="ExternalInput").ap()
             for n in ("wq", "wk", "wv", "wu")}
    bu = nc.dram_tensor("bu", [D], F32, kind="ExternalInput").ap()
    ffnw = {n: nc.dram_tensor(n, [D, D], F32, kind="ExternalInput").ap()
            for n in ("wf", "wi", "wc", "wo", "uf", "ui", "uc", "uo")}
    bias = {n: nc.dram_tensor(n, [D], F32, kind="ExternalInput").ap()
            for n in ("bf", "bi", "bc", "bo", "in_c")}
    masks_dram = nc.dram_tensor("masks", [128, 8 * mask_w], BF16,
                                kind="ExternalInput").ap()
    y = nc.dram_tensor("y", [NT, 256, D], F32, kind="ExternalOutput").ap()
    aln_dram = nc.dram_tensor("alns", [NB, NQ, D], BF16, kind="Internal")
    ln2_dram = nc.dram_tensor("ln2s", [NT, 128, 2048], BF16, kind="Internal")

    with ctx:
        B = alloc_common(nc, ctx, need_attn=True, need_ffn="alias")
        B["lnm"] = B["of"].ap()[:, :]
        B["masks"] = ctx.enter_context(
            nc.sbuf_tensor("masks_sb", [128, 8 * mask_w], BF16))
        exap = B["ex"].ap()
        B["al"] = [exap[:, 0:1024], exap[:, 1024:2048]]
        B["vt"] = exap[:, 0:1024]
        ln_slots = [B["xt"][0].ap()[:, :], B["xt"][1].ap()[:, :],
                    B["ex"].ap().bitcast(F32),
                    B["of"].ap().bitcast(F32)]
        P = Plan(nc, ctx)
        blk = ctx.enter_context(nc.Block())
        cfg_a = {
            "NB": NB, "NQ": NQ, "units": make_units("fused"),
            "xg": xg, "xm": xm, "lnm_is_lnf": False,
            "wq": attnw["wq"], "wk": attnw["wk"], "wv": attnw["wv"],
            "wu": attnw["wu"], "bu": bu,
            "masks_dram": masks_dram, "mask_w": mask_w,
            "aln_dst": lambda b, r0: aln_dram.ap()[b, r0:r0 + 128, :],
            "al_on_ex": True, "ln_slots": ln_slots,
        }
        cfg_f = dict(ffnw)
        cfg_f.update(bias)
        cfg_f.update({
            "xm_ffn": xm, "y": y,
            "aln_src": lambda t, r0: aln_dram.ap()[t, r0:r0 + 128, :],
            "ln2_dram": ln2_dram.ap(),
        })
        emit_setup(P, B)
        emit_attention(P, B, cfg_a)
        for eng in ENGINES:
            P.wait(eng, "st_al0")
            P.wait(eng, "st_al1")
        if not SKIP_FFN:
            emit_ffn(P, B, cfg_f)
        P.run(blk)
    return nc


def rows_for_core(c):
    return np.array([16 * (8 * k + c) + r for k in range(16)
                     for r in range(16)], dtype=np.int64)


def make_masks(version, core):
    mask_w = 128 if version == "unfused" else 16
    rk = np.arange(128)[:, None]
    jj = np.arange(mask_w)[None, :]
    if version == "unfused":
        m = np.where(rk <= jj, 0.0, MASKVAL)
    else:
        m = np.where(rk <= 16 * core + jj, 0.0, MASKVAL)
    return np.ascontiguousarray(np.tile(m, (1, 8)).astype(ml_dtypes.bfloat16))


_CACHE = {}


def fused_in_map(x, w, c):
    rows = rows_for_core(c)
    return {
        "xg": x,
        "xm": np.ascontiguousarray(x[:, rows, :]),
        "wq": w["L0_attn_w_q"], "wk": w["L0_attn_w_k"],
        "wv": w["L0_attn_w_v"], "wu": w["L0_attn_w_u"],
        "bu": w["L0_attn_b_u"],
        "wf": w["L0_ffn_wf"], "wi": w["L0_ffn_wi"], "wc": w["L0_ffn_wc"],
        "wo": w["L0_ffn_wo"], "uf": w["L0_ffn_uf"], "ui": w["L0_ffn_ui"],
        "uc": w["L0_ffn_uc"], "uo": w["L0_ffn_uo"],
        "bf": w["L0_ffn_bf"], "bi": w["L0_ffn_bi"], "bc": w["L0_ffn_bc"],
        "bo": w["L0_ffn_bo"], "in_c": w["L0_ffn_in_c"],
        "masks": make_masks("fused", c),
    }


def kernel_fused(**inp):
    x = np.ascontiguousarray(inp["L0_attn_x"])
    if "fused" not in _CACHE:
        _CACHE["fused"] = build_fused_program()
    in_maps = [fused_in_map(x, inp, c) for c in range(NCORES)]
    res = run_bass_kernel_spmd(_CACHE["fused"], in_maps,
                               core_ids=list(range(NCORES)))
    out = np.empty((16, SEQ, D), dtype=np.float32)
    for c in range(NCORES):
        out[:, rows_for_core(c), :] = np.asarray(res.results[c]["y"],
                                                 dtype=np.float32)
    return out


USE_FUSED = True


def kernel(**inp):
    if USE_FUSED:
        return kernel_fused(**inp)
    x = np.ascontiguousarray(inp["L0_attn_x"])
    w = inp

    version = "unfused"
    if "attn" not in _CACHE:
        _CACHE["attn"] = build_attn_program(version)
    in_maps = []
    for c in range(NCORES):
        in_maps.append({
            "xg": x[2 * c:2 * c + 2],
            "wq": w["L0_attn_w_q"], "wk": w["L0_attn_w_k"],
            "wv": w["L0_attn_w_v"], "wu": w["L0_attn_w_u"],
            "bu": w["L0_attn_b_u"],
            "masks": make_masks(version, c),
        })
    res = run_bass_kernel_spmd(_CACHE["attn"], in_maps,
                               core_ids=list(range(NCORES)))
    aln_full = np.concatenate([r["aln"] for r in res.results], axis=0)

    if "ffn" not in _CACHE:
        _CACHE["ffn"] = build_ffn_program()
    in_maps = []
    for c in range(NCORES):
        rows = slice(256 * c, 256 * (c + 1))
        in_maps.append({
            "xm": np.ascontiguousarray(x[:, rows, :]),
            "aln": np.ascontiguousarray(aln_full[:, rows, :]),
            "wf": w["L0_ffn_wf"], "wi": w["L0_ffn_wi"], "wc": w["L0_ffn_wc"],
            "wo": w["L0_ffn_wo"], "uf": w["L0_ffn_uf"], "ui": w["L0_ffn_ui"],
            "uc": w["L0_ffn_uc"], "uo": w["L0_ffn_uo"],
            "bf": w["L0_ffn_bf"], "bi": w["L0_ffn_bi"], "bc": w["L0_ffn_bc"],
            "bo": w["L0_ffn_bo"], "in_c": w["L0_ffn_in_c"],
        })
    res = run_bass_kernel_spmd(_CACHE["ffn"], in_maps,
                               core_ids=list(range(NCORES)))
    y = np.concatenate([r["y"] for r in res.results], axis=1)
    return np.ascontiguousarray(y.astype(np.float32))
